# version 08
# speedup vs baseline: 2.3416x; 1.0236x over previous
import math
import contextlib
import numpy as np
import ml_dtypes

import concourse.bass as bass
import concourse.mybir as mybir
from concourse.bass_utils import run_bass_kernel_spmd

F32 = mybir.dt.float32
BF16 = mybir.dt.bfloat16
AF = mybir.ActivationFunctionType
ALU = mybir.AluOpType
AX = mybir.AxisListType.X

NC = 8
S = 8192
DM = 4096
H = 32
HKV = 8
HD = 128
NB = 64
TOK = 1024
NT = 8

LN_EPS = 1e-5
QK_EPS = 1e-6
RMS_EPS = 1.1920929e-07
TAU = 0.1
SINK_ITERS = 50
NEG = -1e30
DEN_GUARD = 1e-30
ISQD = 1.0 / math.sqrt(HD)

SLOPES = [2.0 ** (-(h + 1) / 4.0) for h in range(H)]

def _chunk_w(h):
    for w in (128, 64, 32):
        if SLOPES[h] * (w - 1) <= 76.0:
            return w
    return 32
CHUNK_W = [_chunk_w(h) for h in range(H)]

T_SLOT = [64 - 8 * j for j in range(NT)]
WINDOW_TILES = [min(64, int(math.ceil(90.0 / SLOPES[h] / 128.0)) + 1) for h in range(H)]

def blocks_of_core(c):
    return [63 - c, 48 + c, 47 - c, 32 + c, 31 - c, 16 + c, 15 - c, c]

OWNER = [0] * NB
SLOT = [0] * NB
for _c in range(NC):
    for _j, _b in enumerate(blocks_of_core(_c)):
        OWNER[_b] = _c
        SLOT[_b] = _j
KCOL = [OWNER[g] * TOK + SLOT[g] * 128 for g in range(NB)]

def kept_lo(h, j):
    if WINDOW_TILES is None:
        return 0
    return max(0, T_SLOT[j] - 8 - WINDOW_TILES[h])

def active_slots(h, qg, t):
    out = []
    for j in range(4 * qg, 4 * qg + 4):
        if kept_lo(h, j) <= t < T_SLOT[j]:
            out.append(j)
    return out


def att_exp_cols():
    for qg in range(2):
        tmax = T_SLOT[4 * qg]
        for kv in range(HKV):
            for hg in range(4):
                h = 4 * kv + hg
                for t in range(tmax):
                    for j in active_slots(h, qg, t):
                        for ci in range(128 // CHUNK_W[h]):
                            yield (kv, h, qg, j, t, ci)

PLANE_COUNTS = {}
for _rec in att_exp_cols():
    _k = (_rec[2], _rec[0])
    PLANE_COUNTS[_k] = PLANE_COUNTS.get(_k, 0) + 1
PLANE_OFF = {}
_off = 0
for _qg in range(2):
    for _kv in range(HKV):
        PLANE_OFF[(_qg, _kv)] = _off
        _off += PLANE_COUNTS[(_qg, _kv)]
PLANE_TOTAL = _off
PLANE_MAX = max(PLANE_COUNTS.values())



def build_plane_rhs(c):
    blocks = blocks_of_core(c)
    r0 = np.zeros(PLANE_TOTAL, np.float32)
    r1 = np.zeros(PLANE_TOTAL, np.float32)
    i = 0
    for (kv, h, qg, j, t, ci) in att_exp_cols():
        b = blocks[j]
        w = CHUNK_W[h]
        if t > b:
            r0[i] = 0.0
            r1[i] = NEG
        else:
            cq = b * 128 + ci * w + w // 2
            r0[i] = SLOPES[h]
            r1[i] = SLOPES[h] * (128.0 * t - cq)
        i += 1
    return np.stack([r0, r1]).astype(np.float32)


def build_masks(c):
    blocks = blocks_of_core(c)
    tri = (np.arange(128)[:, None] <= np.arange(128)[None, :]).astype(np.float32)
    out = np.zeros((128, NT * 8 * 128), np.float32)
    for j in range(NT):
        b = blocks[j]
        for r8 in range(8):
            t = T_SLOT[j] - 8 + r8
            col = (j * 8 + r8) * 128
            if t < b:
                out[:, col:col + 128] = 1.0
            elif t == b:
                out[:, col:col + 128] = tri
    return out.astype(ml_dtypes.bfloat16)



class Sem:
    def __init__(self, st, nc, name):
        self.h = st.enter_context(nc.semaphore(name))
        self.n = 0
        self._wcache = {}

    def inc(self, instr, k=1):
        instr.then_inc(self.h, k)
        self.n += k

    def wait(self, eng, val=None):
        v = self.n if val is None else val
        key = id(eng)
        if self._wcache.get(key, -1) >= v:
            return
        self._wcache[key] = v
        eng.wait_ge(self.h, v)


class Ring:
    def __init__(self, st, nc, name, n):
        self.s = [Sem(st, nc, f"{name}{i}") for i in range(n)]
        self.n = n

    def inc(self, instr, slot, k=16):
        self.s[slot % self.n].inc(instr, k)

    def wait(self, eng, slot):
        sm = self.s[slot % self.n]
        sm.wait(eng)

    def wait_all(self, eng):
        for sm in self.s:
            sm.wait(eng)


def W(eng, sem, v):
    sem.wait(eng, v)


def build_program(taps=(), phase_limit=None):
    nc = bass.Bass("TRN2", target_bir_lowering=False, debug=False)

    def I(name, shape, dt=F32):
        return nc.dram_tensor(name, shape, dt, kind="ExternalInput").ap()

    x_in = I("x_own", [TOK, DM])
    wsl_in = I("w_slice", [1280, DM])
    wq_in = I("L0_attn_Wq", [DM, DM])
    wk_in = I("L0_attn_Wk", [HKV * HD, DM])
    wv_in = I("L0_attn_Wv", [HKV * HD, DM])
    wo_in = I("L0_attn_Wo", [DM, DM])
    qnw_in = I("L0_attn_q_norm_w", [1, HD])
    knw_in = I("L0_attn_k_norm_w", [1, HD])
    wres_in = I("L0_ffn_W_res", [16, DM])
    wpre_in = I("L0_ffn_W_pre", [4, DM])
    wpost_in = I("L0_ffn_W_post", [4, DM])
    hresb_in = I("L0_ffn_H_res_base", [1, 16])
    hpreb_in = I("L0_ffn_H_pre_base", [1, 4])
    hpostb_in = I("L0_ffn_H_post_base", [1, 4])
    gate_in = I("L0_ffn_gate", [1, 1])
    wb_in = I("L0_ffn_W_branch", [1024, 1024])
    ident_in = I("c_ident", [128, 128], BF16)
    onesf_in = I("c_ones_f32", [1, 128])
    k2_in = I("c_k2", [2, 128])
    prhs_in = I("c_plane_rhs", [2, PLANE_TOTAL])
    mask_in = I("c_masks", [128, NT * 8 * 128], BF16)

    y_out = nc.dram_tensor("y_own", [TOK, DM], F32, kind="ExternalOutput").ap()

    wtr_kv = nc.dram_tensor("wtr_kv", [DM, 256], BF16).ap()
    wtr_all_kv = nc.dram_tensor("wtr_all_kv", [NC * DM, 256], BF16).ap()
    wtr_q = nc.dram_tensor("wtr_q", [DM, 512], BF16).ap()
    wtr_all_q = nc.dram_tensor("wtr_all_q", [NC * DM, 512], BF16).ap()
    wtr_o = nc.dram_tensor("wtr_o", [DM, 512], BF16).ap()
    wtr_all_o = nc.dram_tensor("wtr_all_o", [NC * DM, 512], BF16).ap()
    xn_dram = nc.dram_tensor("xn_dram", [TOK, DM], BF16).ap()
    q_dram = nc.dram_tensor("q_dram", [128, 2 * H * 512], BF16).ap()
    attn_dram = nc.dram_tensor("attn_dram", [TOK, DM], BF16).ap()
    ffn_dram = nc.dram_tensor("ffn_dram", [TOK, DM], F32).ap()
    kv_send = nc.dram_tensor("kv_send", [2 * HKV * HD, TOK], BF16).ap()
    kv_all = nc.dram_tensor("kv_all", [NC * 2 * HKV * HD, TOK], BF16).ap()

    tap_map = {
        "xn": (xn_dram, [TOK, DM], BF16),
        "q": (q_dram, [128, 2 * H * 512], BF16),
        "attn": (attn_dram, [TOK, DM], BF16),
        "ffn": (ffn_dram, [TOK, DM], F32),
        "kvall": (kv_all, [NC * 2 * HKV * HD, TOK], BF16),
    }
    tap_outs = []
    tap_sb = []
    for tname in taps:
        if tname == "hco":
            tap_sb.append(("hco", nc.dram_tensor("tap_hco", [128, 256], F32, kind="ExternalOutput").ap()))
            continue
        if tname == "sink":
            tap_sb.append(("sink", nc.dram_tensor("tap_sink", [128, 1152], F32, kind="ExternalOutput").ap()))
            continue
        tsrc, shp, dt = tap_map[tname]
        tap_outs.append((nc.dram_tensor("tap_" + tname, shp, dt, kind="ExternalOutput").ap(), tsrc))

    st = contextlib.ExitStack()
    with st:
        A = st.enter_context(nc.sbuf_tensor("arenaA", [128, 32768], BF16))
        B = st.enter_context(nc.sbuf_tensor("arenaB", [128, 24576], BF16))
        Q = st.enter_context(nc.sbuf_tensor("arenaQ", [128, 16384], BF16))
        C = st.enter_context(nc.sbuf_tensor("arenaC", [128, 16384], BF16))
        M = st.enter_context(nc.sbuf_tensor("arenaM", [128, 10240], BF16))

        names = ["s_ct", "s_x", "s_v", "s_a", "s_xs", "s_fm", "s_wck", "s_wcq",
                 "s_wco", "s_wb", "s_pe", "s_pp", "s_kvs", "s_cc", "s_kl",
                 "s_ql", "s_pl", "s_sc", "s_scf", "s_ex", "s_mk", "s_av",
                 "s_nf", "s_pb", "s_at", "s_ats", "s_ff", "s_y", "s_ld", "s_cm", "s_ws"]
        SM = {n: Sem(st, nc, n) for n in names}
        (s_ct, s_x, s_v, s_a, s_xs, s_fm, s_wck, s_wcq, s_wco, s_wb, s_pe,
         s_pp, s_kvs, s_cc, s_kl, s_ql, s_pl, s_sc, s_scf, s_ex, s_mk, s_av,
         s_nf, s_pb, s_at, s_ats, s_ff, s_y, s_ld, s_cm, s_ws) = (SM[n] for n in names)

        ring_x = Ring(st, nc, "rx", 2)
        ring_wt = Ring(st, nc, "rwt", 2)
        ring_kv = Ring(st, nc, "rkv", 2)
        ring_pr = Ring(st, nc, "rpr", 2)
        ring_at = Ring(st, nc, "rat", 4)
        ring_y = Ring(st, nc, "ry", 2)
        ring_ws = Ring(st, nc, "rws", 4)
        ring_wc = Ring(st, nc, "rwc", 2)

        T, V, SC, GP, SY = nc.tensor, nc.vector, nc.scalar, nc.gpsimd, nc.sync

        def finish_debug():
            for sm in SM.values():
                GP.wait_ge(sm.h, sm.n)
            for rg in (ring_x, ring_wt, ring_kv, ring_pr, ring_at, ring_y, ring_ws,
                       ring_wc):
                rg.wait_all(GP)
            for tp, tsrc in tap_outs:
                s_y.inc(GP.dma_start(tp, tsrc), 16)
            for kind, tp in tap_sb:
                if kind == "hco":
                    s_y.inc(GP.dma_start(tp, hcoef[:, 0:256]), 16)
                else:
                    s_y.inc(GP.dma_start(tp, sink[:, 0:1152]), 16)
            s_y.wait(GP, s_y.n)

        _mo = [0]
        def mslice(n_bf16, dt=BF16):
            o = _mo[0]
            _mo[0] += (n_bf16 + 15) & ~15
            assert _mo[0] <= 10240, f"M overflow {_mo[0]}"
            ap = M[:, o:o + n_bf16]
            return ap.bitcast(dt) if dt != BF16 else ap

        ident = mslice(128)
        onesf = mslice(256, F32)
        k2 = mslice(256, F32)
        onescol = mslice(16)
        qnw = mslice(16, F32)
        nrow = mslice(512, F32)
        shift_b = mslice(16, F32)
        stats = mslice(160, F32)
        hcoef = mslice(512, F32)
        gcol = mslice(16, F32)
        waT = mslice(768)
        sink = mslice(2304, F32)
        dencol = mslice(32, F32)
        zrow = mslice(288)

        ST_BN, ST_AG, ST_R2, ST_SCR = 0, 48, 50, 58
        HC_RES, HC_PRE, HC_POST, HC_BASE = 0, 128, 160, 192

        CONST_VALS = [LN_EPS, RMS_EPS, QK_EPS, -0.5, -1.0, 1.0 / HD, ISQD,
                      1.0, 0.0, DEN_GUARD]
        cvals = mslice(16 * ((2 * len(CONST_VALS) + 15) // 16), F32)
        for ci_, val_ in enumerate(CONST_VALS):
            nc.const_aps.register(F32, val_, cvals[:, ci_:ci_ + 1])

        hco = hcoef
        assert PLANE_MAX <= 2496, PLANE_MAX

        for ci_, val_ in enumerate(CONST_VALS):
            GP.memset(cvals[:, ci_:ci_ + 1], float(val_))
        GP.memset(onescol[:, 0:1], 1.0)
        GP.memset(sink[:, 0:1152], 0.0)
        GP.memset(hco[:, 0:256], 0.0)
        GP.memset(zrow[:, 0:288], 0.0)
        s_cm.inc(GP.memset(onescol[:, 1:2], 0.0), 1)
        s_cm.wait(SY, 1)
        s_ct.inc(SY.dma_start(ident, ident_in), 16)
        s_ct.inc(SY.dma_start(onesf[0:1, 0:128], onesf_in), 16)
        s_ct.inc(SY.dma_start(k2[0:2, 0:128], k2_in), 16)
        s_ct.inc(SY.dma_start(qnw[:, 0:1], qnw_in.rearrange("o (f u) -> (o f) u", u=1)), 16)
        s_ct.inc(SY.dma_start(qnw[:, 1:2], knw_in.rearrange("o (f u) -> (o f) u", u=1)), 16)
        s_ct.inc(SY.dma_start(nrow[0:1, 0:128], qnw_in), 16)
        s_ct.inc(SY.dma_start(nrow[0:1, 128:256], knw_in), 16)
        s_ct.inc(SY.dma_start(hco[0:1, HC_BASE:HC_BASE + 16], hresb_in), 16)
        s_ct.inc(SY.dma_start(hco[0:1, HC_BASE + 16:HC_BASE + 20], hpreb_in), 16)
        s_ct.inc(SY.dma_start(hco[0:1, HC_BASE + 20:HC_BASE + 24], hpostb_in), 16)
        s_ct.inc(SY.dma_start(gcol[0:1, 3:4], gate_in), 16)

        s_ct.wait(V)
        s_v.inc(V.reduce_max(gcol[0:1, 4:5], nrow[0:1, 0:128], axis=AX,
                             apply_absolute_value=True))
        s_v.inc(V.reduce_max(gcol[0:1, 5:6], nrow[0:1, 128:256], axis=AX,
                             apply_absolute_value=True))
        s_v.wait(V)
        s_v.inc(V.tensor_mul(gcol[0:1, 6:7], gcol[0:1, 4:5], gcol[0:1, 5:6]))
        s_v.wait(V)
        s_v.inc(V.tensor_scalar_mul(gcol[0:1, 7:8], gcol[0:1, 6:7],
                                    float(math.sqrt(128.0))))

        RT_ORDER = [4, 5, 0, 1, 2, 3, 6, 7, 8, 9]

        def wdst(rt):
            if rt == 4:
                return wtr_kv, 0
            if rt == 5:
                return wtr_kv, 128
            if rt < 4:
                return wtr_q, rt * 128
            return wtr_o, (rt - 6) * 128

        xn_fm = A[:, 0:32768]
        with contextlib.ExitStack() as pws:
            wtp = [pws.enter_context(nc.psum_tensor(f"wtp{i}", [128, 128], BF16))
                   for i in range(4)]
            xtp = [pws.enter_context(nc.psum_tensor(f"xtp{i}", [128, 128], BF16))
                   for i in range(4)]
            wrowb = [C[:, 0:4096], C[:, 4096:8192]]
            wslab = [B[:, 16384:20480], B[:, 20480:24576]]

            def p0w_gen():
                ev_w = {}
                tr_end = {}
                s_ct.wait(T)
                for i, rt in enumerate(RT_ORDER):
                    if i >= 2:
                        s_pe.wait(GP, tr_end[i - 2])
                    ring_wc.inc(GP.dma_start(wrowb[i % 2],
                                             wsl_in[rt * 128:(rt + 1) * 128, :]), i)
                    ring_wc.wait(T, i)
                    for fc in range(32):
                        u = i * 32 + fc
                        if u >= 4:
                            s_v.wait(T, ev_w[u - 4])
                        s_pe.inc(T.transpose(wtp[u % 4][:, 0:128],
                                             wrowb[i % 2][:, fc * 128:(fc + 1) * 128],
                                             ident))
                        s_pe.wait(V, s_pe.n)
                        if fc == 0 and i >= 2:
                            ring_ws.wait(V, i - 2)
                        s_v.inc(V.tensor_copy(wslab[i % 2][:, fc * 128:(fc + 1) * 128],
                                              wtp[u % 4][:, 0:128]))
                        ev_w[u] = s_v.n
                    tr_end[i] = s_pe.n
                    s_v.wait(SY, s_v.n)
                    dstt, c0 = wdst(rt)
                    ring_ws.inc(SY.dma_start(
                        dstt[:, c0:c0 + 128].rearrange("(f p) c -> p f c", p=128),
                        wslab[i % 2].rearrange("p (f c) -> p f c", c=128)), i)
                    if i == 1:
                        ring_ws.wait(GP, 0)
                        ring_ws.wait(GP, 1)
                        s_ws.inc(GP.collective_compute(
                            "AllGather", ALU.bypass,
                            replica_groups=[list(range(NC))],
                            ins=[wtr_kv.opt()], outs=[wtr_all_kv.opt()],
                        ))
                    yield
                ring_ws.wait_all(GP)
                s_ws.inc(GP.collective_compute(
                    "AllGather", ALU.bypass,
                    replica_groups=[list(range(NC))],
                    ins=[wtr_q.opt()], outs=[wtr_all_q.opt()],
                ))

            pgen = p0w_gen()

            s_ct.wait(SC)
            s_cm.wait(SC)
            s_cm.wait(V)
            s_ct.wait(T)
            s_cm.wait(T)
            s_ct.wait(GP)
            xbuf = [B[:, 0:8192].bitcast(F32), B[:, 8192:16384].bitcast(F32)]
            xnb = [Q[:, 0:4096], Q[:, 4096:8192]]
            dump = sink[:, 0:1024]
            ev_xn = {}
            tr_xn = {}
            for tt in range(NT):
                par = tt % 2
                xb = xbuf[par]
                if tt >= 2:
                    s_a.wait(SY, a_done_xn[tt - 2])
                ring_x.inc(SY.dma_start(xb, x_in[tt * 128:(tt + 1) * 128, :]), tt)
                ring_x.wait(V, tt)
                if tt >= 1:
                    s_a.wait(V)
                for ch in range(8):
                    s_v.inc(V.bn_stats(stats[:, ST_BN + ch * 6:ST_BN + (ch + 1) * 6],
                                       xb[:, ch * 512:(ch + 1) * 512]))
                s_v.wait(V)
                s_v.inc(V.bn_aggr(stats[:, ST_AG:ST_AG + 2], stats[:, ST_BN:ST_BN + 48]))
                s_v.wait(V)
                cm = stats[:, ST_AG + 0:ST_AG + 1]
                cvv = stats[:, ST_AG + 1:ST_AG + 2]
                crstd = stats[:, ST_SCR + 2:ST_SCR + 3]
                cbias = stats[:, ST_SCR + 3:ST_SCR + 4]
                s_v.wait(SC)
                s_a.inc(SC.activation(stats[:, ST_SCR + 4:ST_SCR + 5], cvv, AF.Ln, bias=LN_EPS))
                s_a.wait(SC)
                s_a.inc(SC.activation(crstd, stats[:, ST_SCR + 4:ST_SCR + 5], AF.Exp, scale=-0.5))
                s_a.wait(V)
                s_v.inc(V.scalar_tensor_tensor(cbias, cm, -1.0, crstd, ALU.mult, ALU.mult))
                s_v.inc(V.scalar_tensor_tensor(stats[:, ST_SCR + 5:ST_SCR + 6], crstd, 1.0, crstd,
                                               ALU.mult, ALU.mult))
                s_v.wait(V)
                s_v.inc(V.tensor_mul(stats[:, ST_SCR + 5:ST_SCR + 6], stats[:, ST_SCR + 5:ST_SCR + 6], cvv))
                s_v.wait(SC)
                s_a.inc(SC.activation(stats[:, ST_SCR + 6:ST_SCR + 7], stats[:, ST_SCR + 5:ST_SCR + 6],
                                      AF.Ln, bias=RMS_EPS))
                s_a.wait(SC)
                s_a.inc(SC.activation(stats[:, ST_R2 + tt:ST_R2 + tt + 1], stats[:, ST_SCR + 6:ST_SCR + 7],
                                      AF.Exp, scale=-0.5))
                s_v.wait(SC)
                s_xs.wait(SC, s_xs.n)
                if tt >= 2:
                    s_pe.wait(SC, tr_xn[tt - 2])
                xo = xnb[par]
                s_a.inc(SC.activation(xo, xb, AF.Identity, bias=cbias, scale=crstd))
                if tt == 0:
                    a_done_xn = {}
                a_done_xn[tt] = s_a.n
                s_a.wait(SC, s_a.n)
                s_xs.inc(SC.dma_start(xn_dram[tt * 128:(tt + 1) * 128, :], xo), 16)
                s_a.wait(T, a_done_xn[tt])
                for fc in range(32):
                    u = tt * 32 + fc
                    if u >= 4:
                        s_fm.wait(T, ev_xn[u - 4])
                    s_pe.inc(T.transpose(xtp[u % 4][:, 0:128],
                                         xo[:, fc * 128:(fc + 1) * 128], ident))
                    s_pe.wait(V, s_pe.n)
                    s_fm.inc(V.tensor_copy(
                        xn_fm[:, fc * 1024 + tt * 128: fc * 1024 + (tt + 1) * 128],
                        xtp[u % 4][:, 0:128]))
                    ev_xn[u] = s_fm.n
                tr_xn[tt] = s_pe.n
                next(pgen, None)
                next(pgen, None)
            for _ in pgen:
                pass
        p0_pe_end = s_pe.n

        if phase_limit == "P0":
            finish_debug()
            return nc
        if phase_limit == "P1":
            finish_debug()
            return nc
        wsm_f = B[:, 0:8192].bitcast(F32)
        wsm_b = B[:, 8192:16384]
        s_v.wait(SY, s_v.n)
        s_ct.inc(SY.dma_start(wsm_f[0:16, :], wres_in), 16)
        s_ct.inc(SY.dma_start(wsm_f[16:20, :], wpre_in), 16)
        s_ct.inc(SY.dma_start(wsm_f[20:24, :], wpost_in), 16)
        s_ct.wait(V)
        s_v.inc(V.tensor_copy(wsm_b[0:32, 0:4096], wsm_f[0:32, :]))

        Zf = sink
        U0, V0, SCR, R64, R32a, R32b, R32c = 128, 160, 192, 320, 384, 416, 448
        ZP, ZQ = 480, 512

        with contextlib.ExitStack() as pst:
            ps_a = pst.enter_context(nc.psum_tensor("ps_a", [128, 512], F32))
            ps_b = pst.enter_context(nc.psum_tensor("ps_b", [128, 512], F32))
            ps_tb = [pst.enter_context(nc.psum_tensor(f"ps_tb{i}", [128, 64], BF16))
                     for i in range(2)]
            psl = [ps_a, ps_b]

            s_v.wait(T)
            s_fm.wait(T)
            s_ct.wait(T)
            for fc in range(32):
                if fc >= 2:
                    s_pp.wait(T, s_pp.n - 1)
                s_pe.inc(T.transpose(ps_tb[fc % 2][:, 0:32],
                                     wsm_b[0:32, fc * 128:(fc + 1) * 128], ident[0:32, 0:32]))
                s_pe.wait(V, s_pe.n)
                s_pp.inc(V.tensor_copy(waT[:, fc * 24:(fc + 1) * 24],
                                       ps_tb[fc % 2][:, 0:24]))

            s_pp.wait(T, s_pp.n)
            s_pe.inc(T.matmul(ps_a[:, 0:24], onesf[0:1, 0:128],
                              hco[0:1, HC_BASE:HC_BASE + 24], start=True, stop=True))
            s_pe.inc(T.matmul(ps_b[:, 0:1], onesf[0:1, 0:128], gcol[0:1, 3:4],
                              start=True, stop=True))
            s_pe.wait(V, s_pe.n)
            s_v.inc(V.tensor_copy(hco[:, HC_BASE:HC_BASE + 24], ps_a[:, 0:24]))
            s_pe.wait(SC, s_pe.n)
            s_a.inc(SC.activation(gcol[:, 4:5], ps_b[:, 0:1], AF.Exp, scale=-1.0))
            s_a.wait(V)
            s_v.inc(V.tensor_scalar(gcol[:, 5:6], gcol[:, 4:5], 1.0, None, ALU.add))
            s_v.wait(V)
            s_v.inc(V.reciprocal(gcol[:, 0:1], gcol[:, 5:6]))
            s_v.wait(V)
            s_v.inc(V.tensor_scalar(gcol[:, 1:2], gcol[:, 0:1], -1.0, 1.0, ALU.mult, ALU.add))
            s_v.wait(V)

            s_fm.wait(T)
            s_v.wait(T, s_v.n)
            s_a.wait(T, s_a.n)
            logit_done = {}
            for tt in range(NT):
                ps = psl[tt % 2]
                if tt >= 2:
                    s_v.wait(T, logit_done[tt - 2])
                mm = None
                for fc in range(32):
                    mm = T.matmul(ps[:, 0:24],
                                  xn_fm[:, fc * 1024 + tt * 128: fc * 1024 + (tt + 1) * 128],
                                  waT[:, fc * 24:(fc + 1) * 24],
                                  start=(fc == 0), stop=(fc == 31))
                s_pe.inc(mm)
                s_pe.wait(V, s_pe.n)
                r2 = stats[:, ST_R2 + tt:ST_R2 + tt + 1]
                s_v.inc(V.scalar_tensor_tensor(Zf[:, tt * 16:(tt + 1) * 16], ps[:, 0:16],
                                               r2, hco[:, HC_BASE:HC_BASE + 16],
                                               ALU.mult, ALU.add))
                s_v.inc(V.scalar_tensor_tensor(Zf[:, ZP + tt * 4:ZP + (tt + 1) * 4], ps[:, 16:20],
                                               r2, hco[:, HC_BASE + 16:HC_BASE + 20],
                                               ALU.mult, ALU.add))
                s_v.inc(V.scalar_tensor_tensor(Zf[:, ZQ + tt * 4:ZQ + (tt + 1) * 4], ps[:, 20:24],
                                               r2, hco[:, HC_BASE + 20:HC_BASE + 24],
                                               ALU.mult, ALU.add))
                logit_done[tt] = s_v.n

            s_v.wait(V)
            if phase_limit == "P2a_logits":
                finish_debug()
                return nc
            s_v.inc(V.tensor_scalar_mul(Zf[:, 0:128], Zf[:, 0:128], 1.0 / TAU))
            s_v.inc(V.memset(Zf[:, U0:U0 + 32], 0.0))
            s_v.inc(V.memset(Zf[:, V0:V0 + 32], 0.0))
            s_v.wait(V)

            z4 = Zf[:, 0:128].rearrange("p (g i j) -> p g i j", g=8, i=4, j=4)
            z4T = Zf[:, 0:128].rearrange("p (g i j) -> p g j i", g=8, i=4, j=4)
            sc4 = Zf[:, SCR:SCR + 128].rearrange("p (g i j) -> p g i j", g=8, i=4, j=4)
            sc4T = Zf[:, SCR:SCR + 128].rearrange("p (g i j) -> p g j i", g=8, i=4, j=4)
            scf = Zf[:, SCR:SCR + 128]
            u3 = Zf[:, U0:U0 + 32].rearrange("p (g i) -> p g i", g=8)
            v3 = Zf[:, V0:V0 + 32].rearrange("p (g j) -> p g j", g=8)
            r64a = Zf[:, R64:R64 + 64]
            r32a = Zf[:, R32a:R32a + 32]
            r32b = Zf[:, R32b:R32b + 32]
            r32c = Zf[:, R32c:R32c + 32]

            def half_iter(out32, reduce_j):
                if reduce_j:
                    s_v.inc(V.tensor_tensor(sc4T, z4T, v3.broadcast_to([128, 8, 4, 4]), ALU.add))
                else:
                    s_v.inc(V.tensor_tensor(sc4, z4, u3.broadcast_to([128, 8, 4, 4]), ALU.add))
                s_v.wait(V)
                if reduce_j:
                    pv = scf.rearrange("p (x b) -> p x b", b=2)
                    s_v.inc(V.tensor_max(r64a, pv[:, :, 0], pv[:, :, 1]))
                    s_v.wait(V)
                    pv2 = r64a.rearrange("p (x b) -> p x b", b=2)
                    s_v.inc(V.tensor_max(r32a, pv2[:, :, 0], pv2[:, :, 1]))
                else:
                    g3a = Zf[:, R64:R64 + 32].rearrange("p (g j) -> p g j", g=8)
                    g3b = Zf[:, R64 + 32:R64 + 64].rearrange("p (g j) -> p g j", g=8)
                    s_v.inc(V.tensor_max(g3a, sc4[:, :, 0, :], sc4[:, :, 1, :]))
                    s_v.inc(V.tensor_max(g3b, sc4[:, :, 2, :], sc4[:, :, 3, :]))
                    s_v.wait(V)
                    s_v.inc(V.tensor_max(r32a, Zf[:, R64:R64 + 32], Zf[:, R64 + 32:R64 + 64]))
                s_v.wait(V)
                m3 = r32a.rearrange("p (g a) -> p g a", g=8)
                if reduce_j:
                    s_v.inc(V.tensor_tensor(sc4, sc4, m3.broadcast_to([128, 8, 4, 4]), ALU.subtract))
                else:
                    s_v.inc(V.tensor_tensor(sc4T, sc4T, m3.broadcast_to([128, 8, 4, 4]), ALU.subtract))
                s_v.wait(SC)
                s_a.inc(SC.activation(scf, scf, AF.Exp))
                s_a.wait(V)
                if reduce_j:
                    pv = scf.rearrange("p (x b) -> p x b", b=2)
                    s_v.inc(V.tensor_add(r64a, pv[:, :, 0], pv[:, :, 1]))
                    s_v.wait(V)
                    pv2 = r64a.rearrange("p (x b) -> p x b", b=2)
                    s_v.inc(V.tensor_add(r32b, pv2[:, :, 0], pv2[:, :, 1]))
                else:
                    g3a = Zf[:, R64:R64 + 32].rearrange("p (g j) -> p g j", g=8)
                    g3b = Zf[:, R64 + 32:R64 + 64].rearrange("p (g j) -> p g j", g=8)
                    s_v.inc(V.tensor_add(g3a, sc4[:, :, 0, :], sc4[:, :, 1, :]))
                    s_v.inc(V.tensor_add(g3b, sc4[:, :, 2, :], sc4[:, :, 3, :]))
                    s_v.wait(V)
                    s_v.inc(V.tensor_add(r32b, Zf[:, R64:R64 + 32], Zf[:, R64 + 32:R64 + 64]))
                s_v.wait(SC)
                s_a.inc(SC.activation(r32c, r32b, AF.Ln))
                s_a.wait(V)
                s_v.inc(V.tensor_add(r32c, r32c, r32a))
                s_v.wait(V)
                s_v.inc(V.tensor_scalar_mul(out32, r32c, -1.0))
                s_v.wait(V)

        def coef_gen():
            for _ in range(SINK_ITERS):
                half_iter(Zf[:, U0:U0 + 32], True)
                yield
                half_iter(Zf[:, V0:V0 + 32], False)
                yield

            s_v.inc(V.tensor_tensor(sc4, z4, u3.broadcast_to([128, 8, 4, 4]), ALU.add))
            s_v.wait(V)
            s_v.inc(V.tensor_tensor(sc4T, sc4T, v3.broadcast_to([128, 8, 4, 4]), ALU.add))
            s_v.wait(SC)
            s_a.inc(SC.activation(scf, scf, AF.Exp))
            s_a.wait(V)
            s_v.inc(V.tensor_scalar(hco[:, HC_RES:HC_RES + 128], scf, gcol[:, 0:1], None, ALU.mult))
            s_v.wait(V)
            hdiag = hco[:, HC_RES:HC_RES + 128].rearrange("p (g x) -> p g x", g=8)[:, :, 0:16:5]
            s_v.inc(V.tensor_scalar(hdiag, hdiag, gcol[:, 1:2], None, ALU.add))
            s_v.wait(V)
            yield

            def soft4(dst_off, src_off):
                sv = Zf[:, src_off:src_off + 32]
                p2 = sv.rearrange("p (x b) -> p x b", b=2)
                s_v.inc(V.tensor_max(Zf[:, R64:R64 + 16], p2[:, :, 0], p2[:, :, 1]))
                s_v.wait(V)
                p1 = Zf[:, R64:R64 + 16].rearrange("p (x b) -> p x b", b=2)
                s_v.inc(V.tensor_max(Zf[:, R32a:R32a + 8], p1[:, :, 0], p1[:, :, 1]))
                s_v.wait(V)
                s_v.inc(V.tensor_tensor(sv.rearrange("p (g j) -> p g j", g=8),
                                        sv.rearrange("p (g j) -> p g j", g=8),
                                        Zf[:, R32a:R32a + 8].broadcast_to([128, 8, 4]),
                                        ALU.subtract))
                s_v.wait(SC)
                s_a.inc(SC.activation(sv, sv, AF.Exp))
                s_a.wait(V)
                s_v.inc(V.tensor_add(Zf[:, R64:R64 + 16], p2[:, :, 0], p2[:, :, 1]))
                s_v.wait(V)
                s_v.inc(V.tensor_add(Zf[:, R32a:R32a + 8], p1[:, :, 0], p1[:, :, 1]))
                s_v.wait(V)
                s_v.inc(V.reciprocal(Zf[:, R32b:R32b + 8], Zf[:, R32a:R32a + 8]))
                s_v.wait(V)
                s_v.inc(V.tensor_tensor(hco[:, dst_off:dst_off + 32].rearrange("p (g j) -> p g j", g=8),
                                        sv.rearrange("p (g j) -> p g j", g=8),
                                        Zf[:, R32b:R32b + 8].broadcast_to([128, 8, 4]),
                                        ALU.mult))
                s_v.wait(V)

            soft4(HC_PRE, ZP)
            yield
            soft4(HC_POST, ZQ)
            s_v.inc(V.tensor_copy(hco[:, HC_BASE:HC_BASE + 1], hco[:, HC_BASE:HC_BASE + 1]))
            coef_done[0] = s_v.n
            yield

        coef_done = [None]
        cgen = coef_gen()

        if phase_limit == "P2a":
            for _ in cgen:
                pass
            finish_debug()
            return nc
        def proj(wbase, col_off, slice_w, fo_total, epilogue, gbase, bg=None,
                 bg_steps=0, ws_need=1, entry_guard=True):
            ngroups = fo_total // 256
            with contextlib.ExitStack() as ps_st:
                pp = [ps_st.enter_context(nc.psum_tensor(f"pp{gbase}_{i}", [128, 512], F32))
                      for i in range(2)]
                ps_ss = [ps_st.enter_context(nc.psum_tensor(f"pss{gbase}{i}", [1, 512], F32))
                         for i in range(1)]
                ps_iv = [ps_st.enter_context(nc.psum_tensor(f"piv{gbase}_{i}", [128, 512], F32))
                         for i in range(1)]
                wtb = [C[:, 8192:16384], C[:, 0:8192]]
                s_v.wait(T, s_v.n)
                s_a.wait(T, s_a.n)
                s_pp.wait(T, s_pp.n)
                if entry_guard:
                    s_pp.wait(SY, s_pp.n)
                    s_v.wait(SY, s_v.n)
                s_ws.wait(SY, ws_need)
                wt_done = {}
                unit_done = {}
                for g in range(ngroups):
                    fo0 = g * 256
                    if g == 1:
                        ring_wc.wait_all(SY)
                        s_pe.wait(SY, p0_pe_end)
                    if g >= 2:
                        sm_, ct_ = wt_done[g - 2]
                        SY.wait_ge(sm_.h, ct_)
                    for fc in range(32):
                        if slice_w == 512:
                            r_ = fo0 // 512
                            lo_ = fo0 % 512
                            ring_wt.inc(SY.dma_start(
                                wtb[g % 2][:, fc * 256:(fc + 1) * 256],
                                wbase[r_ * DM + fc * 128: r_ * DM + (fc + 1) * 128,
                                      col_off + lo_: col_off + lo_ + 256]), g)
                        else:
                            for hf in range(2):
                                r_ = (fo0 + hf * 128) // 128
                                ring_wt.inc(SY.dma_start(
                                    wtb[g % 2][:, fc * 256 + hf * 128: fc * 256 + (hf + 1) * 128],
                                    wbase[r_ * DM + fc * 128: r_ * DM + (fc + 1) * 128,
                                          col_off: col_off + 128]), g)
                    ring_wt.wait(T, g)
                    for blk in range(2):
                        for tch in range(2):
                            u = (g * 2 + blk) * 2 + tch
                            ps = pp[u % 2]
                            if u >= 2:
                                sm_, ct_ = unit_done[u - 2]
                                T.wait_ge(sm_.h, ct_)
                            mm = None
                            for fc in range(32):
                                mm = T.matmul(ps[:, 0:512],
                                              wtb[g % 2][:, fc * 256 + blk * 128: fc * 256 + (blk + 1) * 128],
                                              xn_fm[:, fc * 1024 + tch * 512: fc * 1024 + (tch + 1) * 512],
                                              start=(fc == 0), stop=(fc == 31))
                            s_pe.inc(mm)
                            ud = epilogue(g * 2 + blk, tch, ps, ps_ss[0], ps_iv[0])
                            unit_done[u] = ud
                            if bg is not None:
                                for _ in range(bg_steps):
                                    if next(bg, "DONE") == "DONE":
                                        bg = None
                                        break
                    wt_done[g] = ud

        qf32 = [nrow[:, 0:0], nrow[:, 0:0]]
        qsc = Q[:, 0:4096].bitcast(F32)
        qsq = Q[:, 4096:5120]
        ivrow = Q[:, 5120:7168].bitcast(F32)
        kvdst = {"v": B[:, 8192:16384], "k": B[:, 0:8192]}

        def plain_epi(dst):
            def epi(blk, tch, ps, ps_ss, ps_iv):
                s_pe.wait(V, s_pe.n)
                s_pp.inc(V.tensor_copy(dst[:, blk * 1024 + tch * 512: blk * 1024 + (tch + 1) * 512],
                                       ps[:, 0:512]))
                return (s_pp, s_pp.n)
            return epi

        def norm_epi(dst, wcol, to_dram):
            def epi(blk, tch, ps, ps_ss, ps_iv):
                par = (blk * 2 + tch) % 2
                qf = qsc[:, par * 512:(par + 1) * 512]
                ivr = ivrow[0:1, par * 512:(par + 1) * 512]
                s_pe.wait(V, s_pe.n)
                s_v.inc(V.tensor_copy(qf, ps[:, 0:512]))
                s_v.wait(V)
                s_v.inc(V.tensor_mul(qsq[:, par * 512:(par + 1) * 512], qf, qf))
                s_v.wait(T, s_v.n)
                s_a.wait(T, s_a.n)
                s_pe.inc(T.matmul(ps_ss[0:1, 0:512], onescol[:, 0:1],
                                  qsq[:, par * 512:(par + 1) * 512], start=True, stop=True))
                s_pe.wait(SC, s_pe.n)
                s_a.inc(SC.activation(ivr, ps_ss[0:1, 0:512], AF.Ln,
                                      bias=QK_EPS, scale=1.0 / HD))
                s_a.wait(SC)
                s_a.inc(SC.activation(ivr, ivr, AF.Exp, scale=-0.5))
                s_a.wait(T, s_a.n)
                s_pe.inc(T.matmul(ps_iv[:, 0:512], onesf[0:1, 0:128], ivr,
                                  start=True, stop=True))
                s_pe.wait(V, s_pe.n)
                if to_dram is not None:
                    s_ql.wait(V, s_ql.n)
                s_v.inc(V.scalar_tensor_tensor(dst(blk, tch), qf, qnw[:, wcol:wcol + 1],
                                               ps_iv[:, 0:512], ALU.mult, ALU.mult))
                ret = (s_v, s_v.n)
                if to_dram is not None:
                    s_v.wait(SC, s_v.n)
                    s_ql.inc(SC.dma_start(to_dram(blk, tch), dst(blk, tch)), 16)
                return ret
            return epi

        s_fm.wait(T)
        proj(wtr_all_kv, 128, 128, HKV * HD, plain_epi(kvdst["v"]), "v", bg=cgen,
             bg_steps=4, entry_guard=False)
        s_xs.wait(V)
        s_xs.wait(SC)
        kslice = lambda blk, tch: kvdst["k"][:, blk * 1024 + tch * 512: blk * 1024 + (tch + 1) * 512]
        proj(wtr_all_kv, 0, 128, HKV * HD, norm_epi(kslice, 1, None), "k", bg=cgen, bg_steps=4)
        for _ in cgen:
            pass

        vtm = B[:, 16384:24576]
        ring_ws.wait_all(V)
        with contextlib.ExitStack() as pvs:
            vtp = [pvs.enter_context(nc.psum_tensor(f"vtp{i}", [128, 128], BF16))
                   for i in range(2)]
            s_v.wait(T, s_v.n)
            s_a.wait(T, s_a.n)
            vt_done = {}
            for kv in range(HKV):
                for j in range(NT):
                    u = kv * NT + j
                    if u >= 2:
                        s_v.wait(T, vt_done[u - 2])
                    s_pe.inc(T.transpose(vtp[u % 2][:, 0:128],
                                         kvdst["v"][:, kv * 1024 + j * 128: kv * 1024 + (j + 1) * 128],
                                         ident))
                    s_pe.wait(V, s_pe.n)
                    s_v.inc(V.tensor_copy(vtm[:, (j * 8 + kv) * 128:(j * 8 + kv + 1) * 128],
                                          vtp[u % 2][:, 0:128]))
                    vt_done[u] = s_v.n
        s_pp.wait(SC, s_pp.n)
        s_v.wait(SC, s_v.n)
        for kv in range(HKV):
            s_kvs.inc(SC.dma_start(kv_send[kv * 128:(kv + 1) * 128, :],
                                   kvdst["k"][:, kv * 1024:(kv + 1) * 1024]), 16)
        for j in range(NT):
            s_kvs.inc(SC.dma_start(kv_send[1024 + j * 128:1024 + (j + 1) * 128, :],
                                   vtm[:, j * 1024:(j + 1) * 1024]), 16)
        s_kvs.wait(GP, s_kvs.n)
        s_cc.inc(GP.collective_compute(
            "AllGather", ALU.bypass,
            replica_groups=[list(range(NC))],
            ins=[kv_send.opt()], outs=[kv_all.opt()],
        ))

        s_fm.wait(GP)
        s_xs.wait(GP)
        for ci in range(8):
            s_wb.inc(GP.dma_start(Q[:, 7680 + ci * 1024:7680 + (ci + 1) * 1024],
                                  wb_in[ci * 128:(ci + 1) * 128, :]), 16)
        wbsb = Q[:, 7680:15872]
        xnt = [B[:, 0:4096], B[:, 4096:8192]]
        xpf = B[:, 8192:12288].bitcast(F32)
        xpre_b = B[:, 12288:13312]
        xprefm = B[:, 13312:14336]
        yv = xpf[:, 1024:2048]
        fo_f = B[:, 14336:22528].bitcast(F32)

        def mix_gen(pf, pfb):
            s_kvs.wait(SY, s_kvs.n)
            v_used_xnt = {}
            for tt in range(NT):
                par = tt % 2
                if tt >= 2:
                    s_v.wait(SY, v_used_xnt[tt - 2])
                ring_x.inc(SY.dma_start(xnt[par], xn_dram[tt * 128:(tt + 1) * 128, :]), tt)
                ring_x.wait(V, tt)
                if tt >= 1:
                    s_ff.wait(V, s_ff.n)
                hpre = lambda n: hco[:, HC_PRE + tt * 4 + n: HC_PRE + tt * 4 + n + 1]
                hpost = lambda n: hco[:, HC_POST + tt * 4 + n: HC_POST + tt * 4 + n + 1]
                hres = lambda n, jj: hco[:, HC_RES + tt * 16 + n * 4 + jj: HC_RES + tt * 16 + n * 4 + jj + 1]
                s_v.inc(V.tensor_scalar(xpf[:, 0:1024], xnt[par][:, 0:1024], hpre(0), None, ALU.mult))
                for n in range(1, 4):
                    s_v.wait(V)
                    s_v.inc(V.scalar_tensor_tensor(xpf[:, 0:1024], xnt[par][:, n * 1024:(n + 1) * 1024],
                                                   hpre(n), xpf[:, 0:1024], ALU.mult, ALU.add))
                s_v.wait(V)
                s_v.inc(V.tensor_copy(xpre_b, xpf[:, 0:1024]))
                s_v.wait(T)
                yield
                for cc in range(8):
                    s_pe.inc(T.transpose(pfb[cc % 2][:, 0:128],
                                         xpre_b[:, cc * 128:(cc + 1) * 128], ident))
                    s_pe.wait(V, s_pe.n)
                    s_v.inc(V.tensor_copy(xprefm[:, cc * 128:(cc + 1) * 128],
                                          pfb[cc % 2][:, 0:128]))
                    if cc >= 1:
                        s_v.wait(T, s_v.n - 1)
                s_v.wait(T, s_v.n)
                s_wb.wait(T, s_wb.n)
                yield
                for co in range(2):
                    ps = pf[co % 2]
                    mm = None
                    for cc in range(8):
                        mm = T.matmul(ps[:, 0:512], xprefm[:, cc * 128:(cc + 1) * 128],
                                      wbsb[:, cc * 1024 + co * 512: cc * 1024 + (co + 1) * 512],
                                      start=(cc == 0), stop=(cc == 7))
                    s_pe.inc(mm)
                    s_pe.wait(V, s_pe.n)
                    s_v.inc(V.tensor_copy(yv[:, co * 512:(co + 1) * 512], ps[:, 0:512]))
                s_v.wait(V)
                yield
                for n in range(4):
                    fon = fo_f[:, n * 1024:(n + 1) * 1024]
                    s_v.inc(V.tensor_scalar(fon, xnt[par][:, 0:1024], hres(n, 0), None, ALU.mult))
                    for jj in range(1, 4):
                        s_v.wait(V)
                        s_v.inc(V.scalar_tensor_tensor(fon, xnt[par][:, jj * 1024:(jj + 1) * 1024],
                                                       hres(n, jj), fon, ALU.mult, ALU.add))
                    s_v.wait(V)
                    s_v.inc(V.scalar_tensor_tensor(fon, yv, hpost(n), fon, ALU.mult, ALU.add))
                    s_v.wait(V)
                v_used_xnt[tt] = s_v.n
                s_v.wait(GP, s_v.n)
                s_ff.inc(GP.dma_start(ffn_dram[tt * 128:(tt + 1) * 128, :], fo_f), 16)
                yield

        qslice = lambda blk, tch: qsq[:, 0:512] if False else qsc.bitcast(BF16)[:, 2048 + 0:2048 + 512]
        qtmp = Q[:, 7168:7680]
        qsl = lambda blk, tch: qtmp[:, 0:512]
        qdr = lambda blk, tch: q_dram[:, (tch * H + blk) * 512:(tch * H + blk + 1) * 512]
        with contextlib.ExitStack() as ps_mx:
            pf_ = [ps_mx.enter_context(nc.psum_tensor(f"pf{i}", [128, 512], F32)) for i in range(2)]
            pfb_ = [ps_mx.enter_context(nc.psum_tensor(f"pfb{i}", [128, 128], BF16)) for i in range(2)]
            mgen = mix_gen(pf_, pfb_)
            proj(wtr_all_q, 0, 512, DM, norm_epi(qsl, 0, qdr), "q", bg=mgen,
                 bg_steps=2, ws_need=2)
            for _ in mgen:
                pass

        if phase_limit == "P3":
            finish_debug()
            return nc

        if phase_limit == "P2b":
            finish_debug()
            return nc
        VOFF = [0, 9216]
        POFF = 18432
        ATOFF = 21504
        PLOFF = [22016, 23680]
        masksb = B[:, 16384:24576]
        qslab = Q[:, 0:16384]

        with contextlib.ExitStack() as ps_st:
            ps_sc = [ps_st.enter_context(nc.psum_tensor(f"psc{i}", [128, 512], F32)) for i in range(5)]
            ps_av = [ps_st.enter_context(nc.psum_tensor(f"pav{i}", [128, 512], F32)) for i in range(2)]
            ps_pl = ps_st.enter_context(nc.psum_tensor("pspl", [128, 512], F32))

            s_pe.wait(SY, s_pe.n)
            s_ff.wait(SY, s_ff.n)
            s_v.wait(SY, s_v.n)
            s_kvs.wait(SY, s_kvs.n)
            s_ct.inc(SY.dma_start(B[:, 16384:24576], mask_in), 16)
            ring_ws.wait_all(GP)
            s_ws.inc(GP.collective_compute(
                "AllGather", ALU.bypass,
                replica_groups=[list(range(NC))],
                ins=[wtr_o.opt()], outs=[wtr_all_o.opt()],
            ))
            s_ct.wait(GP, s_ct.n)

            s_pe.wait(GP, s_pe.n)
            s_v.wait(GP, s_v.n)
            for par_ in range(2):
                vv = A[:, VOFF[par_]:VOFF[par_] + 9216].rearrange(
                    "p (t c) -> p t c", c=144)[:, :, 128:129]
                s_cm.inc(GP.memset(vv, 1.0), 1)
            s_cm.wait(T)

            score_cnt_end = {}
            av_cnt_end = {}
            ex_cnt_end = {}
            KEYSEQ = [(qg_, kv_) for qg_ in range(2) for kv_ in range(HKV)]

            def load_kv(qg, kv):
                seq = KEYSEQ.index((qg, kv))
                par = seq % 2
                kb = C[:, par * 8192:(par + 1) * 8192]
                s_cc.wait(SY, 1)
                if seq >= 2:
                    key = KEYSEQ[seq - 2]
                    s_sc.wait(SY, score_cnt_end[key])
                    s_av.wait(SY, av_cnt_end[key])
                jlo = 0 if qg == 0 else 4
                nj = NT - jlo
                for r in range(NC):
                    ring_kv.inc(SY.dma_start(kb[:, r * 1024:(r + 1) * 1024],
                                             kv_all[r * 2048 + kv * 128: r * 2048 + (kv + 1) * 128, :]), seq)
                    vsrc = kv_all[r * 2048 + 1024 + jlo * 128: r * 2048 + 2048,
                                  kv * 128:(kv + 1) * 128].rearrange(
                        "(j p) h -> p j h", p=128)
                    vdst = A[:, VOFF[par] + (r * 8 + jlo) * 144:
                             VOFF[par] + (r * 8 + 8) * 144].rearrange(
                        "p (j c) -> p j c", c=144)[:, :, 0:128]
                    ring_kv.inc(SY.dma_start(vdst, vsrc), seq)
                return seq

            def build_plane(qg, kv):
                seq = KEYSEQ.index((qg, kv))
                par = seq % 2
                ncols = PLANE_COUNTS[(qg, kv)]
                off = PLANE_OFF[(qg, kv)]
                if seq >= 2:
                    key = KEYSEQ[seq - 2]
                    s_ex.wait(SY, ex_cnt_end[key])
                prb = prhs_bufs[seq % 2]
                ring_pr.inc(SY.dma_start(prb[0:2, 0:ncols], prhs_in[:, off:off + ncols]), seq)
                ring_pr.wait(T, seq)
                pl = A[:, PLOFF[par]:PLOFF[par] + 1664].bitcast(F32)
                nch = (ncols + 511) // 512
                for chk in range(nch):
                    c0 = chk * 512
                    n = min(512, ncols - c0)
                    s_pl.wait(T, s_pl.n)
                    s_pe.inc(T.matmul(ps_pl[:, 0:n], k2[0:2, 0:128], prb[0:2, c0:c0 + n],
                                      start=True, stop=True))
                    s_pe.wait(V, s_pe.n)
                    s_pl.inc(V.tensor_scalar(pl[:, c0:c0 + n], ps_pl[:, 0:n],
                                             shift_b[:, 0:1], None, ALU.subtract))
                return s_pl.n

            s_v.wait(T, s_v.n)
            s_a.wait(T, s_a.n)
            s_pe.inc(T.matmul(ps_pl[:, 0:1], onesf[0:1, 0:128], gcol[0:1, 7:8],
                              start=True, stop=True))
            s_pe.wait(V, s_pe.n)
            s_pl.inc(V.tensor_copy(shift_b[:, 0:1], ps_pl[:, 0:1]))

            prhs_bufs = [B[:, 0:4992].bitcast(F32), B[:, 4992:9984].bitcast(F32)]
            pb_cnt = [0]
            at_cnt = [0]
            ex_tile_cnt = {}
            pfree = {}
            pending = {}

            pending[(0, 0)] = (load_kv(0, 0), build_plane(0, 0))
            pending[(0, 1)] = (load_kv(0, 1), build_plane(0, 1))

            for qg in range(2):
                s_ql.wait(SY)
                s_sc.wait(SY, s_sc.n)
                s_ld.inc(SY.dma_start(qslab, q_dram[:, qg * 16384:(qg + 1) * 16384]), 16)
                qload = s_ld.n
                tmax = T_SLOT[4 * qg]
                for kv in range(HKV):
                    klcnt, plcnt = pending.pop((qg, kv))
                    pcol = [0]
                    par = kv % 2
                    kb = C[:, par * 8192:(par + 1) * 8192]
                    pl = A[:, PLOFF[par]:PLOFF[par] + 1664].bitcast(F32)
                    for hg in range(4):
                        h = 4 * kv + hg
                        nf_before = s_nf.n
                        s_nf.wait(T, nf_before)
                        for bk in range(2):
                            T.matmul(ps_av[bk][:, 0:265], ident[0:1, 0:128],
                                     zrow[0:1, 0:265], start=True, stop=False)
                        kept = [t_ for t_ in range(tmax) if active_slots(h, qg, t_)]
                        PIPE = 3
                        sc_cnt = {}

                        def emit_score(idx):
                            t_ = kept[idx]
                            act_ = active_slots(h, qg, t_)
                            jr0_ = act_[0] - 4 * qg
                            nc_ = len(act_) * 128
                            ab = pb_cnt[0] + idx
                            ring_kv.wait(T, klcnt)
                            s_ld.wait(T, qload)
                            if ab >= 5:
                                s_ex.wait(T, ex_tile_cnt[ab - 5])
                            mm_ = T.matmul(ps_sc[ab % 5][:, jr0_ * 128: jr0_ * 128 + nc_],
                                           kb[:, KCOL[t_]:KCOL[t_] + 128],
                                           qslab[:, h * 512 + jr0_ * 128: h * 512 + jr0_ * 128 + nc_],
                                           start=True, stop=True)
                            s_sc.inc(mm_)
                            sc_cnt[ab] = s_sc.n

                        for i0 in range(min(PIPE, len(kept))):
                            emit_score(i0)
                        for idx, t in enumerate(kept):
                            act = active_slots(h, qg, t)
                            ab = pb_cnt[0] + idx
                            scpar = ab % 5
                            pbuf = A[:, POFF + (ab % 6) * 512: POFF + (ab % 6 + 1) * 512]
                            s_sc.wait(SC, sc_cnt[ab])
                            s_pl.wait(SC, plcnt)
                            if ab >= 6:
                                knd, cnt_ = pfree[ab - 6]
                                (s_av if knd == 'av' else s_pb).wait(SC, cnt_)
                            w = CHUNK_W[h]
                            nch = 128 // w
                            last_act = None
                            for j in act:
                                jr = j - 4 * qg
                                for ci in range(nch):
                                    last_act = SC.activation(
                                        pbuf[:, jr * 128 + ci * w: jr * 128 + (ci + 1) * w],
                                        ps_sc[scpar][:, jr * 128 + ci * w: jr * 128 + (ci + 1) * w],
                                        AF.Exp, bias=pl[:, pcol[0]:pcol[0] + 1], scale=ISQD)
                                    pcol[0] += 1
                            s_ex.inc(last_act)
                            ex_tile_cnt[ab] = s_ex.n
                            masked = [j for j in act if t >= T_SLOT[j] - 8]
                            for j in masked:
                                jr = j - 4 * qg
                                s_ex.wait(GP, s_ex.n)
                                s_mk.inc(GP.tensor_tensor(
                                    pbuf[:, jr * 128:(jr + 1) * 128],
                                    pbuf[:, jr * 128:(jr + 1) * 128],
                                    masksb[:, (j * 8 + (t - (T_SLOT[j] - 8))) * 128:
                                           (j * 8 + (t - (T_SLOT[j] - 8)) + 1) * 128],
                                    ALU.mult))
                            s_ex.wait(T, s_ex.n)
                            if masked:
                                s_mk.wait(T, s_mk.n)
                            bank_close = False
                            lastav = None
                            for j in act:
                                jr = j - 4 * qg
                                bk, half = jr // 2, jr % 2
                                closes = (jr in (0, 2) and t == T_SLOT[j] - 1)
                                pos_t = OWNER[t] * 8 + SLOT[t]
                                av = T.matmul(ps_av[bk][:, half * 136: half * 136 + 129],
                                              pbuf[:, jr * 128:(jr + 1) * 128],
                                              A[:, VOFF[par] + pos_t * 144: VOFF[par] + pos_t * 144 + 129],
                                              start=False, stop=closes)
                                if closes:
                                    s_av.inc(av)
                                    bank_close = True
                                lastav = av
                            if bank_close:
                                pfree[ab] = ('av', s_av.n)
                            else:
                                s_pb.inc(lastav)
                                pfree[ab] = ('pb', s_pb.n)
                            if idx + PIPE < len(kept):
                                emit_score(idx + PIPE)
                            for j in act:
                                jr = j - 4 * qg
                                if not (jr in (0, 2) and t == T_SLOT[j] - 1):
                                    continue
                                bk = jr // 2
                                s_av.wait(V, s_av.n)
                                for half in range(2):
                                    jn = 4 * qg + 2 * bk + half
                                    atb = A[:, ATOFF + (at_cnt[0] % 4) * 128: ATOFF + (at_cnt[0] % 4 + 1) * 128]
                                    ring_at.wait(V, at_cnt[0])
                                    dc = dencol[:, (at_cnt[0] % 4) * 2:(at_cnt[0] % 4) * 2 + 2]
                                    s_v.inc(V.tensor_scalar(dc[:, 0:1],
                                                            ps_av[bk][:, half * 136 + 128: half * 136 + 129],
                                                            DEN_GUARD, None, ALU.add))
                                    s_v.wait(V)
                                    s_v.inc(V.reciprocal(dc[:, 1:2], dc[:, 0:1]))
                                    s_v.wait(V)
                                    nrm = V.tensor_scalar(atb, ps_av[bk][:, half * 136: half * 136 + 128],
                                                          dc[:, 1:2], None, ALU.mult)
                                    s_nf.inc(nrm)
                                    s_nf.wait(SC, s_nf.n)
                                    ring_at.inc(SC.dma_start(
                                        attn_dram[jn * 128:(jn + 1) * 128, h * 128:(h + 1) * 128], atb), at_cnt[0])
                                    at_cnt[0] += 1
                        pb_cnt[0] += len(kept)
                    score_cnt_end[(qg, kv)] = s_sc.n
                    av_cnt_end[(qg, kv)] = s_av.n
                    ex_cnt_end[(qg, kv)] = s_ex.n
                    nseq = KEYSEQ.index((qg, kv)) + 2
                    if nseq < len(KEYSEQ):
                        nk = KEYSEQ[nseq]
                        pending[nk] = (load_kv(*nk), build_plane(*nk))

        if phase_limit == "P4":
            finish_debug()
            return nc
        attn_fm = A[:, 0:32768]
        xres = Q[:, 0:4096].bitcast(F32)
        outt = Q[:, 4096:8192].bitcast(F32)
        s_ff.wait(SC, s_ff.n)
        with contextlib.ExitStack() as ps_st:
            pw = [ps_st.enter_context(nc.psum_tensor(f"pw{i}", [128, 512], F32)) for i in range(2)]
            ftp = [ps_st.enter_context(nc.psum_tensor(f"ftp{i}", [128, 128], BF16))
                   for i in range(4)]
            s_v.wait(T, s_v.n)
            s_a.wait(T, s_a.n)
            wobs = [C[:, 0:16384], B[:, 8192:24576]]

            def wo_load(g):
                wob = wobs[g % 2]
                for hc in range(32):
                    ring_wt.inc(SY.dma_start(
                        wob[:, hc * 512:(hc + 1) * 512],
                        wtr_all_o[g * DM + hc * 128: g * DM + (hc + 1) * 128, 0:512]), g)
                return wob

            s_ws.wait(SY, 3)
            s_sc.wait(SY, s_sc.n)
            wob_pre = {0: wo_load(0)}

            ring_at.wait_all(SY)
            s_pe.wait(SY, s_pe.n)
            s_mk.wait(SY, s_mk.n)
            wob_pre[1] = wo_load(1)
            arow = [B[:, 0:4096], B[:, 4096:8192]]
            evf = {}
            trf = {}
            for j in range(NT):
                if j >= 2:
                    s_pe.wait(SY, trf[j - 2])
                ring_kv.inc(SY.dma_start(arow[j % 2], attn_dram[j * 128:(j + 1) * 128, :]),
                            16 + j)
                ring_kv.wait(T, 16 + j)
                for hc in range(32):
                    u = j * 32 + hc
                    if u >= 4:
                        s_fm.wait(T, evf[u - 4])
                    s_pe.inc(T.transpose(ftp[u % 4][:, 0:128],
                                         arow[j % 2][:, hc * 128:(hc + 1) * 128], ident))
                    s_pe.wait(V, s_pe.n)
                    if u == 0:
                        s_av.wait(V, s_av.n)
                        s_mk.wait(V, s_mk.n)
                        ring_at.wait_all(V)
                    s_fm.inc(V.tensor_copy(
                        attn_fm[:, hc * 1024 + j * 128: hc * 1024 + (j + 1) * 128],
                        ftp[u % 4][:, 0:128]))
                    evf[u] = s_fm.n
                trf[j] = s_pe.n
            fm_all = s_fm.n

            wo_done = {}
            p5_done = {}
            for g in range(8):
                wob = wob_pre.pop(g, None)
                if wob is None:
                    s_pe.wait(SY, wo_done[g - 2])
                    wob = wo_load(g)
                ring_wt.wait(T, g)
                s_fm.wait(T, fm_all)
                for tt in range(NT):
                    un = (g * 8 + tt)
                    ps = pw[un % 2]
                    if un >= 2:
                        s_v.wait(T, p5_done[un - 2])
                    ring_y.wait(SC, un)
                    ring_x.inc(SC.dma_start(xres[:, (un % 2) * 1024: (un % 2) * 1024 + 512],
                                            x_in[tt * 128:(tt + 1) * 128, g * 512:(g + 1) * 512]), un)
                    ring_x.inc(SC.dma_start(xres[:, (un % 2) * 1024 + 512: (un % 2) * 1024 + 1024],
                                            ffn_dram[tt * 128:(tt + 1) * 128, g * 512:(g + 1) * 512]), un)
                    mm = None
                    for hc in range(32):
                        mm = T.matmul(ps[:, 0:512],
                                      attn_fm[:, hc * 1024 + tt * 128: hc * 1024 + (tt + 1) * 128],
                                      wob[:, hc * 512:(hc + 1) * 512],
                                      start=(hc == 0), stop=(hc == 31))
                    s_pe.inc(mm)
                    s_pe.wait(V, s_pe.n)
                    ring_x.wait(V, un)
                    ot = outt[:, (un % 2) * 1024:(un % 2) * 1024 + 512]
                    s_v.inc(V.tensor_add(ot, ps[:, 0:512], xres[:, (un % 2) * 1024: (un % 2) * 1024 + 512]))
                    s_v.wait(V)
                    s_v.inc(V.tensor_add(ot, ot, xres[:, (un % 2) * 1024 + 512: (un % 2) * 1024 + 1024]))
                    p5_done[un] = s_v.n
                    s_v.wait(GP, s_v.n)
                    ring_y.inc(GP.dma_start(y_out[tt * 128:(tt + 1) * 128, g * 512:(g + 1) * 512], ot), un)
                wo_done[g] = s_pe.n

            for tp, src in tap_outs:
                s_y.wait(GP, s_y.n)
                s_y.inc(GP.dma_start(tp, src), 16)
            s_y.wait(GP, s_y.n)

    return nc



_CONST = {}

def _consts():
    if not _CONST:
        _CONST["ident"] = np.eye(128, dtype=np.float32).astype(ml_dtypes.bfloat16)
        _CONST["ones"] = np.ones((1, 128), np.float32)
        _CONST["k2"] = np.stack([np.arange(128, dtype=np.float32),
                                 np.ones(128, np.float32)])
        _CONST["planes"] = [build_plane_rhs(c) for c in range(NC)]
        _CONST["masks"] = [build_masks(c) for c in range(NC)]
    return _CONST


def run(inputs, taps=(), phase_limit=None):
    cst = _consts()
    x = np.asarray(inputs["L0_attn_x"])
    xr = np.ascontiguousarray(x.reshape(NB, 128, DM))
    wq_np = np.asarray(inputs["L0_attn_Wq"])
    wk_np = np.asarray(inputs["L0_attn_Wk"])
    wv_np = np.asarray(inputs["L0_attn_Wv"])
    wo_np = np.asarray(inputs["L0_attn_Wo"])
    in_maps = []
    for c in range(NC):
        blocks = blocks_of_core(c)
        x_own = np.ascontiguousarray(xr[blocks].reshape(TOK, DM))
        w_slice = np.ascontiguousarray(np.concatenate([
            wq_np[512 * c:512 * (c + 1)], wk_np[128 * c:128 * (c + 1)],
            wv_np[128 * c:128 * (c + 1)], wo_np[512 * c:512 * (c + 1)]], axis=0))
        m = {
            "x_own": x_own,
            "w_slice": w_slice,
            "L0_attn_Wq": np.ascontiguousarray(inputs["L0_attn_Wq"]),
            "L0_attn_Wk": np.ascontiguousarray(inputs["L0_attn_Wk"]),
            "L0_attn_Wv": np.ascontiguousarray(inputs["L0_attn_Wv"]),
            "L0_attn_Wo": np.ascontiguousarray(inputs["L0_attn_Wo"]),
            "L0_attn_q_norm_w": np.ascontiguousarray(np.asarray(inputs["L0_attn_q_norm_w"]).reshape(1, HD)),
            "L0_attn_k_norm_w": np.ascontiguousarray(np.asarray(inputs["L0_attn_k_norm_w"]).reshape(1, HD)),
            "L0_ffn_W_res": np.ascontiguousarray(inputs["L0_ffn_W_res"]),
            "L0_ffn_W_pre": np.ascontiguousarray(inputs["L0_ffn_W_pre"]),
            "L0_ffn_W_post": np.ascontiguousarray(inputs["L0_ffn_W_post"]),
            "L0_ffn_H_res_base": np.ascontiguousarray(np.asarray(inputs["L0_ffn_H_res_base"]).reshape(1, 16)),
            "L0_ffn_H_pre_base": np.ascontiguousarray(np.asarray(inputs["L0_ffn_H_pre_base"]).reshape(1, 4)),
            "L0_ffn_H_post_base": np.ascontiguousarray(np.asarray(inputs["L0_ffn_H_post_base"]).reshape(1, 4)),
            "L0_ffn_gate": np.ascontiguousarray(np.asarray(inputs["L0_ffn_gate"]).reshape(1, 1)),
            "L0_ffn_W_branch": np.ascontiguousarray(inputs["L0_ffn_W_branch"]),
            "c_ident": cst["ident"],
            "c_ones_f32": cst["ones"],
            "c_k2": cst["k2"],
            "c_plane_rhs": cst["planes"][c],
            "c_masks": cst["masks"][c],
        }
        in_maps.append(m)

    nc_prog = build_program(taps=taps, phase_limit=phase_limit)
    res = run_bass_kernel_spmd(nc_prog, in_maps, core_ids=list(range(NC)))

    y = np.zeros((NB, 128, DM), np.float32)
    for c in range(NC):
        blocks = blocks_of_core(c)
        yc = np.asarray(res.results[c]["y_own"]).reshape(NT, 128, DM)
        for j, b in enumerate(blocks):
            y[b] = yc[j]
    out = y.reshape(1, S, DM)
    if taps:
        return out, res
    return out


def kernel(**inputs):
    return run(inputs)


# version 09
# speedup vs baseline: 2.3449x; 1.0014x over previous
import math
import contextlib
import numpy as np
import ml_dtypes

import concourse.bass as bass
import concourse.mybir as mybir
from concourse.bass_utils import run_bass_kernel_spmd

F32 = mybir.dt.float32
BF16 = mybir.dt.bfloat16
AF = mybir.ActivationFunctionType
ALU = mybir.AluOpType
AX = mybir.AxisListType.X

NC = 8
S = 8192
DM = 4096
H = 32
HKV = 8
HD = 128
NB = 64
TOK = 1024
NT = 8

LN_EPS = 1e-5
QK_EPS = 1e-6
RMS_EPS = 1.1920929e-07
TAU = 0.1
SINK_ITERS = 50
NEG = -1e30
DEN_GUARD = 1e-30
ISQD = 1.0 / math.sqrt(HD)

SLOPES = [2.0 ** (-(h + 1) / 4.0) for h in range(H)]

def _chunk_w(h):
    for w in (128, 64, 32):
        if SLOPES[h] * (w - 1) <= 76.0:
            return w
    return 32
CHUNK_W = [_chunk_w(h) for h in range(H)]

T_SLOT = [64 - 8 * j for j in range(NT)]
WINDOW_TILES = [min(64, int(math.ceil(90.0 / SLOPES[h] / 128.0)) + 1) for h in range(H)]

def blocks_of_core(c):
    return [63 - c, 48 + c, 47 - c, 32 + c, 31 - c, 16 + c, 15 - c, c]

OWNER = [0] * NB
SLOT = [0] * NB
for _c in range(NC):
    for _j, _b in enumerate(blocks_of_core(_c)):
        OWNER[_b] = _c
        SLOT[_b] = _j
KCOL = [OWNER[g] * TOK + SLOT[g] * 128 for g in range(NB)]

def kept_lo(h, j):
    if WINDOW_TILES is None:
        return 0
    return max(0, T_SLOT[j] - 8 - WINDOW_TILES[h])

def active_slots(h, qg, t):
    out = []
    for j in range(4 * qg, 4 * qg + 4):
        if kept_lo(h, j) <= t < T_SLOT[j]:
            out.append(j)
    return out


def att_exp_cols():
    for qg in range(2):
        tmax = T_SLOT[4 * qg]
        for kv in range(HKV):
            for hg in range(4):
                h = 4 * kv + hg
                for t in range(tmax):
                    for j in active_slots(h, qg, t):
                        for ci in range(128 // CHUNK_W[h]):
                            yield (kv, h, qg, j, t, ci)

PLANE_COUNTS = {}
for _rec in att_exp_cols():
    _k = (_rec[2], _rec[0])
    PLANE_COUNTS[_k] = PLANE_COUNTS.get(_k, 0) + 1
PLANE_OFF = {}
_off = 0
for _qg in range(2):
    for _kv in range(HKV):
        PLANE_OFF[(_qg, _kv)] = _off
        _off += PLANE_COUNTS[(_qg, _kv)]
PLANE_TOTAL = _off
PLANE_MAX = max(PLANE_COUNTS.values())



def build_plane_rhs(c):
    blocks = blocks_of_core(c)
    r0 = np.zeros(PLANE_TOTAL, np.float32)
    r1 = np.zeros(PLANE_TOTAL, np.float32)
    i = 0
    for (kv, h, qg, j, t, ci) in att_exp_cols():
        b = blocks[j]
        w = CHUNK_W[h]
        if t > b:
            r0[i] = 0.0
            r1[i] = NEG
        else:
            cq = b * 128 + ci * w + w // 2
            r0[i] = SLOPES[h]
            r1[i] = SLOPES[h] * (128.0 * t - cq)
        i += 1
    return np.stack([r0, r1]).astype(np.float32)


def build_masks(c):
    blocks = blocks_of_core(c)
    tri = (np.arange(128)[:, None] <= np.arange(128)[None, :]).astype(np.float32)
    out = np.zeros((128, NT * 8 * 128), np.float32)
    for j in range(NT):
        b = blocks[j]
        for r8 in range(8):
            t = T_SLOT[j] - 8 + r8
            col = (j * 8 + r8) * 128
            if t < b:
                out[:, col:col + 128] = 1.0
            elif t == b:
                out[:, col:col + 128] = tri
    return out.astype(ml_dtypes.bfloat16)



class Sem:
    def __init__(self, st, nc, name):
        self.h = st.enter_context(nc.semaphore(name))
        self.n = 0
        self._wcache = {}

    def inc(self, instr, k=1):
        instr.then_inc(self.h, k)
        self.n += k

    def wait(self, eng, val=None):
        v = self.n if val is None else val
        key = id(eng)
        if self._wcache.get(key, -1) >= v:
            return
        self._wcache[key] = v
        eng.wait_ge(self.h, v)


class Ring:
    def __init__(self, st, nc, name, n):
        self.s = [Sem(st, nc, f"{name}{i}") for i in range(n)]
        self.n = n

    def inc(self, instr, slot, k=16):
        self.s[slot % self.n].inc(instr, k)

    def wait(self, eng, slot):
        sm = self.s[slot % self.n]
        sm.wait(eng)

    def wait_all(self, eng):
        for sm in self.s:
            sm.wait(eng)


def W(eng, sem, v):
    sem.wait(eng, v)


def build_program(taps=(), phase_limit=None):
    nc = bass.Bass("TRN2", target_bir_lowering=False, debug=False)

    def I(name, shape, dt=F32):
        return nc.dram_tensor(name, shape, dt, kind="ExternalInput").ap()

    x_in = I("x_own", [TOK, DM])
    wsl_in = I("w_slice", [1280, DM])
    wq_in = I("L0_attn_Wq", [DM, DM])
    wk_in = I("L0_attn_Wk", [HKV * HD, DM])
    wv_in = I("L0_attn_Wv", [HKV * HD, DM])
    wo_in = I("L0_attn_Wo", [DM, DM])
    qnw_in = I("L0_attn_q_norm_w", [1, HD])
    knw_in = I("L0_attn_k_norm_w", [1, HD])
    wres_in = I("L0_ffn_W_res", [16, DM])
    wpre_in = I("L0_ffn_W_pre", [4, DM])
    wpost_in = I("L0_ffn_W_post", [4, DM])
    hresb_in = I("L0_ffn_H_res_base", [1, 16])
    hpreb_in = I("L0_ffn_H_pre_base", [1, 4])
    hpostb_in = I("L0_ffn_H_post_base", [1, 4])
    gate_in = I("L0_ffn_gate", [1, 1])
    wb_in = I("L0_ffn_W_branch", [1024, 1024])
    ident_in = I("c_ident", [128, 128], BF16)
    onesf_in = I("c_ones_f32", [1, 128])
    k2_in = I("c_k2", [2, 128])
    prhs_in = I("c_plane_rhs", [2, PLANE_TOTAL])
    mask_in = I("c_masks", [128, NT * 8 * 128], BF16)

    y_out = nc.dram_tensor("y_own", [TOK, DM], F32, kind="ExternalOutput").ap()

    wtr_kv = nc.dram_tensor("wtr_kv", [DM, 256], BF16).ap()
    wtr_all_kv = nc.dram_tensor("wtr_all_kv", [NC * DM, 256], BF16).ap()
    wtr_q = nc.dram_tensor("wtr_q", [DM, 512], BF16).ap()
    wtr_all_q = nc.dram_tensor("wtr_all_q", [NC * DM, 512], BF16).ap()
    wtr_o = nc.dram_tensor("wtr_o", [DM, 512], BF16).ap()
    wtr_all_o = nc.dram_tensor("wtr_all_o", [NC * DM, 512], BF16).ap()
    xn_dram = nc.dram_tensor("xn_dram", [TOK, DM], BF16).ap()
    q_dram = nc.dram_tensor("q_dram", [128, 2 * H * 512], BF16).ap()
    attn_dram = nc.dram_tensor("attn_dram", [TOK, DM], BF16).ap()
    ffn_dram = nc.dram_tensor("ffn_dram", [TOK, DM], F32).ap()
    kv_send = nc.dram_tensor("kv_send", [2 * HKV * HD, TOK], BF16).ap()
    kv_all = nc.dram_tensor("kv_all", [NC * 2 * HKV * HD, TOK], BF16).ap()

    tap_map = {
        "xn": (xn_dram, [TOK, DM], BF16),
        "q": (q_dram, [128, 2 * H * 512], BF16),
        "attn": (attn_dram, [TOK, DM], BF16),
        "ffn": (ffn_dram, [TOK, DM], F32),
        "kvall": (kv_all, [NC * 2 * HKV * HD, TOK], BF16),
    }
    tap_outs = []
    tap_sb = []
    for tname in taps:
        if tname == "hco":
            tap_sb.append(("hco", nc.dram_tensor("tap_hco", [128, 256], F32, kind="ExternalOutput").ap()))
            continue
        if tname == "sink":
            tap_sb.append(("sink", nc.dram_tensor("tap_sink", [128, 1152], F32, kind="ExternalOutput").ap()))
            continue
        tsrc, shp, dt = tap_map[tname]
        tap_outs.append((nc.dram_tensor("tap_" + tname, shp, dt, kind="ExternalOutput").ap(), tsrc))

    st = contextlib.ExitStack()
    with st:
        A = st.enter_context(nc.sbuf_tensor("arenaA", [128, 32768], BF16))
        B = st.enter_context(nc.sbuf_tensor("arenaB", [128, 24576], BF16))
        Q = st.enter_context(nc.sbuf_tensor("arenaQ", [128, 16384], BF16))
        C = st.enter_context(nc.sbuf_tensor("arenaC", [128, 16384], BF16))
        M = st.enter_context(nc.sbuf_tensor("arenaM", [128, 10240], BF16))

        names = ["s_ct", "s_x", "s_v", "s_a", "s_xs", "s_fm", "s_wck", "s_wcq",
                 "s_wco", "s_wb", "s_pe", "s_pp", "s_kvs", "s_cc", "s_kl",
                 "s_ql", "s_pl", "s_sc", "s_scf", "s_ex", "s_mk", "s_av",
                 "s_nf", "s_pb", "s_at", "s_ats", "s_ff", "s_y", "s_ld", "s_cm", "s_ws"]
        SM = {n: Sem(st, nc, n) for n in names}
        (s_ct, s_x, s_v, s_a, s_xs, s_fm, s_wck, s_wcq, s_wco, s_wb, s_pe,
         s_pp, s_kvs, s_cc, s_kl, s_ql, s_pl, s_sc, s_scf, s_ex, s_mk, s_av,
         s_nf, s_pb, s_at, s_ats, s_ff, s_y, s_ld, s_cm, s_ws) = (SM[n] for n in names)

        ring_x = Ring(st, nc, "rx", 2)
        ring_wt = Ring(st, nc, "rwt", 2)
        ring_kv = Ring(st, nc, "rkv", 2)
        ring_pr = Ring(st, nc, "rpr", 2)
        ring_at = Ring(st, nc, "rat", 4)
        ring_y = Ring(st, nc, "ry", 2)
        ring_ws = Ring(st, nc, "rws", 4)
        ring_wc = Ring(st, nc, "rwc", 2)

        T, V, SC, GP, SY = nc.tensor, nc.vector, nc.scalar, nc.gpsimd, nc.sync

        def finish_debug():
            for sm in SM.values():
                GP.wait_ge(sm.h, sm.n)
            for rg in (ring_x, ring_wt, ring_kv, ring_pr, ring_at, ring_y, ring_ws,
                       ring_wc):
                rg.wait_all(GP)
            for tp, tsrc in tap_outs:
                s_y.inc(GP.dma_start(tp, tsrc), 16)
            for kind, tp in tap_sb:
                if kind == "hco":
                    s_y.inc(GP.dma_start(tp, hcoef[:, 0:256]), 16)
                else:
                    s_y.inc(GP.dma_start(tp, sink[:, 0:1152]), 16)
            s_y.wait(GP, s_y.n)

        _mo = [0]
        def mslice(n_bf16, dt=BF16):
            o = _mo[0]
            _mo[0] += (n_bf16 + 15) & ~15
            assert _mo[0] <= 10240, f"M overflow {_mo[0]}"
            ap = M[:, o:o + n_bf16]
            return ap.bitcast(dt) if dt != BF16 else ap

        ident = mslice(128)
        onesf = mslice(256, F32)
        k2 = mslice(256, F32)
        onescol = mslice(16)
        qnw = mslice(16, F32)
        nrow = mslice(512, F32)
        shift_b = mslice(16, F32)
        stats = mslice(160, F32)
        hcoef = mslice(512, F32)
        gcol = mslice(16, F32)
        waT = mslice(768)
        sink = mslice(2304, F32)
        dencol = mslice(32, F32)
        zrow = mslice(288)

        ST_BN, ST_AG, ST_R2, ST_SCR = 0, 48, 50, 58
        HC_RES, HC_PRE, HC_POST, HC_BASE = 0, 128, 160, 192

        CONST_VALS = [LN_EPS, RMS_EPS, QK_EPS, -0.5, -1.0, 1.0 / HD, ISQD,
                      1.0, 0.0, DEN_GUARD]
        cvals = mslice(16 * ((2 * len(CONST_VALS) + 15) // 16), F32)
        for ci_, val_ in enumerate(CONST_VALS):
            nc.const_aps.register(F32, val_, cvals[:, ci_:ci_ + 1])

        hco = hcoef
        assert PLANE_MAX <= 2496, PLANE_MAX

        for ci_, val_ in enumerate(CONST_VALS):
            GP.memset(cvals[:, ci_:ci_ + 1], float(val_))
        GP.memset(onescol[:, 0:1], 1.0)
        GP.memset(sink[:, 0:1152], 0.0)
        GP.memset(hco[:, 0:256], 0.0)
        GP.memset(zrow[:, 0:288], 0.0)
        s_cm.inc(GP.memset(onescol[:, 1:2], 0.0), 1)
        s_cm.wait(SY, 1)
        s_ct.inc(SY.dma_start(ident, ident_in), 16)
        s_ct.inc(SY.dma_start(onesf[0:1, 0:128], onesf_in), 16)
        s_ct.inc(SY.dma_start(k2[0:2, 0:128], k2_in), 16)
        s_ct.inc(SY.dma_start(qnw[:, 0:1], qnw_in.rearrange("o (f u) -> (o f) u", u=1)), 16)
        s_ct.inc(SY.dma_start(qnw[:, 1:2], knw_in.rearrange("o (f u) -> (o f) u", u=1)), 16)
        s_ct.inc(SY.dma_start(nrow[0:1, 0:128], qnw_in), 16)
        s_ct.inc(SY.dma_start(nrow[0:1, 128:256], knw_in), 16)
        s_ct.inc(SY.dma_start(hco[0:1, HC_BASE:HC_BASE + 16], hresb_in), 16)
        s_ct.inc(SY.dma_start(hco[0:1, HC_BASE + 16:HC_BASE + 20], hpreb_in), 16)
        s_ct.inc(SY.dma_start(hco[0:1, HC_BASE + 20:HC_BASE + 24], hpostb_in), 16)
        s_ct.inc(SY.dma_start(gcol[0:1, 3:4], gate_in), 16)

        s_ct.wait(V)
        s_v.inc(V.reduce_max(gcol[0:1, 4:5], nrow[0:1, 0:128], axis=AX,
                             apply_absolute_value=True))
        s_v.inc(V.reduce_max(gcol[0:1, 5:6], nrow[0:1, 128:256], axis=AX,
                             apply_absolute_value=True))
        s_v.wait(V)
        s_v.inc(V.tensor_mul(gcol[0:1, 6:7], gcol[0:1, 4:5], gcol[0:1, 5:6]))
        s_v.wait(V)
        s_v.inc(V.tensor_scalar_mul(gcol[0:1, 7:8], gcol[0:1, 6:7],
                                    float(math.sqrt(128.0))))

        RT_ORDER = [4, 5, 0, 1, 2, 3, 6, 7, 8, 9]

        def wdst(rt):
            if rt == 4:
                return wtr_kv, 0
            if rt == 5:
                return wtr_kv, 128
            if rt < 4:
                return wtr_q, rt * 128
            return wtr_o, (rt - 6) * 128

        xn_fm = A[:, 0:32768]
        with contextlib.ExitStack() as pws:
            wtp = [pws.enter_context(nc.psum_tensor(f"wtp{i}", [128, 128], BF16))
                   for i in range(4)]
            xtp = [pws.enter_context(nc.psum_tensor(f"xtp{i}", [128, 128], BF16))
                   for i in range(4)]
            wrowb = [C[:, 0:4096], C[:, 4096:8192]]
            wslab = [B[:, 16384:20480], B[:, 20480:24576]]

            def p0w_gen():
                ev_w = {}
                tr_end = {}
                s_ct.wait(T)
                for i, rt in enumerate(RT_ORDER):
                    if i >= 2:
                        s_pe.wait(GP, tr_end[i - 2])
                    ring_wc.inc(GP.dma_start(wrowb[i % 2],
                                             wsl_in[rt * 128:(rt + 1) * 128, :]), i)
                    ring_wc.wait(T, i)
                    for fc in range(32):
                        u = i * 32 + fc
                        if u >= 4:
                            s_v.wait(T, ev_w[u - 4])
                        s_pe.inc(T.transpose(wtp[u % 4][:, 0:128],
                                             wrowb[i % 2][:, fc * 128:(fc + 1) * 128],
                                             ident))
                        s_pe.wait(V, s_pe.n)
                        if fc == 0 and i >= 2:
                            ring_ws.wait(V, i - 2)
                        s_v.inc(V.tensor_copy(wslab[i % 2][:, fc * 128:(fc + 1) * 128],
                                              wtp[u % 4][:, 0:128]))
                        ev_w[u] = s_v.n
                    tr_end[i] = s_pe.n
                    s_v.wait(SY, s_v.n)
                    dstt, c0 = wdst(rt)
                    ring_ws.inc(SY.dma_start(
                        dstt[:, c0:c0 + 128].rearrange("(f p) c -> p f c", p=128),
                        wslab[i % 2].rearrange("p (f c) -> p f c", c=128)), i)
                    if i == 1:
                        ring_ws.wait(GP, 0)
                        ring_ws.wait(GP, 1)
                        s_ws.inc(GP.collective_compute(
                            "AllGather", ALU.bypass,
                            replica_groups=[list(range(NC))],
                            ins=[wtr_kv.opt()], outs=[wtr_all_kv.opt()],
                        ))
                    if i == 5:
                        ring_ws.wait_all(GP)
                        s_ws.inc(GP.collective_compute(
                            "AllGather", ALU.bypass,
                            replica_groups=[list(range(NC))],
                            ins=[wtr_q.opt()], outs=[wtr_all_q.opt()],
                        ))
                    yield

            pgen = p0w_gen()

            s_ct.wait(SC)
            s_cm.wait(SC)
            s_cm.wait(V)
            s_ct.wait(T)
            s_cm.wait(T)
            s_ct.wait(GP)
            xbuf = [B[:, 0:8192].bitcast(F32), B[:, 8192:16384].bitcast(F32)]
            xnb = [Q[:, 0:4096], Q[:, 4096:8192]]
            dump = sink[:, 0:1024]
            ev_xn = {}
            tr_xn = {}
            for tt in range(NT):
                par = tt % 2
                xb = xbuf[par]
                if tt >= 2:
                    s_a.wait(SY, a_done_xn[tt - 2])
                ring_x.inc(SY.dma_start(xb, x_in[tt * 128:(tt + 1) * 128, :]), tt)
                ring_x.wait(V, tt)
                if tt >= 1:
                    s_a.wait(V)
                for ch in range(8):
                    s_v.inc(V.bn_stats(stats[:, ST_BN + ch * 6:ST_BN + (ch + 1) * 6],
                                       xb[:, ch * 512:(ch + 1) * 512]))
                s_v.wait(V)
                s_v.inc(V.bn_aggr(stats[:, ST_AG:ST_AG + 2], stats[:, ST_BN:ST_BN + 48]))
                s_v.wait(V)
                cm = stats[:, ST_AG + 0:ST_AG + 1]
                cvv = stats[:, ST_AG + 1:ST_AG + 2]
                crstd = stats[:, ST_SCR + 2:ST_SCR + 3]
                cbias = stats[:, ST_SCR + 3:ST_SCR + 4]
                s_v.wait(SC)
                s_a.inc(SC.activation(stats[:, ST_SCR + 4:ST_SCR + 5], cvv, AF.Ln, bias=LN_EPS))
                s_a.wait(SC)
                s_a.inc(SC.activation(crstd, stats[:, ST_SCR + 4:ST_SCR + 5], AF.Exp, scale=-0.5))
                s_a.wait(V)
                s_v.inc(V.scalar_tensor_tensor(cbias, cm, -1.0, crstd, ALU.mult, ALU.mult))
                s_v.inc(V.scalar_tensor_tensor(stats[:, ST_SCR + 5:ST_SCR + 6], crstd, 1.0, crstd,
                                               ALU.mult, ALU.mult))
                s_v.wait(V)
                s_v.inc(V.tensor_mul(stats[:, ST_SCR + 5:ST_SCR + 6], stats[:, ST_SCR + 5:ST_SCR + 6], cvv))
                s_v.wait(SC)
                s_a.inc(SC.activation(stats[:, ST_SCR + 6:ST_SCR + 7], stats[:, ST_SCR + 5:ST_SCR + 6],
                                      AF.Ln, bias=RMS_EPS))
                s_a.wait(SC)
                s_a.inc(SC.activation(stats[:, ST_R2 + tt:ST_R2 + tt + 1], stats[:, ST_SCR + 6:ST_SCR + 7],
                                      AF.Exp, scale=-0.5))
                s_v.wait(SC)
                s_xs.wait(SC, s_xs.n)
                if tt >= 2:
                    s_pe.wait(SC, tr_xn[tt - 2])
                xo = xnb[par]
                s_a.inc(SC.activation(xo, xb, AF.Identity, bias=cbias, scale=crstd))
                if tt == 0:
                    a_done_xn = {}
                a_done_xn[tt] = s_a.n
                s_a.wait(SC, s_a.n)
                s_xs.inc(SC.dma_start(xn_dram[tt * 128:(tt + 1) * 128, :], xo), 16)
                s_a.wait(T, a_done_xn[tt])
                for fc in range(32):
                    u = tt * 32 + fc
                    if u >= 4:
                        s_fm.wait(T, ev_xn[u - 4])
                    s_pe.inc(T.transpose(xtp[u % 4][:, 0:128],
                                         xo[:, fc * 128:(fc + 1) * 128], ident))
                    s_pe.wait(V, s_pe.n)
                    s_fm.inc(V.tensor_copy(
                        xn_fm[:, fc * 1024 + tt * 128: fc * 1024 + (tt + 1) * 128],
                        xtp[u % 4][:, 0:128]))
                    ev_xn[u] = s_fm.n
                tr_xn[tt] = s_pe.n
                next(pgen, None)
                next(pgen, None)
            for _ in pgen:
                pass
        p0_pe_end = s_pe.n

        if phase_limit == "P0":
            finish_debug()
            return nc
        if phase_limit == "P1":
            finish_debug()
            return nc
        wsm_f = B[:, 0:8192].bitcast(F32)
        wsm_b = B[:, 8192:16384]
        s_v.wait(SY, s_v.n)
        s_ct.inc(SY.dma_start(wsm_f[0:16, :], wres_in), 16)
        s_ct.inc(SY.dma_start(wsm_f[16:20, :], wpre_in), 16)
        s_ct.inc(SY.dma_start(wsm_f[20:24, :], wpost_in), 16)
        s_ct.wait(V)
        s_v.inc(V.tensor_copy(wsm_b[0:32, 0:4096], wsm_f[0:32, :]))

        Zf = sink
        U0, V0, SCR, R64, R32a, R32b, R32c = 128, 160, 192, 320, 384, 416, 448
        ZP, ZQ = 480, 512

        with contextlib.ExitStack() as pst:
            ps_a = pst.enter_context(nc.psum_tensor("ps_a", [128, 512], F32))
            ps_b = pst.enter_context(nc.psum_tensor("ps_b", [128, 512], F32))
            ps_tb = [pst.enter_context(nc.psum_tensor(f"ps_tb{i}", [128, 64], BF16))
                     for i in range(2)]
            psl = [ps_a, ps_b]

            s_v.wait(T)
            s_fm.wait(T)
            s_ct.wait(T)
            for fc in range(32):
                if fc >= 2:
                    s_pp.wait(T, s_pp.n - 1)
                s_pe.inc(T.transpose(ps_tb[fc % 2][:, 0:32],
                                     wsm_b[0:32, fc * 128:(fc + 1) * 128], ident[0:32, 0:32]))
                s_pe.wait(V, s_pe.n)
                s_pp.inc(V.tensor_copy(waT[:, fc * 24:(fc + 1) * 24],
                                       ps_tb[fc % 2][:, 0:24]))

            s_pp.wait(T, s_pp.n)
            s_pe.inc(T.matmul(ps_a[:, 0:24], onesf[0:1, 0:128],
                              hco[0:1, HC_BASE:HC_BASE + 24], start=True, stop=True))
            s_pe.inc(T.matmul(ps_b[:, 0:1], onesf[0:1, 0:128], gcol[0:1, 3:4],
                              start=True, stop=True))
            s_pe.wait(V, s_pe.n)
            s_v.inc(V.tensor_copy(hco[:, HC_BASE:HC_BASE + 24], ps_a[:, 0:24]))
            s_pe.wait(SC, s_pe.n)
            s_a.inc(SC.activation(gcol[:, 4:5], ps_b[:, 0:1], AF.Exp, scale=-1.0))
            s_a.wait(V)
            s_v.inc(V.tensor_scalar(gcol[:, 5:6], gcol[:, 4:5], 1.0, None, ALU.add))
            s_v.wait(V)
            s_v.inc(V.reciprocal(gcol[:, 0:1], gcol[:, 5:6]))
            s_v.wait(V)
            s_v.inc(V.tensor_scalar(gcol[:, 1:2], gcol[:, 0:1], -1.0, 1.0, ALU.mult, ALU.add))
            s_v.wait(V)

            s_fm.wait(T)
            s_v.wait(T, s_v.n)
            s_a.wait(T, s_a.n)
            logit_done = {}
            for tt in range(NT):
                ps = psl[tt % 2]
                if tt >= 2:
                    s_v.wait(T, logit_done[tt - 2])
                mm = None
                for fc in range(32):
                    mm = T.matmul(ps[:, 0:24],
                                  xn_fm[:, fc * 1024 + tt * 128: fc * 1024 + (tt + 1) * 128],
                                  waT[:, fc * 24:(fc + 1) * 24],
                                  start=(fc == 0), stop=(fc == 31))
                s_pe.inc(mm)
                s_pe.wait(V, s_pe.n)
                r2 = stats[:, ST_R2 + tt:ST_R2 + tt + 1]
                s_v.inc(V.scalar_tensor_tensor(Zf[:, tt * 16:(tt + 1) * 16], ps[:, 0:16],
                                               r2, hco[:, HC_BASE:HC_BASE + 16],
                                               ALU.mult, ALU.add))
                s_v.inc(V.scalar_tensor_tensor(Zf[:, ZP + tt * 4:ZP + (tt + 1) * 4], ps[:, 16:20],
                                               r2, hco[:, HC_BASE + 16:HC_BASE + 20],
                                               ALU.mult, ALU.add))
                s_v.inc(V.scalar_tensor_tensor(Zf[:, ZQ + tt * 4:ZQ + (tt + 1) * 4], ps[:, 20:24],
                                               r2, hco[:, HC_BASE + 20:HC_BASE + 24],
                                               ALU.mult, ALU.add))
                logit_done[tt] = s_v.n

            s_v.wait(V)
            if phase_limit == "P2a_logits":
                finish_debug()
                return nc
            s_v.inc(V.tensor_scalar_mul(Zf[:, 0:128], Zf[:, 0:128], 1.0 / TAU))
            s_v.inc(V.memset(Zf[:, U0:U0 + 32], 0.0))
            s_v.inc(V.memset(Zf[:, V0:V0 + 32], 0.0))
            s_v.wait(V)

            z4 = Zf[:, 0:128].rearrange("p (g i j) -> p g i j", g=8, i=4, j=4)
            z4T = Zf[:, 0:128].rearrange("p (g i j) -> p g j i", g=8, i=4, j=4)
            sc4 = Zf[:, SCR:SCR + 128].rearrange("p (g i j) -> p g i j", g=8, i=4, j=4)
            sc4T = Zf[:, SCR:SCR + 128].rearrange("p (g i j) -> p g j i", g=8, i=4, j=4)
            scf = Zf[:, SCR:SCR + 128]
            u3 = Zf[:, U0:U0 + 32].rearrange("p (g i) -> p g i", g=8)
            v3 = Zf[:, V0:V0 + 32].rearrange("p (g j) -> p g j", g=8)
            r64a = Zf[:, R64:R64 + 64]
            r32a = Zf[:, R32a:R32a + 32]
            r32b = Zf[:, R32b:R32b + 32]
            r32c = Zf[:, R32c:R32c + 32]

            def half_iter(out32, reduce_j):
                if reduce_j:
                    s_v.inc(V.tensor_tensor(sc4T, z4T, v3.broadcast_to([128, 8, 4, 4]), ALU.add))
                else:
                    s_v.inc(V.tensor_tensor(sc4, z4, u3.broadcast_to([128, 8, 4, 4]), ALU.add))
                s_v.wait(V)
                if reduce_j:
                    pv = scf.rearrange("p (x b) -> p x b", b=2)
                    s_v.inc(V.tensor_max(r64a, pv[:, :, 0], pv[:, :, 1]))
                    s_v.wait(V)
                    pv2 = r64a.rearrange("p (x b) -> p x b", b=2)
                    s_v.inc(V.tensor_max(r32a, pv2[:, :, 0], pv2[:, :, 1]))
                else:
                    g3a = Zf[:, R64:R64 + 32].rearrange("p (g j) -> p g j", g=8)
                    g3b = Zf[:, R64 + 32:R64 + 64].rearrange("p (g j) -> p g j", g=8)
                    s_v.inc(V.tensor_max(g3a, sc4[:, :, 0, :], sc4[:, :, 1, :]))
                    s_v.inc(V.tensor_max(g3b, sc4[:, :, 2, :], sc4[:, :, 3, :]))
                    s_v.wait(V)
                    s_v.inc(V.tensor_max(r32a, Zf[:, R64:R64 + 32], Zf[:, R64 + 32:R64 + 64]))
                s_v.wait(V)
                m3 = r32a.rearrange("p (g a) -> p g a", g=8)
                if reduce_j:
                    s_v.inc(V.tensor_tensor(sc4, sc4, m3.broadcast_to([128, 8, 4, 4]), ALU.subtract))
                else:
                    s_v.inc(V.tensor_tensor(sc4T, sc4T, m3.broadcast_to([128, 8, 4, 4]), ALU.subtract))
                s_v.wait(SC)
                s_a.inc(SC.activation(scf, scf, AF.Exp))
                s_a.wait(V)
                if reduce_j:
                    pv = scf.rearrange("p (x b) -> p x b", b=2)
                    s_v.inc(V.tensor_add(r64a, pv[:, :, 0], pv[:, :, 1]))
                    s_v.wait(V)
                    pv2 = r64a.rearrange("p (x b) -> p x b", b=2)
                    s_v.inc(V.tensor_add(r32b, pv2[:, :, 0], pv2[:, :, 1]))
                else:
                    g3a = Zf[:, R64:R64 + 32].rearrange("p (g j) -> p g j", g=8)
                    g3b = Zf[:, R64 + 32:R64 + 64].rearrange("p (g j) -> p g j", g=8)
                    s_v.inc(V.tensor_add(g3a, sc4[:, :, 0, :], sc4[:, :, 1, :]))
                    s_v.inc(V.tensor_add(g3b, sc4[:, :, 2, :], sc4[:, :, 3, :]))
                    s_v.wait(V)
                    s_v.inc(V.tensor_add(r32b, Zf[:, R64:R64 + 32], Zf[:, R64 + 32:R64 + 64]))
                s_v.wait(SC)
                s_a.inc(SC.activation(r32c, r32b, AF.Ln))
                s_a.wait(V)
                s_v.inc(V.tensor_add(r32c, r32c, r32a))
                s_v.wait(V)
                s_v.inc(V.tensor_scalar_mul(out32, r32c, -1.0))
                s_v.wait(V)

        def coef_gen():
            for _ in range(SINK_ITERS):
                half_iter(Zf[:, U0:U0 + 32], True)
                yield
                half_iter(Zf[:, V0:V0 + 32], False)
                yield

            s_v.inc(V.tensor_tensor(sc4, z4, u3.broadcast_to([128, 8, 4, 4]), ALU.add))
            s_v.wait(V)
            s_v.inc(V.tensor_tensor(sc4T, sc4T, v3.broadcast_to([128, 8, 4, 4]), ALU.add))
            s_v.wait(SC)
            s_a.inc(SC.activation(scf, scf, AF.Exp))
            s_a.wait(V)
            s_v.inc(V.tensor_scalar(hco[:, HC_RES:HC_RES + 128], scf, gcol[:, 0:1], None, ALU.mult))
            s_v.wait(V)
            hdiag = hco[:, HC_RES:HC_RES + 128].rearrange("p (g x) -> p g x", g=8)[:, :, 0:16:5]
            s_v.inc(V.tensor_scalar(hdiag, hdiag, gcol[:, 1:2], None, ALU.add))
            s_v.wait(V)
            yield

            def soft4(dst_off, src_off):
                sv = Zf[:, src_off:src_off + 32]
                p2 = sv.rearrange("p (x b) -> p x b", b=2)
                s_v.inc(V.tensor_max(Zf[:, R64:R64 + 16], p2[:, :, 0], p2[:, :, 1]))
                s_v.wait(V)
                p1 = Zf[:, R64:R64 + 16].rearrange("p (x b) -> p x b", b=2)
                s_v.inc(V.tensor_max(Zf[:, R32a:R32a + 8], p1[:, :, 0], p1[:, :, 1]))
                s_v.wait(V)
                s_v.inc(V.tensor_tensor(sv.rearrange("p (g j) -> p g j", g=8),
                                        sv.rearrange("p (g j) -> p g j", g=8),
                                        Zf[:, R32a:R32a + 8].broadcast_to([128, 8, 4]),
                                        ALU.subtract))
                s_v.wait(SC)
                s_a.inc(SC.activation(sv, sv, AF.Exp))
                s_a.wait(V)
                s_v.inc(V.tensor_add(Zf[:, R64:R64 + 16], p2[:, :, 0], p2[:, :, 1]))
                s_v.wait(V)
                s_v.inc(V.tensor_add(Zf[:, R32a:R32a + 8], p1[:, :, 0], p1[:, :, 1]))
                s_v.wait(V)
                s_v.inc(V.reciprocal(Zf[:, R32b:R32b + 8], Zf[:, R32a:R32a + 8]))
                s_v.wait(V)
                s_v.inc(V.tensor_tensor(hco[:, dst_off:dst_off + 32].rearrange("p (g j) -> p g j", g=8),
                                        sv.rearrange("p (g j) -> p g j", g=8),
                                        Zf[:, R32b:R32b + 8].broadcast_to([128, 8, 4]),
                                        ALU.mult))
                s_v.wait(V)

            soft4(HC_PRE, ZP)
            yield
            soft4(HC_POST, ZQ)
            s_v.inc(V.tensor_copy(hco[:, HC_BASE:HC_BASE + 1], hco[:, HC_BASE:HC_BASE + 1]))
            coef_done[0] = s_v.n
            yield

        coef_done = [None]
        cgen = coef_gen()

        if phase_limit == "P2a":
            for _ in cgen:
                pass
            finish_debug()
            return nc
        def proj(wbase, col_off, slice_w, fo_total, epilogue, gbase, bg=None,
                 bg_steps=0, ws_need=1, entry_guard=True):
            ngroups = fo_total // 256
            with contextlib.ExitStack() as ps_st:
                pp = [ps_st.enter_context(nc.psum_tensor(f"pp{gbase}_{i}", [128, 512], F32))
                      for i in range(2)]
                ps_ss = [ps_st.enter_context(nc.psum_tensor(f"pss{gbase}{i}", [1, 512], F32))
                         for i in range(1)]
                ps_iv = [ps_st.enter_context(nc.psum_tensor(f"piv{gbase}_{i}", [128, 512], F32))
                         for i in range(1)]
                wtb = [C[:, 8192:16384], C[:, 0:8192]]
                s_v.wait(T, s_v.n)
                s_a.wait(T, s_a.n)
                s_pp.wait(T, s_pp.n)
                if entry_guard:
                    s_pp.wait(SY, s_pp.n)
                    s_v.wait(SY, s_v.n)
                s_ws.wait(SY, ws_need)
                wt_done = {}
                unit_done = {}
                for g in range(ngroups):
                    fo0 = g * 256
                    if g == 1:
                        ring_wc.wait_all(SY)
                        s_pe.wait(SY, p0_pe_end)
                    if g >= 2:
                        sm_, ct_ = wt_done[g - 2]
                        SY.wait_ge(sm_.h, ct_)
                    for fc in range(32):
                        if slice_w == 512:
                            r_ = fo0 // 512
                            lo_ = fo0 % 512
                            ring_wt.inc(SY.dma_start(
                                wtb[g % 2][:, fc * 256:(fc + 1) * 256],
                                wbase[r_ * DM + fc * 128: r_ * DM + (fc + 1) * 128,
                                      col_off + lo_: col_off + lo_ + 256]), g)
                        else:
                            for hf in range(2):
                                r_ = (fo0 + hf * 128) // 128
                                ring_wt.inc(SY.dma_start(
                                    wtb[g % 2][:, fc * 256 + hf * 128: fc * 256 + (hf + 1) * 128],
                                    wbase[r_ * DM + fc * 128: r_ * DM + (fc + 1) * 128,
                                          col_off: col_off + 128]), g)
                    ring_wt.wait(T, g)
                    for blk in range(2):
                        for tch in range(2):
                            u = (g * 2 + blk) * 2 + tch
                            ps = pp[u % 2]
                            if u >= 2:
                                sm_, ct_ = unit_done[u - 2]
                                T.wait_ge(sm_.h, ct_)
                            mm = None
                            for fc in range(32):
                                mm = T.matmul(ps[:, 0:512],
                                              wtb[g % 2][:, fc * 256 + blk * 128: fc * 256 + (blk + 1) * 128],
                                              xn_fm[:, fc * 1024 + tch * 512: fc * 1024 + (tch + 1) * 512],
                                              start=(fc == 0), stop=(fc == 31))
                            s_pe.inc(mm)
                            ud = epilogue(g * 2 + blk, tch, ps, ps_ss[0], ps_iv[0])
                            unit_done[u] = ud
                            if bg is not None:
                                for _ in range(bg_steps):
                                    if next(bg, "DONE") == "DONE":
                                        bg = None
                                        break
                    wt_done[g] = ud

        qf32 = [nrow[:, 0:0], nrow[:, 0:0]]
        qsc = Q[:, 0:4096].bitcast(F32)
        qsq = Q[:, 4096:5120]
        ivrow = Q[:, 5120:7168].bitcast(F32)
        kvdst = {"v": B[:, 8192:16384], "k": B[:, 0:8192]}

        def plain_epi(dst):
            def epi(blk, tch, ps, ps_ss, ps_iv):
                s_pe.wait(V, s_pe.n)
                s_pp.inc(V.tensor_copy(dst[:, blk * 1024 + tch * 512: blk * 1024 + (tch + 1) * 512],
                                       ps[:, 0:512]))
                return (s_pp, s_pp.n)
            return epi

        def norm_epi(dst, wcol, to_dram):
            def epi(blk, tch, ps, ps_ss, ps_iv):
                par = (blk * 2 + tch) % 2
                qf = qsc[:, par * 512:(par + 1) * 512]
                ivr = ivrow[0:1, par * 512:(par + 1) * 512]
                s_pe.wait(V, s_pe.n)
                s_v.inc(V.tensor_copy(qf, ps[:, 0:512]))
                s_v.wait(V)
                s_v.inc(V.tensor_mul(qsq[:, par * 512:(par + 1) * 512], qf, qf))
                s_v.wait(T, s_v.n)
                s_a.wait(T, s_a.n)
                s_pe.inc(T.matmul(ps_ss[0:1, 0:512], onescol[:, 0:1],
                                  qsq[:, par * 512:(par + 1) * 512], start=True, stop=True))
                s_pe.wait(SC, s_pe.n)
                s_a.inc(SC.activation(ivr, ps_ss[0:1, 0:512], AF.Ln,
                                      bias=QK_EPS, scale=1.0 / HD))
                s_a.wait(SC)
                s_a.inc(SC.activation(ivr, ivr, AF.Exp, scale=-0.5))
                s_a.wait(T, s_a.n)
                s_pe.inc(T.matmul(ps_iv[:, 0:512], onesf[0:1, 0:128], ivr,
                                  start=True, stop=True))
                s_pe.wait(V, s_pe.n)
                if to_dram is not None:
                    s_ql.wait(V, s_ql.n)
                s_v.inc(V.scalar_tensor_tensor(dst(blk, tch), qf, qnw[:, wcol:wcol + 1],
                                               ps_iv[:, 0:512], ALU.mult, ALU.mult))
                ret = (s_v, s_v.n)
                if to_dram is not None:
                    s_v.wait(SC, s_v.n)
                    s_ql.inc(SC.dma_start(to_dram(blk, tch), dst(blk, tch)), 16)
                return ret
            return epi

        s_fm.wait(T)
        proj(wtr_all_kv, 128, 128, HKV * HD, plain_epi(kvdst["v"]), "v", bg=cgen,
             bg_steps=4, entry_guard=False)
        s_xs.wait(V)
        s_xs.wait(SC)
        kslice = lambda blk, tch: kvdst["k"][:, blk * 1024 + tch * 512: blk * 1024 + (tch + 1) * 512]
        proj(wtr_all_kv, 0, 128, HKV * HD, norm_epi(kslice, 1, None), "k", bg=cgen, bg_steps=4)
        for _ in cgen:
            pass

        vtm = B[:, 16384:24576]
        ring_ws.wait_all(V)
        with contextlib.ExitStack() as pvs:
            vtp = [pvs.enter_context(nc.psum_tensor(f"vtp{i}", [128, 128], BF16))
                   for i in range(2)]
            s_v.wait(T, s_v.n)
            s_a.wait(T, s_a.n)
            vt_done = {}
            for kv in range(HKV):
                for j in range(NT):
                    u = kv * NT + j
                    if u >= 2:
                        s_v.wait(T, vt_done[u - 2])
                    s_pe.inc(T.transpose(vtp[u % 2][:, 0:128],
                                         kvdst["v"][:, kv * 1024 + j * 128: kv * 1024 + (j + 1) * 128],
                                         ident))
                    s_pe.wait(V, s_pe.n)
                    s_v.inc(V.tensor_copy(vtm[:, (j * 8 + kv) * 128:(j * 8 + kv + 1) * 128],
                                          vtp[u % 2][:, 0:128]))
                    vt_done[u] = s_v.n
        s_pp.wait(SC, s_pp.n)
        s_v.wait(SC, s_v.n)
        for kv in range(HKV):
            s_kvs.inc(SC.dma_start(kv_send[kv * 128:(kv + 1) * 128, :],
                                   kvdst["k"][:, kv * 1024:(kv + 1) * 1024]), 16)
        for j in range(NT):
            s_kvs.inc(SC.dma_start(kv_send[1024 + j * 128:1024 + (j + 1) * 128, :],
                                   vtm[:, j * 1024:(j + 1) * 1024]), 16)
        s_kvs.wait(GP, s_kvs.n)
        s_cc.inc(GP.collective_compute(
            "AllGather", ALU.bypass,
            replica_groups=[list(range(NC))],
            ins=[kv_send.opt()], outs=[kv_all.opt()],
        ))

        s_fm.wait(GP)
        s_xs.wait(GP)
        for ci in range(8):
            s_wb.inc(GP.dma_start(Q[:, 7680 + ci * 1024:7680 + (ci + 1) * 1024],
                                  wb_in[ci * 128:(ci + 1) * 128, :]), 16)
        wbsb = Q[:, 7680:15872]
        xnt = [B[:, 0:4096], B[:, 4096:8192]]
        xpf = B[:, 8192:12288].bitcast(F32)
        xpre_b = B[:, 12288:13312]
        xprefm = B[:, 13312:14336]
        yv = xpf[:, 1024:2048]
        fo_f = B[:, 14336:22528].bitcast(F32)

        def mix_gen(pf, pfb):
            s_kvs.wait(SY, s_kvs.n)
            v_used_xnt = {}
            for tt in range(NT):
                par = tt % 2
                if tt >= 2:
                    s_v.wait(SY, v_used_xnt[tt - 2])
                ring_x.inc(SY.dma_start(xnt[par], xn_dram[tt * 128:(tt + 1) * 128, :]), tt)
                ring_x.wait(V, tt)
                if tt >= 1:
                    s_ff.wait(V, s_ff.n)
                hpre = lambda n: hco[:, HC_PRE + tt * 4 + n: HC_PRE + tt * 4 + n + 1]
                hpost = lambda n: hco[:, HC_POST + tt * 4 + n: HC_POST + tt * 4 + n + 1]
                hres = lambda n, jj: hco[:, HC_RES + tt * 16 + n * 4 + jj: HC_RES + tt * 16 + n * 4 + jj + 1]
                s_v.inc(V.tensor_scalar(xpf[:, 0:1024], xnt[par][:, 0:1024], hpre(0), None, ALU.mult))
                for n in range(1, 4):
                    s_v.wait(V)
                    s_v.inc(V.scalar_tensor_tensor(xpf[:, 0:1024], xnt[par][:, n * 1024:(n + 1) * 1024],
                                                   hpre(n), xpf[:, 0:1024], ALU.mult, ALU.add))
                s_v.wait(V)
                s_v.inc(V.tensor_copy(xpre_b, xpf[:, 0:1024]))
                s_v.wait(T)
                yield
                for cc in range(8):
                    s_pe.inc(T.transpose(pfb[cc % 2][:, 0:128],
                                         xpre_b[:, cc * 128:(cc + 1) * 128], ident))
                    s_pe.wait(V, s_pe.n)
                    s_v.inc(V.tensor_copy(xprefm[:, cc * 128:(cc + 1) * 128],
                                          pfb[cc % 2][:, 0:128]))
                    if cc >= 1:
                        s_v.wait(T, s_v.n - 1)
                s_v.wait(T, s_v.n)
                s_wb.wait(T, s_wb.n)
                yield
                for co in range(2):
                    ps = pf[co % 2]
                    mm = None
                    for cc in range(8):
                        mm = T.matmul(ps[:, 0:512], xprefm[:, cc * 128:(cc + 1) * 128],
                                      wbsb[:, cc * 1024 + co * 512: cc * 1024 + (co + 1) * 512],
                                      start=(cc == 0), stop=(cc == 7))
                    s_pe.inc(mm)
                    s_pe.wait(V, s_pe.n)
                    s_v.inc(V.tensor_copy(yv[:, co * 512:(co + 1) * 512], ps[:, 0:512]))
                s_v.wait(V)
                yield
                for n in range(4):
                    fon = fo_f[:, n * 1024:(n + 1) * 1024]
                    s_v.inc(V.tensor_scalar(fon, xnt[par][:, 0:1024], hres(n, 0), None, ALU.mult))
                    for jj in range(1, 4):
                        s_v.wait(V)
                        s_v.inc(V.scalar_tensor_tensor(fon, xnt[par][:, jj * 1024:(jj + 1) * 1024],
                                                       hres(n, jj), fon, ALU.mult, ALU.add))
                    s_v.wait(V)
                    s_v.inc(V.scalar_tensor_tensor(fon, yv, hpost(n), fon, ALU.mult, ALU.add))
                    s_v.wait(V)
                v_used_xnt[tt] = s_v.n
                s_v.wait(GP, s_v.n)
                s_ff.inc(GP.dma_start(ffn_dram[tt * 128:(tt + 1) * 128, :], fo_f), 16)
                yield

        qslice = lambda blk, tch: qsq[:, 0:512] if False else qsc.bitcast(BF16)[:, 2048 + 0:2048 + 512]
        qtmp = Q[:, 7168:7680]
        qsl = lambda blk, tch: qtmp[:, 0:512]
        qdr = lambda blk, tch: q_dram[:, (tch * H + blk) * 512:(tch * H + blk + 1) * 512]
        with contextlib.ExitStack() as ps_mx:
            pf_ = [ps_mx.enter_context(nc.psum_tensor(f"pf{i}", [128, 512], F32)) for i in range(2)]
            pfb_ = [ps_mx.enter_context(nc.psum_tensor(f"pfb{i}", [128, 128], BF16)) for i in range(2)]
            mgen = mix_gen(pf_, pfb_)
            proj(wtr_all_q, 0, 512, DM, norm_epi(qsl, 0, qdr), "q", bg=mgen,
                 bg_steps=2, ws_need=2)
            for _ in mgen:
                pass

        if phase_limit == "P3":
            finish_debug()
            return nc

        if phase_limit == "P2b":
            finish_debug()
            return nc
        VOFF = [0, 9216]
        POFF = 18432
        ATOFF = 21504
        PLOFF = [22016, 23680]
        masksb = B[:, 16384:24576]
        qslab = Q[:, 0:16384]

        with contextlib.ExitStack() as ps_st:
            ps_sc = [ps_st.enter_context(nc.psum_tensor(f"psc{i}", [128, 512], F32)) for i in range(5)]
            ps_av = [ps_st.enter_context(nc.psum_tensor(f"pav{i}", [128, 512], F32)) for i in range(2)]
            ps_pl = ps_st.enter_context(nc.psum_tensor("pspl", [128, 512], F32))

            s_pe.wait(SY, s_pe.n)
            s_ff.wait(SY, s_ff.n)
            s_v.wait(SY, s_v.n)
            s_kvs.wait(SY, s_kvs.n)
            s_ct.inc(SY.dma_start(B[:, 16384:24576], mask_in), 16)
            ring_ws.wait_all(GP)
            s_ws.inc(GP.collective_compute(
                "AllGather", ALU.bypass,
                replica_groups=[list(range(NC))],
                ins=[wtr_o.opt()], outs=[wtr_all_o.opt()],
            ))
            s_ct.wait(GP, s_ct.n)

            s_pe.wait(GP, s_pe.n)
            s_v.wait(GP, s_v.n)
            for par_ in range(2):
                vv = A[:, VOFF[par_]:VOFF[par_] + 9216].rearrange(
                    "p (t c) -> p t c", c=144)[:, :, 128:129]
                s_cm.inc(GP.memset(vv, 1.0), 1)
            s_cm.wait(T)

            score_cnt_end = {}
            av_cnt_end = {}
            ex_cnt_end = {}
            KEYSEQ = [(qg_, kv_) for qg_ in range(2) for kv_ in range(HKV)]

            def load_kv(qg, kv):
                seq = KEYSEQ.index((qg, kv))
                par = seq % 2
                kb = C[:, par * 8192:(par + 1) * 8192]
                s_cc.wait(SY, 1)
                if seq >= 2:
                    key = KEYSEQ[seq - 2]
                    s_sc.wait(SY, score_cnt_end[key])
                    s_av.wait(SY, av_cnt_end[key])
                jlo = 0 if qg == 0 else 4
                nj = NT - jlo
                for r in range(NC):
                    ring_kv.inc(SY.dma_start(kb[:, r * 1024:(r + 1) * 1024],
                                             kv_all[r * 2048 + kv * 128: r * 2048 + (kv + 1) * 128, :]), seq)
                    vsrc = kv_all[r * 2048 + 1024 + jlo * 128: r * 2048 + 2048,
                                  kv * 128:(kv + 1) * 128].rearrange(
                        "(j p) h -> p j h", p=128)
                    vdst = A[:, VOFF[par] + (r * 8 + jlo) * 144:
                             VOFF[par] + (r * 8 + 8) * 144].rearrange(
                        "p (j c) -> p j c", c=144)[:, :, 0:128]
                    ring_kv.inc(SY.dma_start(vdst, vsrc), seq)
                return seq

            def build_plane(qg, kv):
                seq = KEYSEQ.index((qg, kv))
                par = seq % 2
                ncols = PLANE_COUNTS[(qg, kv)]
                off = PLANE_OFF[(qg, kv)]
                if seq >= 2:
                    key = KEYSEQ[seq - 2]
                    s_ex.wait(SY, ex_cnt_end[key])
                prb = prhs_bufs[seq % 2]
                ring_pr.inc(SY.dma_start(prb[0:2, 0:ncols], prhs_in[:, off:off + ncols]), seq)
                ring_pr.wait(T, seq)
                pl = A[:, PLOFF[par]:PLOFF[par] + 1664].bitcast(F32)
                nch = (ncols + 511) // 512
                for chk in range(nch):
                    c0 = chk * 512
                    n = min(512, ncols - c0)
                    s_pl.wait(T, s_pl.n)
                    s_pe.inc(T.matmul(ps_pl[:, 0:n], k2[0:2, 0:128], prb[0:2, c0:c0 + n],
                                      start=True, stop=True))
                    s_pe.wait(V, s_pe.n)
                    s_pl.inc(V.tensor_scalar(pl[:, c0:c0 + n], ps_pl[:, 0:n],
                                             shift_b[:, 0:1], None, ALU.subtract))
                return s_pl.n

            s_v.wait(T, s_v.n)
            s_a.wait(T, s_a.n)
            s_pe.inc(T.matmul(ps_pl[:, 0:1], onesf[0:1, 0:128], gcol[0:1, 7:8],
                              start=True, stop=True))
            s_pe.wait(V, s_pe.n)
            s_pl.inc(V.tensor_copy(shift_b[:, 0:1], ps_pl[:, 0:1]))

            prhs_bufs = [B[:, 0:4992].bitcast(F32), B[:, 4992:9984].bitcast(F32)]
            pb_cnt = [0]
            at_cnt = [0]
            ex_tile_cnt = {}
            pfree = {}
            pending = {}

            pending[(0, 0)] = (load_kv(0, 0), build_plane(0, 0))
            pending[(0, 1)] = (load_kv(0, 1), build_plane(0, 1))

            for qg in range(2):
                s_ql.wait(SY)
                s_sc.wait(SY, s_sc.n)
                s_ld.inc(SY.dma_start(qslab, q_dram[:, qg * 16384:(qg + 1) * 16384]), 16)
                qload = s_ld.n
                tmax = T_SLOT[4 * qg]
                for kv in range(HKV):
                    klcnt, plcnt = pending.pop((qg, kv))
                    pcol = [0]
                    par = kv % 2
                    kb = C[:, par * 8192:(par + 1) * 8192]
                    pl = A[:, PLOFF[par]:PLOFF[par] + 1664].bitcast(F32)
                    for hg in range(4):
                        h = 4 * kv + hg
                        nf_before = s_nf.n
                        s_nf.wait(T, nf_before)
                        for bk in range(2):
                            T.matmul(ps_av[bk][:, 0:265], ident[0:1, 0:128],
                                     zrow[0:1, 0:265], start=True, stop=False)
                        kept = [t_ for t_ in range(tmax) if active_slots(h, qg, t_)]
                        PIPE = 3
                        sc_cnt = {}

                        def emit_score(idx):
                            t_ = kept[idx]
                            act_ = active_slots(h, qg, t_)
                            jr0_ = act_[0] - 4 * qg
                            nc_ = len(act_) * 128
                            ab = pb_cnt[0] + idx
                            ring_kv.wait(T, klcnt)
                            s_ld.wait(T, qload)
                            if ab >= 5:
                                s_ex.wait(T, ex_tile_cnt[ab - 5])
                            mm_ = T.matmul(ps_sc[ab % 5][:, jr0_ * 128: jr0_ * 128 + nc_],
                                           kb[:, KCOL[t_]:KCOL[t_] + 128],
                                           qslab[:, h * 512 + jr0_ * 128: h * 512 + jr0_ * 128 + nc_],
                                           start=True, stop=True)
                            s_sc.inc(mm_)
                            sc_cnt[ab] = s_sc.n

                        for i0 in range(min(PIPE, len(kept))):
                            emit_score(i0)
                        for idx, t in enumerate(kept):
                            act = active_slots(h, qg, t)
                            ab = pb_cnt[0] + idx
                            scpar = ab % 5
                            pbuf = A[:, POFF + (ab % 6) * 512: POFF + (ab % 6 + 1) * 512]
                            s_sc.wait(SC, sc_cnt[ab])
                            s_pl.wait(SC, plcnt)
                            if ab >= 6:
                                knd, cnt_ = pfree[ab - 6]
                                (s_av if knd == 'av' else s_pb).wait(SC, cnt_)
                            w = CHUNK_W[h]
                            nch = 128 // w
                            last_act = None
                            for j in act:
                                jr = j - 4 * qg
                                for ci in range(nch):
                                    last_act = SC.activation(
                                        pbuf[:, jr * 128 + ci * w: jr * 128 + (ci + 1) * w],
                                        ps_sc[scpar][:, jr * 128 + ci * w: jr * 128 + (ci + 1) * w],
                                        AF.Exp, bias=pl[:, pcol[0]:pcol[0] + 1], scale=ISQD)
                                    pcol[0] += 1
                            s_ex.inc(last_act)
                            ex_tile_cnt[ab] = s_ex.n
                            masked = [j for j in act if t >= T_SLOT[j] - 8]
                            for j in masked:
                                jr = j - 4 * qg
                                s_ex.wait(GP, s_ex.n)
                                s_mk.inc(GP.tensor_tensor(
                                    pbuf[:, jr * 128:(jr + 1) * 128],
                                    pbuf[:, jr * 128:(jr + 1) * 128],
                                    masksb[:, (j * 8 + (t - (T_SLOT[j] - 8))) * 128:
                                           (j * 8 + (t - (T_SLOT[j] - 8)) + 1) * 128],
                                    ALU.mult))
                            s_ex.wait(T, s_ex.n)
                            if masked:
                                s_mk.wait(T, s_mk.n)
                            bank_close = False
                            lastav = None
                            for j in act:
                                jr = j - 4 * qg
                                bk, half = jr // 2, jr % 2
                                closes = (jr in (0, 2) and t == T_SLOT[j] - 1)
                                pos_t = OWNER[t] * 8 + SLOT[t]
                                av = T.matmul(ps_av[bk][:, half * 136: half * 136 + 129],
                                              pbuf[:, jr * 128:(jr + 1) * 128],
                                              A[:, VOFF[par] + pos_t * 144: VOFF[par] + pos_t * 144 + 129],
                                              start=False, stop=closes)
                                if closes:
                                    s_av.inc(av)
                                    bank_close = True
                                lastav = av
                            if bank_close:
                                pfree[ab] = ('av', s_av.n)
                            else:
                                s_pb.inc(lastav)
                                pfree[ab] = ('pb', s_pb.n)
                            if idx + PIPE < len(kept):
                                emit_score(idx + PIPE)
                            for j in act:
                                jr = j - 4 * qg
                                if not (jr in (0, 2) and t == T_SLOT[j] - 1):
                                    continue
                                bk = jr // 2
                                s_av.wait(V, s_av.n)
                                for half in range(2):
                                    jn = 4 * qg + 2 * bk + half
                                    atb = A[:, ATOFF + (at_cnt[0] % 4) * 128: ATOFF + (at_cnt[0] % 4 + 1) * 128]
                                    ring_at.wait(V, at_cnt[0])
                                    dc = dencol[:, (at_cnt[0] % 4) * 2:(at_cnt[0] % 4) * 2 + 2]
                                    s_v.inc(V.tensor_scalar(dc[:, 0:1],
                                                            ps_av[bk][:, half * 136 + 128: half * 136 + 129],
                                                            DEN_GUARD, None, ALU.add))
                                    s_v.wait(V)
                                    s_v.inc(V.reciprocal(dc[:, 1:2], dc[:, 0:1]))
                                    s_v.wait(V)
                                    nrm = V.tensor_scalar(atb, ps_av[bk][:, half * 136: half * 136 + 128],
                                                          dc[:, 1:2], None, ALU.mult)
                                    s_nf.inc(nrm)
                                    s_nf.wait(SC, s_nf.n)
                                    ring_at.inc(SC.dma_start(
                                        attn_dram[jn * 128:(jn + 1) * 128, h * 128:(h + 1) * 128], atb), at_cnt[0])
                                    at_cnt[0] += 1
                        pb_cnt[0] += len(kept)
                    score_cnt_end[(qg, kv)] = s_sc.n
                    av_cnt_end[(qg, kv)] = s_av.n
                    ex_cnt_end[(qg, kv)] = s_ex.n
                    nseq = KEYSEQ.index((qg, kv)) + 2
                    if nseq < len(KEYSEQ):
                        nk = KEYSEQ[nseq]
                        pending[nk] = (load_kv(*nk), build_plane(*nk))

        if phase_limit == "P4":
            finish_debug()
            return nc
        attn_fm = A[:, 0:32768]
        xres = Q[:, 0:4096].bitcast(F32)
        outt = Q[:, 4096:8192].bitcast(F32)
        s_ff.wait(SC, s_ff.n)
        with contextlib.ExitStack() as ps_st:
            pw = [ps_st.enter_context(nc.psum_tensor(f"pw{i}", [128, 512], F32)) for i in range(2)]
            ftp = [ps_st.enter_context(nc.psum_tensor(f"ftp{i}", [128, 128], BF16))
                   for i in range(4)]
            s_v.wait(T, s_v.n)
            s_a.wait(T, s_a.n)
            wobs = [C[:, 0:16384], B[:, 8192:24576]]

            def wo_load(g):
                wob = wobs[g % 2]
                for hc in range(32):
                    ring_wt.inc(SY.dma_start(
                        wob[:, hc * 512:(hc + 1) * 512],
                        wtr_all_o[g * DM + hc * 128: g * DM + (hc + 1) * 128, 0:512]), g)
                return wob

            s_ws.wait(SY, 3)
            s_sc.wait(SY, s_sc.n)
            wob_pre = {0: wo_load(0)}

            ring_at.wait_all(SY)
            s_pe.wait(SY, s_pe.n)
            s_mk.wait(SY, s_mk.n)
            wob_pre[1] = wo_load(1)
            arow = [B[:, 0:4096], B[:, 4096:8192]]
            evf = {}
            trf = {}
            for j in range(NT):
                if j >= 2:
                    s_pe.wait(SY, trf[j - 2])
                ring_kv.inc(SY.dma_start(arow[j % 2], attn_dram[j * 128:(j + 1) * 128, :]),
                            16 + j)
                ring_kv.wait(T, 16 + j)
                for hc in range(32):
                    u = j * 32 + hc
                    if u >= 4:
                        s_fm.wait(T, evf[u - 4])
                    s_pe.inc(T.transpose(ftp[u % 4][:, 0:128],
                                         arow[j % 2][:, hc * 128:(hc + 1) * 128], ident))
                    s_pe.wait(V, s_pe.n)
                    if u == 0:
                        s_av.wait(V, s_av.n)
                        s_mk.wait(V, s_mk.n)
                        ring_at.wait_all(V)
                    s_fm.inc(V.tensor_copy(
                        attn_fm[:, hc * 1024 + j * 128: hc * 1024 + (j + 1) * 128],
                        ftp[u % 4][:, 0:128]))
                    evf[u] = s_fm.n
                trf[j] = s_pe.n
            fm_all = s_fm.n

            wo_done = {}
            p5_done = {}
            for g in range(8):
                wob = wob_pre.pop(g, None)
                if wob is None:
                    s_pe.wait(SY, wo_done[g - 2])
                    wob = wo_load(g)
                ring_wt.wait(T, g)
                s_fm.wait(T, fm_all)
                for tt in range(NT):
                    un = (g * 8 + tt)
                    ps = pw[un % 2]
                    if un >= 2:
                        s_v.wait(T, p5_done[un - 2])
                    ring_y.wait(SC, un)
                    ring_x.inc(SC.dma_start(xres[:, (un % 2) * 1024: (un % 2) * 1024 + 512],
                                            x_in[tt * 128:(tt + 1) * 128, g * 512:(g + 1) * 512]), un)
                    ring_x.inc(SC.dma_start(xres[:, (un % 2) * 1024 + 512: (un % 2) * 1024 + 1024],
                                            ffn_dram[tt * 128:(tt + 1) * 128, g * 512:(g + 1) * 512]), un)
                    mm = None
                    for hc in range(32):
                        mm = T.matmul(ps[:, 0:512],
                                      attn_fm[:, hc * 1024 + tt * 128: hc * 1024 + (tt + 1) * 128],
                                      wob[:, hc * 512:(hc + 1) * 512],
                                      start=(hc == 0), stop=(hc == 31))
                    s_pe.inc(mm)
                    s_pe.wait(V, s_pe.n)
                    ring_x.wait(V, un)
                    ot = outt[:, (un % 2) * 1024:(un % 2) * 1024 + 512]
                    s_v.inc(V.tensor_add(ot, ps[:, 0:512], xres[:, (un % 2) * 1024: (un % 2) * 1024 + 512]))
                    s_v.wait(V)
                    s_v.inc(V.tensor_add(ot, ot, xres[:, (un % 2) * 1024 + 512: (un % 2) * 1024 + 1024]))
                    p5_done[un] = s_v.n
                    s_v.wait(GP, s_v.n)
                    ring_y.inc(GP.dma_start(y_out[tt * 128:(tt + 1) * 128, g * 512:(g + 1) * 512], ot), un)
                wo_done[g] = s_pe.n

            for tp, src in tap_outs:
                s_y.wait(GP, s_y.n)
                s_y.inc(GP.dma_start(tp, src), 16)
            s_y.wait(GP, s_y.n)

    return nc



_CONST = {}

def _consts():
    if not _CONST:
        _CONST["ident"] = np.eye(128, dtype=np.float32).astype(ml_dtypes.bfloat16)
        _CONST["ones"] = np.ones((1, 128), np.float32)
        _CONST["k2"] = np.stack([np.arange(128, dtype=np.float32),
                                 np.ones(128, np.float32)])
        _CONST["planes"] = [build_plane_rhs(c) for c in range(NC)]
        _CONST["masks"] = [build_masks(c) for c in range(NC)]
    return _CONST


def run(inputs, taps=(), phase_limit=None):
    cst = _consts()
    x = np.asarray(inputs["L0_attn_x"])
    xr = np.ascontiguousarray(x.reshape(NB, 128, DM))
    wq_np = np.asarray(inputs["L0_attn_Wq"])
    wk_np = np.asarray(inputs["L0_attn_Wk"])
    wv_np = np.asarray(inputs["L0_attn_Wv"])
    wo_np = np.asarray(inputs["L0_attn_Wo"])
    in_maps = []
    for c in range(NC):
        blocks = blocks_of_core(c)
        x_own = np.ascontiguousarray(xr[blocks].reshape(TOK, DM))
        w_slice = np.ascontiguousarray(np.concatenate([
            wq_np[512 * c:512 * (c + 1)], wk_np[128 * c:128 * (c + 1)],
            wv_np[128 * c:128 * (c + 1)], wo_np[512 * c:512 * (c + 1)]], axis=0))
        m = {
            "x_own": x_own,
            "w_slice": w_slice,
            "L0_attn_Wq": np.ascontiguousarray(inputs["L0_attn_Wq"]),
            "L0_attn_Wk": np.ascontiguousarray(inputs["L0_attn_Wk"]),
            "L0_attn_Wv": np.ascontiguousarray(inputs["L0_attn_Wv"]),
            "L0_attn_Wo": np.ascontiguousarray(inputs["L0_attn_Wo"]),
            "L0_attn_q_norm_w": np.ascontiguousarray(np.asarray(inputs["L0_attn_q_norm_w"]).reshape(1, HD)),
            "L0_attn_k_norm_w": np.ascontiguousarray(np.asarray(inputs["L0_attn_k_norm_w"]).reshape(1, HD)),
            "L0_ffn_W_res": np.ascontiguousarray(inputs["L0_ffn_W_res"]),
            "L0_ffn_W_pre": np.ascontiguousarray(inputs["L0_ffn_W_pre"]),
            "L0_ffn_W_post": np.ascontiguousarray(inputs["L0_ffn_W_post"]),
            "L0_ffn_H_res_base": np.ascontiguousarray(np.asarray(inputs["L0_ffn_H_res_base"]).reshape(1, 16)),
            "L0_ffn_H_pre_base": np.ascontiguousarray(np.asarray(inputs["L0_ffn_H_pre_base"]).reshape(1, 4)),
            "L0_ffn_H_post_base": np.ascontiguousarray(np.asarray(inputs["L0_ffn_H_post_base"]).reshape(1, 4)),
            "L0_ffn_gate": np.ascontiguousarray(np.asarray(inputs["L0_ffn_gate"]).reshape(1, 1)),
            "L0_ffn_W_branch": np.ascontiguousarray(inputs["L0_ffn_W_branch"]),
            "c_ident": cst["ident"],
            "c_ones_f32": cst["ones"],
            "c_k2": cst["k2"],
            "c_plane_rhs": cst["planes"][c],
            "c_masks": cst["masks"][c],
        }
        in_maps.append(m)

    nc_prog = build_program(taps=taps, phase_limit=phase_limit)
    res = run_bass_kernel_spmd(nc_prog, in_maps, core_ids=list(range(NC)))

    y = np.zeros((NB, 128, DM), np.float32)
    for c in range(NC):
        blocks = blocks_of_core(c)
        yc = np.asarray(res.results[c]["y_own"]).reshape(NT, 128, DM)
        for j, b in enumerate(blocks):
            y[b] = yc[j]
    out = y.reshape(1, S, DM)
    if taps:
        return out, res
    return out


def kernel(**inputs):
    return run(inputs)


# version 10
# speedup vs baseline: 2.4728x; 1.0546x over previous
import math
import contextlib
import numpy as np
import ml_dtypes

import concourse.bass as bass
import concourse.mybir as mybir
from concourse.bass_utils import run_bass_kernel_spmd

F32 = mybir.dt.float32
BF16 = mybir.dt.bfloat16
AF = mybir.ActivationFunctionType
ALU = mybir.AluOpType
AX = mybir.AxisListType.X

NC = 8
S = 8192
DM = 4096
H = 32
HKV = 8
HD = 128
NB = 64
TOK = 1024
NT = 8

LN_EPS = 1e-5
QK_EPS = 1e-6
RMS_EPS = 1.1920929e-07
TAU = 0.1
SINK_ITERS = 50
NEG = -1e30
DEN_GUARD = 1e-30
ISQD = 1.0 / math.sqrt(HD)

SLOPES = [2.0 ** (-(h + 1) / 4.0) for h in range(H)]

def _chunk_w(h):
    for w in (128, 64, 32):
        if SLOPES[h] * (w - 1) <= 76.0:
            return w
    return 32
CHUNK_W = [_chunk_w(h) for h in range(H)]

T_SLOT = [64 - 8 * j for j in range(NT)]
WINDOW_TILES = [min(64, int(math.ceil(90.0 / SLOPES[h] / 128.0)) + 1) for h in range(H)]

def blocks_of_core(c):
    return [63 - c, 48 + c, 47 - c, 32 + c, 31 - c, 16 + c, 15 - c, c]

OWNER = [0] * NB
SLOT = [0] * NB
for _c in range(NC):
    for _j, _b in enumerate(blocks_of_core(_c)):
        OWNER[_b] = _c
        SLOT[_b] = _j
KCOL = [OWNER[g] * TOK + SLOT[g] * 128 for g in range(NB)]

def kept_lo(h, j):
    if WINDOW_TILES is None:
        return 0
    return max(0, T_SLOT[j] - 8 - WINDOW_TILES[h])

def active_slots(h, qg, t):
    out = []
    for j in range(4 * qg, 4 * qg + 4):
        if kept_lo(h, j) <= t < T_SLOT[j]:
            out.append(j)
    return out


def att_exp_cols():
    for qg in range(2):
        tmax = T_SLOT[4 * qg]
        for kv in range(HKV):
            for hg in range(4):
                h = 4 * kv + hg
                for t in range(tmax):
                    for j in active_slots(h, qg, t):
                        for ci in range(128 // CHUNK_W[h]):
                            yield (kv, h, qg, j, t, ci)

PLANE_COUNTS = {}
for _rec in att_exp_cols():
    _k = (_rec[2], _rec[0])
    PLANE_COUNTS[_k] = PLANE_COUNTS.get(_k, 0) + 1
PLANE_OFF = {}
_off = 0
for _qg in range(2):
    for _kv in range(HKV):
        PLANE_OFF[(_qg, _kv)] = _off
        _off += PLANE_COUNTS[(_qg, _kv)]
PLANE_TOTAL = _off
PLANE_MAX = max(PLANE_COUNTS.values())



def build_plane_rhs(c):
    blocks = blocks_of_core(c)
    r0 = np.zeros(PLANE_TOTAL, np.float32)
    r1 = np.zeros(PLANE_TOTAL, np.float32)
    i = 0
    for (kv, h, qg, j, t, ci) in att_exp_cols():
        b = blocks[j]
        w = CHUNK_W[h]
        if t > b:
            r0[i] = 0.0
            r1[i] = NEG
        else:
            cq = b * 128 + ci * w + w // 2
            r0[i] = SLOPES[h]
            r1[i] = SLOPES[h] * (128.0 * t - cq)
        i += 1
    return np.stack([r0, r1]).astype(np.float32)


def build_masks(c):
    blocks = blocks_of_core(c)
    tri = (np.arange(128)[:, None] <= np.arange(128)[None, :]).astype(np.float32)
    out = np.zeros((128, NT * 8 * 128), np.float32)
    for j in range(NT):
        b = blocks[j]
        for r8 in range(8):
            t = T_SLOT[j] - 8 + r8
            col = (j * 8 + r8) * 128
            if t < b:
                out[:, col:col + 128] = 1.0
            elif t == b:
                out[:, col:col + 128] = tri
    return out.astype(ml_dtypes.bfloat16)



class Sem:
    def __init__(self, st, nc, name):
        self.h = st.enter_context(nc.semaphore(name))
        self.n = 0
        self._wcache = {}

    def inc(self, instr, k=1):
        instr.then_inc(self.h, k)
        self.n += k

    def wait(self, eng, val=None):
        v = self.n if val is None else val
        key = id(eng)
        if self._wcache.get(key, -1) >= v:
            return
        self._wcache[key] = v
        eng.wait_ge(self.h, v)


class Ring:
    def __init__(self, st, nc, name, n):
        self.s = [Sem(st, nc, f"{name}{i}") for i in range(n)]
        self.n = n

    def inc(self, instr, slot, k=16):
        self.s[slot % self.n].inc(instr, k)

    def wait(self, eng, slot):
        sm = self.s[slot % self.n]
        sm.wait(eng)

    def wait_all(self, eng):
        for sm in self.s:
            sm.wait(eng)


def W(eng, sem, v):
    sem.wait(eng, v)


def build_program(taps=(), phase_limit=None):
    nc = bass.Bass("TRN2", target_bir_lowering=False, debug=False)

    def I(name, shape, dt=F32):
        return nc.dram_tensor(name, shape, dt, kind="ExternalInput").ap()

    x_in = I("x_own", [TOK, DM])
    wsl_in = I("w_slice", [1280, DM])
    wq_in = I("L0_attn_Wq", [DM, DM])
    wk_in = I("L0_attn_Wk", [HKV * HD, DM])
    wv_in = I("L0_attn_Wv", [HKV * HD, DM])
    wo_in = I("L0_attn_Wo", [DM, DM])
    qnw_in = I("L0_attn_q_norm_w", [1, HD])
    knw_in = I("L0_attn_k_norm_w", [1, HD])
    wres_in = I("L0_ffn_W_res", [16, DM])
    wpre_in = I("L0_ffn_W_pre", [4, DM])
    wpost_in = I("L0_ffn_W_post", [4, DM])
    hresb_in = I("L0_ffn_H_res_base", [1, 16])
    hpreb_in = I("L0_ffn_H_pre_base", [1, 4])
    hpostb_in = I("L0_ffn_H_post_base", [1, 4])
    gate_in = I("L0_ffn_gate", [1, 1])
    wb_in = I("L0_ffn_W_branch", [1024, 1024])
    ident_in = I("c_ident", [128, 128], BF16)
    onesf_in = I("c_ones_f32", [1, 128])
    k2_in = I("c_k2", [2, 128])
    prhs_in = I("c_plane_rhs", [2, PLANE_TOTAL])
    mask_in = I("c_masks", [128, NT * 8 * 128], BF16)

    y_out = nc.dram_tensor("y_own", [TOK, DM], F32, kind="ExternalOutput").ap()

    wtr_kv = nc.dram_tensor("wtr_kv", [DM, 256], BF16).ap()
    wtr_all_kv = nc.dram_tensor("wtr_all_kv", [NC * DM, 256], BF16).ap()
    wtr_q = nc.dram_tensor("wtr_q", [DM, 512], BF16).ap()
    wtr_all_q = nc.dram_tensor("wtr_all_q", [NC * DM, 512], BF16).ap()
    wtr_o = nc.dram_tensor("wtr_o", [DM, 512], BF16).ap()
    wtr_all_o = nc.dram_tensor("wtr_all_o", [NC * DM, 512], BF16).ap()
    xn_dram = nc.dram_tensor("xn_dram", [TOK, DM], BF16).ap()
    q_dram = nc.dram_tensor("q_dram", [128, 2 * H * 512], BF16).ap()
    attn_dram = nc.dram_tensor("attn_dram", [TOK, DM], BF16).ap()
    ffn_dram = nc.dram_tensor("ffn_dram", [TOK, DM], F32).ap()
    kv_send = nc.dram_tensor("kv_send", [2 * HKV * HD, TOK], BF16).ap()
    kv_all = nc.dram_tensor("kv_all", [NC * 2 * HKV * HD, TOK], BF16).ap()

    tap_map = {
        "xn": (xn_dram, [TOK, DM], BF16),
        "q": (q_dram, [128, 2 * H * 512], BF16),
        "attn": (attn_dram, [TOK, DM], BF16),
        "ffn": (ffn_dram, [TOK, DM], F32),
        "kvall": (kv_all, [NC * 2 * HKV * HD, TOK], BF16),
    }
    tap_outs = []
    tap_sb = []
    for tname in taps:
        if tname == "hco":
            tap_sb.append(("hco", nc.dram_tensor("tap_hco", [128, 256], F32, kind="ExternalOutput").ap()))
            continue
        if tname == "sink":
            tap_sb.append(("sink", nc.dram_tensor("tap_sink", [128, 1152], F32, kind="ExternalOutput").ap()))
            continue
        tsrc, shp, dt = tap_map[tname]
        tap_outs.append((nc.dram_tensor("tap_" + tname, shp, dt, kind="ExternalOutput").ap(), tsrc))

    st = contextlib.ExitStack()
    with st:
        A = st.enter_context(nc.sbuf_tensor("arenaA", [128, 32768], BF16))
        B = st.enter_context(nc.sbuf_tensor("arenaB", [128, 24576], BF16))
        Q = st.enter_context(nc.sbuf_tensor("arenaQ", [128, 16384], BF16))
        C = st.enter_context(nc.sbuf_tensor("arenaC", [128, 16384], BF16))
        M = st.enter_context(nc.sbuf_tensor("arenaM", [128, 10240], BF16))

        names = ["s_ct", "s_x", "s_v", "s_a", "s_xs", "s_fm", "s_wck", "s_wcq",
                 "s_wco", "s_wb", "s_pe", "s_pp", "s_kvs", "s_cc", "s_kl",
                 "s_ql", "s_pl", "s_sc", "s_scf", "s_ex", "s_mk", "s_av",
                 "s_nf", "s_pb", "s_at", "s_ats", "s_ff", "s_y", "s_ld", "s_cm", "s_ws"]
        SM = {n: Sem(st, nc, n) for n in names}
        (s_ct, s_x, s_v, s_a, s_xs, s_fm, s_wck, s_wcq, s_wco, s_wb, s_pe,
         s_pp, s_kvs, s_cc, s_kl, s_ql, s_pl, s_sc, s_scf, s_ex, s_mk, s_av,
         s_nf, s_pb, s_at, s_ats, s_ff, s_y, s_ld, s_cm, s_ws) = (SM[n] for n in names)

        ring_x = Ring(st, nc, "rx", 2)
        ring_wt = Ring(st, nc, "rwt", 2)
        ring_kv = Ring(st, nc, "rkv", 2)
        ring_pr = Ring(st, nc, "rpr", 2)
        ring_at = Ring(st, nc, "rat", 4)
        ring_y = Ring(st, nc, "ry", 2)
        ring_ws = Ring(st, nc, "rws", 4)
        ring_wc = Ring(st, nc, "rwc", 2)
        ring_q = Ring(st, nc, "rq", 2)

        T, V, SC, GP, SY = nc.tensor, nc.vector, nc.scalar, nc.gpsimd, nc.sync

        def finish_debug():
            for sm in SM.values():
                GP.wait_ge(sm.h, sm.n)
            for rg in (ring_x, ring_wt, ring_kv, ring_pr, ring_at, ring_y, ring_ws,
                       ring_wc, ring_q):
                rg.wait_all(GP)
            for tp, tsrc in tap_outs:
                s_y.inc(GP.dma_start(tp, tsrc), 16)
            for kind, tp in tap_sb:
                if kind == "hco":
                    s_y.inc(GP.dma_start(tp, hcoef[:, 0:256]), 16)
                else:
                    s_y.inc(GP.dma_start(tp, sink[:, 0:1152]), 16)
            s_y.wait(GP, s_y.n)

        _mo = [0]
        def mslice(n_bf16, dt=BF16):
            o = _mo[0]
            _mo[0] += (n_bf16 + 15) & ~15
            assert _mo[0] <= 10240, f"M overflow {_mo[0]}"
            ap = M[:, o:o + n_bf16]
            return ap.bitcast(dt) if dt != BF16 else ap

        ident = mslice(128)
        onesf = mslice(256, F32)
        k2 = mslice(256, F32)
        onescol = mslice(16)
        qnw = mslice(16, F32)
        nrow = mslice(512, F32)
        shift_b = mslice(16, F32)
        stats = mslice(160, F32)
        hcoef = mslice(512, F32)
        gcol = mslice(16, F32)
        waT = mslice(768)
        sink = mslice(2304, F32)
        dencol = mslice(32, F32)
        zrow = mslice(288)

        ST_BN, ST_AG, ST_R2, ST_SCR = 0, 48, 50, 58
        HC_RES, HC_PRE, HC_POST, HC_BASE = 0, 128, 160, 192

        CONST_VALS = [LN_EPS, RMS_EPS, QK_EPS, -0.5, -1.0, 1.0 / HD, ISQD,
                      1.0, 0.0, DEN_GUARD]
        cvals = mslice(16 * ((2 * len(CONST_VALS) + 15) // 16), F32)
        for ci_, val_ in enumerate(CONST_VALS):
            nc.const_aps.register(F32, val_, cvals[:, ci_:ci_ + 1])

        hco = hcoef
        assert PLANE_MAX <= 2496, PLANE_MAX

        for ci_, val_ in enumerate(CONST_VALS):
            GP.memset(cvals[:, ci_:ci_ + 1], float(val_))
        GP.memset(onescol[:, 0:1], 1.0)
        GP.memset(sink[:, 0:1152], 0.0)
        GP.memset(hco[:, 0:256], 0.0)
        GP.memset(zrow[:, 0:288], 0.0)
        s_cm.inc(GP.memset(onescol[:, 1:2], 0.0), 1)
        s_cm.wait(SY, 1)
        s_ct.inc(SY.dma_start(ident, ident_in), 16)
        s_ct.inc(SY.dma_start(onesf[0:1, 0:128], onesf_in), 16)
        s_ct.inc(SY.dma_start(k2[0:2, 0:128], k2_in), 16)
        s_ct.inc(SY.dma_start(qnw[:, 0:1], qnw_in.rearrange("o (f u) -> (o f) u", u=1)), 16)
        s_ct.inc(SY.dma_start(qnw[:, 1:2], knw_in.rearrange("o (f u) -> (o f) u", u=1)), 16)
        s_ct.inc(SY.dma_start(nrow[0:1, 0:128], qnw_in), 16)
        s_ct.inc(SY.dma_start(nrow[0:1, 128:256], knw_in), 16)
        s_ct.inc(SY.dma_start(hco[0:1, HC_BASE:HC_BASE + 16], hresb_in), 16)
        s_ct.inc(SY.dma_start(hco[0:1, HC_BASE + 16:HC_BASE + 20], hpreb_in), 16)
        s_ct.inc(SY.dma_start(hco[0:1, HC_BASE + 20:HC_BASE + 24], hpostb_in), 16)
        s_ct.inc(SY.dma_start(gcol[0:1, 3:4], gate_in), 16)

        s_ct.wait(V)
        s_v.inc(V.reduce_max(gcol[0:1, 4:5], nrow[0:1, 0:128], axis=AX,
                             apply_absolute_value=True))
        s_v.inc(V.reduce_max(gcol[0:1, 5:6], nrow[0:1, 128:256], axis=AX,
                             apply_absolute_value=True))
        s_v.wait(V)
        s_v.inc(V.tensor_mul(gcol[0:1, 6:7], gcol[0:1, 4:5], gcol[0:1, 5:6]))
        s_v.wait(V)
        s_v.inc(V.tensor_scalar_mul(gcol[0:1, 7:8], gcol[0:1, 6:7],
                                    float(math.sqrt(128.0))))

        RT_ORDER = [4, 5, 0, 1, 2, 3, 6, 7, 8, 9]

        def wdst(rt):
            if rt == 4:
                return wtr_kv, 0
            if rt == 5:
                return wtr_kv, 128
            if rt < 4:
                return wtr_q, rt * 128
            return wtr_o, (rt - 6) * 128

        xn_fm = A[:, 0:32768]
        with contextlib.ExitStack() as pws:
            wtp = [pws.enter_context(nc.psum_tensor(f"wtp{i}", [128, 128], BF16))
                   for i in range(4)]
            xtp = [pws.enter_context(nc.psum_tensor(f"xtp{i}", [128, 128], BF16))
                   for i in range(4)]
            wrowb = [C[:, 0:4096], C[:, 4096:8192]]
            wslab = [B[:, 16384:20480], B[:, 20480:24576]]

            def p0w_gen():
                ev_w = {}
                tr_end = {}
                s_ct.wait(T)
                for i, rt in enumerate(RT_ORDER):
                    if i >= 2:
                        s_pe.wait(GP, tr_end[i - 2])
                    ring_wc.inc(GP.dma_start(wrowb[i % 2],
                                             wsl_in[rt * 128:(rt + 1) * 128, :]), i)
                    ring_wc.wait(T, i)
                    for fc in range(32):
                        u = i * 32 + fc
                        if u >= 4:
                            s_v.wait(T, ev_w[u - 4])
                        s_pe.inc(T.transpose(wtp[u % 4][:, 0:128],
                                             wrowb[i % 2][:, fc * 128:(fc + 1) * 128],
                                             ident))
                        s_pe.wait(V, s_pe.n)
                        if fc == 0 and i >= 2:
                            ring_ws.wait(V, i - 2)
                        s_v.inc(V.tensor_copy(wslab[i % 2][:, fc * 128:(fc + 1) * 128],
                                              wtp[u % 4][:, 0:128]))
                        ev_w[u] = s_v.n
                    tr_end[i] = s_pe.n
                    s_v.wait(SY, s_v.n)
                    dstt, c0 = wdst(rt)
                    ring_ws.inc(SY.dma_start(
                        dstt[:, c0:c0 + 128].rearrange("(f p) c -> p f c", p=128),
                        wslab[i % 2].rearrange("p (f c) -> p f c", c=128)), i)
                    if i == 1:
                        ring_ws.wait(GP, 0)
                        ring_ws.wait(GP, 1)
                        s_ws.inc(GP.collective_compute(
                            "AllGather", ALU.bypass,
                            replica_groups=[list(range(NC))],
                            ins=[wtr_kv.opt()], outs=[wtr_all_kv.opt()],
                        ))
                    if i == 5:
                        ring_ws.wait_all(GP)
                        s_ws.inc(GP.collective_compute(
                            "AllGather", ALU.bypass,
                            replica_groups=[list(range(NC))],
                            ins=[wtr_q.opt()], outs=[wtr_all_q.opt()],
                        ))
                    yield

            pgen = p0w_gen()

            s_ct.wait(SC)
            s_cm.wait(SC)
            s_cm.wait(V)
            s_ct.wait(T)
            s_cm.wait(T)
            s_ct.wait(GP)
            xbuf = [B[:, 0:8192].bitcast(F32), B[:, 8192:16384].bitcast(F32)]
            xnb = [Q[:, 0:4096], Q[:, 4096:8192]]
            dump = sink[:, 0:1024]
            ev_xn = {}
            tr_xn = {}
            for tt in range(NT):
                par = tt % 2
                xb = xbuf[par]
                if tt >= 2:
                    s_a.wait(SY, a_done_xn[tt - 2])
                ring_x.inc(SY.dma_start(xb, x_in[tt * 128:(tt + 1) * 128, :]), tt)
                ring_x.wait(V, tt)
                if tt >= 1:
                    s_a.wait(V)
                for ch in range(8):
                    s_v.inc(V.bn_stats(stats[:, ST_BN + ch * 6:ST_BN + (ch + 1) * 6],
                                       xb[:, ch * 512:(ch + 1) * 512]))
                s_v.wait(V)
                s_v.inc(V.bn_aggr(stats[:, ST_AG:ST_AG + 2], stats[:, ST_BN:ST_BN + 48]))
                s_v.wait(V)
                cm = stats[:, ST_AG + 0:ST_AG + 1]
                cvv = stats[:, ST_AG + 1:ST_AG + 2]
                crstd = stats[:, ST_SCR + 2:ST_SCR + 3]
                cbias = stats[:, ST_SCR + 3:ST_SCR + 4]
                s_v.wait(SC)
                s_a.inc(SC.activation(stats[:, ST_SCR + 4:ST_SCR + 5], cvv, AF.Ln, bias=LN_EPS))
                s_a.wait(SC)
                s_a.inc(SC.activation(crstd, stats[:, ST_SCR + 4:ST_SCR + 5], AF.Exp, scale=-0.5))
                s_a.wait(V)
                s_v.inc(V.scalar_tensor_tensor(cbias, cm, -1.0, crstd, ALU.mult, ALU.mult))
                s_v.inc(V.scalar_tensor_tensor(stats[:, ST_SCR + 5:ST_SCR + 6], crstd, 1.0, crstd,
                                               ALU.mult, ALU.mult))
                s_v.wait(V)
                s_v.inc(V.tensor_mul(stats[:, ST_SCR + 5:ST_SCR + 6], stats[:, ST_SCR + 5:ST_SCR + 6], cvv))
                s_v.wait(SC)
                s_a.inc(SC.activation(stats[:, ST_SCR + 6:ST_SCR + 7], stats[:, ST_SCR + 5:ST_SCR + 6],
                                      AF.Ln, bias=RMS_EPS))
                s_a.wait(SC)
                s_a.inc(SC.activation(stats[:, ST_R2 + tt:ST_R2 + tt + 1], stats[:, ST_SCR + 6:ST_SCR + 7],
                                      AF.Exp, scale=-0.5))
                s_v.wait(SC)
                s_xs.wait(SC, s_xs.n)
                if tt >= 2:
                    s_pe.wait(SC, tr_xn[tt - 2])
                xo = xnb[par]
                s_a.inc(SC.activation(xo, xb, AF.Identity, bias=cbias, scale=crstd))
                if tt == 0:
                    a_done_xn = {}
                a_done_xn[tt] = s_a.n
                s_a.wait(SC, s_a.n)
                s_xs.inc(SC.dma_start(xn_dram[tt * 128:(tt + 1) * 128, :], xo), 16)
                s_a.wait(T, a_done_xn[tt])
                for fc in range(32):
                    u = tt * 32 + fc
                    if u >= 4:
                        s_fm.wait(T, ev_xn[u - 4])
                    s_pe.inc(T.transpose(xtp[u % 4][:, 0:128],
                                         xo[:, fc * 128:(fc + 1) * 128], ident))
                    s_pe.wait(V, s_pe.n)
                    s_fm.inc(V.tensor_copy(
                        xn_fm[:, fc * 1024 + tt * 128: fc * 1024 + (tt + 1) * 128],
                        xtp[u % 4][:, 0:128]))
                    ev_xn[u] = s_fm.n
                tr_xn[tt] = s_pe.n
                next(pgen, None)
                next(pgen, None)
            for _ in pgen:
                pass
        p0_pe_end = s_pe.n

        if phase_limit == "P0":
            finish_debug()
            return nc
        if phase_limit == "P1":
            finish_debug()
            return nc
        wsm_f = B[:, 0:8192].bitcast(F32)
        wsm_b = B[:, 8192:16384]
        s_v.wait(SY, s_v.n)
        s_ct.inc(SY.dma_start(wsm_f[0:16, :], wres_in), 16)
        s_ct.inc(SY.dma_start(wsm_f[16:20, :], wpre_in), 16)
        s_ct.inc(SY.dma_start(wsm_f[20:24, :], wpost_in), 16)
        s_ct.wait(V)
        s_v.inc(V.tensor_copy(wsm_b[0:32, 0:4096], wsm_f[0:32, :]))

        Zf = sink
        U0, V0, SCR, R64, R32a, R32b, R32c = 128, 160, 192, 320, 384, 416, 448
        ZP, ZQ = 480, 512

        with contextlib.ExitStack() as pst:
            ps_a = pst.enter_context(nc.psum_tensor("ps_a", [128, 512], F32))
            ps_b = pst.enter_context(nc.psum_tensor("ps_b", [128, 512], F32))
            ps_tb = [pst.enter_context(nc.psum_tensor(f"ps_tb{i}", [128, 64], BF16))
                     for i in range(2)]
            psl = [ps_a, ps_b]

            s_v.wait(T)
            s_fm.wait(T)
            s_ct.wait(T)
            for fc in range(32):
                if fc >= 2:
                    s_pp.wait(T, s_pp.n - 1)
                s_pe.inc(T.transpose(ps_tb[fc % 2][:, 0:32],
                                     wsm_b[0:32, fc * 128:(fc + 1) * 128], ident[0:32, 0:32]))
                s_pe.wait(V, s_pe.n)
                s_pp.inc(V.tensor_copy(waT[:, fc * 24:(fc + 1) * 24],
                                       ps_tb[fc % 2][:, 0:24]))

            s_pp.wait(T, s_pp.n)
            s_pe.inc(T.matmul(ps_a[:, 0:24], onesf[0:1, 0:128],
                              hco[0:1, HC_BASE:HC_BASE + 24], start=True, stop=True))
            s_pe.inc(T.matmul(ps_b[:, 0:1], onesf[0:1, 0:128], gcol[0:1, 3:4],
                              start=True, stop=True))
            s_pe.wait(V, s_pe.n)
            s_v.inc(V.tensor_copy(hco[:, HC_BASE:HC_BASE + 24], ps_a[:, 0:24]))
            s_pe.wait(SC, s_pe.n)
            s_a.inc(SC.activation(gcol[:, 4:5], ps_b[:, 0:1], AF.Exp, scale=-1.0))
            s_a.wait(V)
            s_v.inc(V.tensor_scalar(gcol[:, 5:6], gcol[:, 4:5], 1.0, None, ALU.add))
            s_v.wait(V)
            s_v.inc(V.reciprocal(gcol[:, 0:1], gcol[:, 5:6]))
            s_v.wait(V)
            s_v.inc(V.tensor_scalar(gcol[:, 1:2], gcol[:, 0:1], -1.0, 1.0, ALU.mult, ALU.add))
            s_v.wait(V)

            s_fm.wait(T)
            s_v.wait(T, s_v.n)
            s_a.wait(T, s_a.n)
            logit_done = {}
            for tt in range(NT):
                ps = psl[tt % 2]
                if tt >= 2:
                    s_v.wait(T, logit_done[tt - 2])
                mm = None
                for fc in range(32):
                    mm = T.matmul(ps[:, 0:24],
                                  xn_fm[:, fc * 1024 + tt * 128: fc * 1024 + (tt + 1) * 128],
                                  waT[:, fc * 24:(fc + 1) * 24],
                                  start=(fc == 0), stop=(fc == 31))
                s_pe.inc(mm)
                s_pe.wait(V, s_pe.n)
                r2 = stats[:, ST_R2 + tt:ST_R2 + tt + 1]
                s_v.inc(V.scalar_tensor_tensor(Zf[:, tt * 16:(tt + 1) * 16], ps[:, 0:16],
                                               r2, hco[:, HC_BASE:HC_BASE + 16],
                                               ALU.mult, ALU.add))
                s_v.inc(V.scalar_tensor_tensor(Zf[:, ZP + tt * 4:ZP + (tt + 1) * 4], ps[:, 16:20],
                                               r2, hco[:, HC_BASE + 16:HC_BASE + 20],
                                               ALU.mult, ALU.add))
                s_v.inc(V.scalar_tensor_tensor(Zf[:, ZQ + tt * 4:ZQ + (tt + 1) * 4], ps[:, 20:24],
                                               r2, hco[:, HC_BASE + 20:HC_BASE + 24],
                                               ALU.mult, ALU.add))
                logit_done[tt] = s_v.n

            s_v.wait(V)
            if phase_limit == "P2a_logits":
                finish_debug()
                return nc
            s_v.inc(V.tensor_scalar_mul(Zf[:, 0:128], Zf[:, 0:128], 1.0 / TAU))
            s_v.inc(V.memset(Zf[:, U0:U0 + 32], 0.0))
            s_v.inc(V.memset(Zf[:, V0:V0 + 32], 0.0))
            s_v.wait(V)

            z4 = Zf[:, 0:128].rearrange("p (g i j) -> p g i j", g=8, i=4, j=4)
            z4T = Zf[:, 0:128].rearrange("p (g i j) -> p g j i", g=8, i=4, j=4)
            sc4 = Zf[:, SCR:SCR + 128].rearrange("p (g i j) -> p g i j", g=8, i=4, j=4)
            sc4T = Zf[:, SCR:SCR + 128].rearrange("p (g i j) -> p g j i", g=8, i=4, j=4)
            scf = Zf[:, SCR:SCR + 128]
            u3 = Zf[:, U0:U0 + 32].rearrange("p (g i) -> p g i", g=8)
            v3 = Zf[:, V0:V0 + 32].rearrange("p (g j) -> p g j", g=8)
            r64a = Zf[:, R64:R64 + 64]
            r32a = Zf[:, R32a:R32a + 32]
            r32b = Zf[:, R32b:R32b + 32]
            r32c = Zf[:, R32c:R32c + 32]

            def half_iter(out32, reduce_j):
                if reduce_j:
                    s_v.inc(V.tensor_tensor(sc4T, z4T, v3.broadcast_to([128, 8, 4, 4]), ALU.add))
                else:
                    s_v.inc(V.tensor_tensor(sc4, z4, u3.broadcast_to([128, 8, 4, 4]), ALU.add))
                s_v.wait(V)
                if reduce_j:
                    pv = scf.rearrange("p (x b) -> p x b", b=2)
                    s_v.inc(V.tensor_max(r64a, pv[:, :, 0], pv[:, :, 1]))
                    s_v.wait(V)
                    pv2 = r64a.rearrange("p (x b) -> p x b", b=2)
                    s_v.inc(V.tensor_max(r32a, pv2[:, :, 0], pv2[:, :, 1]))
                else:
                    g3a = Zf[:, R64:R64 + 32].rearrange("p (g j) -> p g j", g=8)
                    g3b = Zf[:, R64 + 32:R64 + 64].rearrange("p (g j) -> p g j", g=8)
                    s_v.inc(V.tensor_max(g3a, sc4[:, :, 0, :], sc4[:, :, 1, :]))
                    s_v.inc(V.tensor_max(g3b, sc4[:, :, 2, :], sc4[:, :, 3, :]))
                    s_v.wait(V)
                    s_v.inc(V.tensor_max(r32a, Zf[:, R64:R64 + 32], Zf[:, R64 + 32:R64 + 64]))
                s_v.wait(V)
                m3 = r32a.rearrange("p (g a) -> p g a", g=8)
                if reduce_j:
                    s_v.inc(V.tensor_tensor(sc4, sc4, m3.broadcast_to([128, 8, 4, 4]), ALU.subtract))
                else:
                    s_v.inc(V.tensor_tensor(sc4T, sc4T, m3.broadcast_to([128, 8, 4, 4]), ALU.subtract))
                s_v.wait(SC)
                s_a.inc(SC.activation(scf, scf, AF.Exp))
                s_a.wait(V)
                if reduce_j:
                    pv = scf.rearrange("p (x b) -> p x b", b=2)
                    s_v.inc(V.tensor_add(r64a, pv[:, :, 0], pv[:, :, 1]))
                    s_v.wait(V)
                    pv2 = r64a.rearrange("p (x b) -> p x b", b=2)
                    s_v.inc(V.tensor_add(r32b, pv2[:, :, 0], pv2[:, :, 1]))
                else:
                    g3a = Zf[:, R64:R64 + 32].rearrange("p (g j) -> p g j", g=8)
                    g3b = Zf[:, R64 + 32:R64 + 64].rearrange("p (g j) -> p g j", g=8)
                    s_v.inc(V.tensor_add(g3a, sc4[:, :, 0, :], sc4[:, :, 1, :]))
                    s_v.inc(V.tensor_add(g3b, sc4[:, :, 2, :], sc4[:, :, 3, :]))
                    s_v.wait(V)
                    s_v.inc(V.tensor_add(r32b, Zf[:, R64:R64 + 32], Zf[:, R64 + 32:R64 + 64]))
                s_v.wait(SC)
                s_a.inc(SC.activation(r32c, r32b, AF.Ln))
                s_a.wait(V)
                s_v.inc(V.tensor_add(r32c, r32c, r32a))
                s_v.wait(V)
                s_v.inc(V.tensor_scalar_mul(out32, r32c, -1.0))
                s_v.wait(V)

        def coef_gen():
            for _ in range(SINK_ITERS):
                half_iter(Zf[:, U0:U0 + 32], True)
                yield
                half_iter(Zf[:, V0:V0 + 32], False)
                yield

            s_v.inc(V.tensor_tensor(sc4, z4, u3.broadcast_to([128, 8, 4, 4]), ALU.add))
            s_v.wait(V)
            s_v.inc(V.tensor_tensor(sc4T, sc4T, v3.broadcast_to([128, 8, 4, 4]), ALU.add))
            s_v.wait(SC)
            s_a.inc(SC.activation(scf, scf, AF.Exp))
            s_a.wait(V)
            s_v.inc(V.tensor_scalar(hco[:, HC_RES:HC_RES + 128], scf, gcol[:, 0:1], None, ALU.mult))
            s_v.wait(V)
            hdiag = hco[:, HC_RES:HC_RES + 128].rearrange("p (g x) -> p g x", g=8)[:, :, 0:16:5]
            s_v.inc(V.tensor_scalar(hdiag, hdiag, gcol[:, 1:2], None, ALU.add))
            s_v.wait(V)
            yield

            def soft4(dst_off, src_off):
                sv = Zf[:, src_off:src_off + 32]
                p2 = sv.rearrange("p (x b) -> p x b", b=2)
                s_v.inc(V.tensor_max(Zf[:, R64:R64 + 16], p2[:, :, 0], p2[:, :, 1]))
                s_v.wait(V)
                p1 = Zf[:, R64:R64 + 16].rearrange("p (x b) -> p x b", b=2)
                s_v.inc(V.tensor_max(Zf[:, R32a:R32a + 8], p1[:, :, 0], p1[:, :, 1]))
                s_v.wait(V)
                s_v.inc(V.tensor_tensor(sv.rearrange("p (g j) -> p g j", g=8),
                                        sv.rearrange("p (g j) -> p g j", g=8),
                                        Zf[:, R32a:R32a + 8].broadcast_to([128, 8, 4]),
                                        ALU.subtract))
                s_v.wait(SC)
                s_a.inc(SC.activation(sv, sv, AF.Exp))
                s_a.wait(V)
                s_v.inc(V.tensor_add(Zf[:, R64:R64 + 16], p2[:, :, 0], p2[:, :, 1]))
                s_v.wait(V)
                s_v.inc(V.tensor_add(Zf[:, R32a:R32a + 8], p1[:, :, 0], p1[:, :, 1]))
                s_v.wait(V)
                s_v.inc(V.reciprocal(Zf[:, R32b:R32b + 8], Zf[:, R32a:R32a + 8]))
                s_v.wait(V)
                s_v.inc(V.tensor_tensor(hco[:, dst_off:dst_off + 32].rearrange("p (g j) -> p g j", g=8),
                                        sv.rearrange("p (g j) -> p g j", g=8),
                                        Zf[:, R32b:R32b + 8].broadcast_to([128, 8, 4]),
                                        ALU.mult))
                s_v.wait(V)

            soft4(HC_PRE, ZP)
            yield
            soft4(HC_POST, ZQ)
            s_v.inc(V.tensor_copy(hco[:, HC_BASE:HC_BASE + 1], hco[:, HC_BASE:HC_BASE + 1]))
            coef_done[0] = s_v.n
            yield

        coef_done = [None]
        cgen = coef_gen()

        if phase_limit == "P2a":
            for _ in cgen:
                pass
            finish_debug()
            return nc
        def proj(wbase, col_off, slice_w, fo_total, epilogue, gbase, bg=None,
                 bg_steps=0, ws_need=1, entry_guard=True):
            ngroups = fo_total // 256
            with contextlib.ExitStack() as ps_st:
                pp = [ps_st.enter_context(nc.psum_tensor(f"pp{gbase}_{i}", [128, 512], F32))
                      for i in range(2)]
                ps_ss = [ps_st.enter_context(nc.psum_tensor(f"pss{gbase}{i}", [1, 512], F32))
                         for i in range(1)]
                ps_iv = [ps_st.enter_context(nc.psum_tensor(f"piv{gbase}_{i}", [128, 512], F32))
                         for i in range(1)]
                wtb = [C[:, 8192:16384], C[:, 0:8192]]
                s_v.wait(T, s_v.n)
                s_a.wait(T, s_a.n)
                s_pp.wait(T, s_pp.n)
                if entry_guard:
                    s_pp.wait(SY, s_pp.n)
                    s_v.wait(SY, s_v.n)
                s_ws.wait(SY, ws_need)
                wt_done = {}
                unit_done = {}
                for g in range(ngroups):
                    fo0 = g * 256
                    if g == 1:
                        ring_wc.wait_all(SY)
                        s_pe.wait(SY, p0_pe_end)
                    if g >= 2:
                        sm_, ct_ = wt_done[g - 2]
                        SY.wait_ge(sm_.h, ct_)
                    for fc in range(32):
                        if slice_w == 512:
                            r_ = fo0 // 512
                            lo_ = fo0 % 512
                            ring_wt.inc(SY.dma_start(
                                wtb[g % 2][:, fc * 256:(fc + 1) * 256],
                                wbase[r_ * DM + fc * 128: r_ * DM + (fc + 1) * 128,
                                      col_off + lo_: col_off + lo_ + 256]), g)
                        else:
                            for hf in range(2):
                                r_ = (fo0 + hf * 128) // 128
                                ring_wt.inc(SY.dma_start(
                                    wtb[g % 2][:, fc * 256 + hf * 128: fc * 256 + (hf + 1) * 128],
                                    wbase[r_ * DM + fc * 128: r_ * DM + (fc + 1) * 128,
                                          col_off: col_off + 128]), g)
                    ring_wt.wait(T, g)
                    for blk in range(2):
                        for tch in range(2):
                            u = (g * 2 + blk) * 2 + tch
                            ps = pp[u % 2]
                            if u >= 2:
                                sm_, ct_ = unit_done[u - 2]
                                T.wait_ge(sm_.h, ct_)
                            mm = None
                            for fc in range(32):
                                mm = T.matmul(ps[:, 0:512],
                                              wtb[g % 2][:, fc * 256 + blk * 128: fc * 256 + (blk + 1) * 128],
                                              xn_fm[:, fc * 1024 + tch * 512: fc * 1024 + (tch + 1) * 512],
                                              start=(fc == 0), stop=(fc == 31))
                            s_pe.inc(mm)
                            ud = epilogue(g * 2 + blk, tch, ps, ps_ss[0], ps_iv[0])
                            unit_done[u] = ud
                            if bg is not None:
                                for _ in range(bg_steps):
                                    if next(bg, "DONE") == "DONE":
                                        bg = None
                                        break
                    wt_done[g] = ud
                flush = getattr(epilogue, "flush", None)
                if flush is not None:
                    flush()

        qf32 = [nrow[:, 0:0], nrow[:, 0:0]]
        qsc = Q[:, 0:4096].bitcast(F32)
        qsq = Q[:, 4096:5120]
        ivrow = Q[:, 5120:7168].bitcast(F32)
        kvdst = {"v": B[:, 8192:16384], "k": B[:, 0:8192]}

        def plain_epi(dst):
            def epi(blk, tch, ps, ps_ss, ps_iv):
                s_pe.wait(V, s_pe.n)
                s_pp.inc(V.tensor_copy(dst[:, blk * 1024 + tch * 512: blk * 1024 + (tch + 1) * 512],
                                       ps[:, 0:512]))
                return (s_pp, s_pp.n)
            return epi

        def norm_epi(dst, wcol, to_dram):
            st_ = {"recs": {}, "u": 0, "ln": s_a.n, "stt": s_v.n,
                   "ssmm": {}, "ivmm": {}, "ps_ss": None, "ps_iv": None}

            def stage_b(r):
                ps_ss = st_["ps_ss"]
                u = r["u"]
                s_v.wait(T, r["mul"])
                s_a.wait(T, st_["ln"])
                s_pe.inc(T.matmul(ps_ss[0:1, 0:512], onescol[:, 0:1],
                                  qsq[:, (u % 2) * 512:(u % 2) * 512 + 512],
                                  start=True, stop=True))
                st_["ssmm"][u] = s_pe.n
                ivr = ivrow[0:1, (u % 2) * 512:(u % 2) * 512 + 512]
                s_pe.wait(SC, s_pe.n)
                if u >= 2:
                    s_pe.wait(SC, st_["ivmm"][u - 2])
                s_a.inc(SC.activation(ivr, ps_ss[0:1, 0:512], AF.Ln,
                                      bias=QK_EPS, scale=1.0 / HD))
                s_a.wait(SC)
                s_a.inc(SC.activation(ivr, ivr, AF.Exp, scale=-0.5))
                r["exp"] = s_a.n
                st_["ln"] = s_a.n

            def stage_c(r):
                ps_iv = st_["ps_iv"]
                u = r["u"]
                ivr = ivrow[0:1, (u % 2) * 512:(u % 2) * 512 + 512]
                s_a.wait(T, r["exp"])
                s_v.wait(T, st_["stt"])
                s_pe.inc(T.matmul(ps_iv[:, 0:512], onesf[0:1, 0:128], ivr,
                                  start=True, stop=True))
                st_["ivmm"][u] = s_pe.n
                s_pe.wait(V, s_pe.n)
                if to_dram is not None and u >= 2:
                    ring_q.wait(V, u - 2)
                qf = qsc[:, (u % 4) * 512:(u % 4) * 512 + 512]
                d = dst(r["blk"], r["tch"], u)
                s_v.inc(V.scalar_tensor_tensor(d, qf, qnw[:, wcol:wcol + 1],
                                               ps_iv[:, 0:512], ALU.mult, ALU.mult))
                st_["stt"] = s_v.n
                if to_dram is not None:
                    s_v.wait(SC, s_v.n)
                    ring_q.inc(SC.dma_start(to_dram(r["blk"], r["tch"]), d), u)

            def epi(blk, tch, ps, ps_ss, ps_iv):
                st_["ps_ss"], st_["ps_iv"] = ps_ss, ps_iv
                u = st_["u"]
                st_["u"] = u + 1
                s_pe.wait(V, s_pe.n)
                qf = qsc[:, (u % 4) * 512:(u % 4) * 512 + 512]
                s_v.inc(V.tensor_copy(qf, ps[:, 0:512]))
                ud = (s_v, s_v.n)
                if u >= 2:
                    s_pe.wait(V, st_["ssmm"][u - 2])
                s_v.wait(V)
                s_v.inc(V.tensor_mul(qsq[:, (u % 2) * 512:(u % 2) * 512 + 512], qf, qf))
                st_["recs"][u] = {"u": u, "blk": blk, "tch": tch, "mul": s_v.n}
                if u >= 1:
                    stage_b(st_["recs"][u - 1])
                if u >= 2:
                    stage_c(st_["recs"].pop(u - 2))
                return ud

            def flush():
                n = st_["u"]
                if n >= 1:
                    stage_b(st_["recs"][n - 1])
                for uu in (n - 2, n - 1):
                    if uu >= 0 and uu in st_["recs"]:
                        stage_c(st_["recs"].pop(uu))
            epi.flush = flush
            return epi

        s_fm.wait(T)
        proj(wtr_all_kv, 128, 128, HKV * HD, plain_epi(kvdst["v"]), "v", bg=cgen,
             bg_steps=4, entry_guard=False)
        s_xs.wait(V)
        s_xs.wait(SC)
        kslice = lambda blk, tch, u: kvdst["k"][:, blk * 1024 + tch * 512: blk * 1024 + (tch + 1) * 512]
        proj(wtr_all_kv, 0, 128, HKV * HD, norm_epi(kslice, 1, None), "k", bg=cgen, bg_steps=4)
        for _ in cgen:
            pass

        vtm = B[:, 16384:24576]
        ring_ws.wait_all(V)
        with contextlib.ExitStack() as pvs:
            vtp = [pvs.enter_context(nc.psum_tensor(f"vtp{i}", [128, 128], BF16))
                   for i in range(2)]
            s_v.wait(T, s_v.n)
            s_a.wait(T, s_a.n)
            vt_done = {}
            for kv in range(HKV):
                for j in range(NT):
                    u = kv * NT + j
                    if u >= 2:
                        s_v.wait(T, vt_done[u - 2])
                    s_pe.inc(T.transpose(vtp[u % 2][:, 0:128],
                                         kvdst["v"][:, kv * 1024 + j * 128: kv * 1024 + (j + 1) * 128],
                                         ident))
                    s_pe.wait(V, s_pe.n)
                    s_v.inc(V.tensor_copy(vtm[:, (j * 8 + kv) * 128:(j * 8 + kv + 1) * 128],
                                          vtp[u % 2][:, 0:128]))
                    vt_done[u] = s_v.n
        s_pp.wait(SC, s_pp.n)
        s_v.wait(SC, s_v.n)
        for kv in range(HKV):
            s_kvs.inc(SC.dma_start(kv_send[kv * 128:(kv + 1) * 128, :],
                                   kvdst["k"][:, kv * 1024:(kv + 1) * 1024]), 16)
        for j in range(NT):
            s_kvs.inc(SC.dma_start(kv_send[1024 + j * 128:1024 + (j + 1) * 128, :],
                                   vtm[:, j * 1024:(j + 1) * 1024]), 16)
        s_kvs.wait(GP, s_kvs.n)
        s_cc.inc(GP.collective_compute(
            "AllGather", ALU.bypass,
            replica_groups=[list(range(NC))],
            ins=[kv_send.opt()], outs=[kv_all.opt()],
        ))

        s_fm.wait(GP)
        s_xs.wait(GP)
        for ci in range(8):
            s_wb.inc(GP.dma_start(Q[:, 7680 + ci * 1024:7680 + (ci + 1) * 1024],
                                  wb_in[ci * 128:(ci + 1) * 128, :]), 16)
        wbsb = Q[:, 7680:15872]
        xnt = [B[:, 0:4096], B[:, 4096:8192]]
        xpf = B[:, 8192:12288].bitcast(F32)
        xpre_b = B[:, 12288:13312]
        xprefm = B[:, 13312:14336]
        yv = xpf[:, 1024:2048]
        fo_f = B[:, 14336:22528].bitcast(F32)

        def mix_gen(pf, pfb):
            s_kvs.wait(SY, s_kvs.n)
            v_used_xnt = {}
            for tt in range(NT):
                par = tt % 2
                if tt >= 2:
                    s_v.wait(SY, v_used_xnt[tt - 2])
                ring_x.inc(SY.dma_start(xnt[par], xn_dram[tt * 128:(tt + 1) * 128, :]), tt)
                ring_x.wait(V, tt)
                if tt >= 1:
                    s_ff.wait(V, s_ff.n)
                hpre = lambda n: hco[:, HC_PRE + tt * 4 + n: HC_PRE + tt * 4 + n + 1]
                hpost = lambda n: hco[:, HC_POST + tt * 4 + n: HC_POST + tt * 4 + n + 1]
                hres = lambda n, jj: hco[:, HC_RES + tt * 16 + n * 4 + jj: HC_RES + tt * 16 + n * 4 + jj + 1]
                s_v.inc(V.tensor_scalar(xpf[:, 0:1024], xnt[par][:, 0:1024], hpre(0), None, ALU.mult))
                for n in range(1, 4):
                    s_v.wait(V)
                    s_v.inc(V.scalar_tensor_tensor(xpf[:, 0:1024], xnt[par][:, n * 1024:(n + 1) * 1024],
                                                   hpre(n), xpf[:, 0:1024], ALU.mult, ALU.add))
                s_v.wait(V)
                s_v.inc(V.tensor_copy(xpre_b, xpf[:, 0:1024]))
                s_v.wait(T)
                yield
                for cc in range(8):
                    s_pe.inc(T.transpose(pfb[cc % 2][:, 0:128],
                                         xpre_b[:, cc * 128:(cc + 1) * 128], ident))
                    s_pe.wait(V, s_pe.n)
                    s_v.inc(V.tensor_copy(xprefm[:, cc * 128:(cc + 1) * 128],
                                          pfb[cc % 2][:, 0:128]))
                    if cc >= 1:
                        s_v.wait(T, s_v.n - 1)
                s_v.wait(T, s_v.n)
                s_wb.wait(T, s_wb.n)
                yield
                for co in range(2):
                    ps = pf[co % 2]
                    mm = None
                    for cc in range(8):
                        mm = T.matmul(ps[:, 0:512], xprefm[:, cc * 128:(cc + 1) * 128],
                                      wbsb[:, cc * 1024 + co * 512: cc * 1024 + (co + 1) * 512],
                                      start=(cc == 0), stop=(cc == 7))
                    s_pe.inc(mm)
                    s_pe.wait(V, s_pe.n)
                    s_v.inc(V.tensor_copy(yv[:, co * 512:(co + 1) * 512], ps[:, 0:512]))
                s_v.wait(V)
                yield
                for n in range(4):
                    fon = fo_f[:, n * 1024:(n + 1) * 1024]
                    s_v.inc(V.tensor_scalar(fon, xnt[par][:, 0:1024], hres(n, 0), None, ALU.mult))
                    for jj in range(1, 4):
                        s_v.wait(V)
                        s_v.inc(V.scalar_tensor_tensor(fon, xnt[par][:, jj * 1024:(jj + 1) * 1024],
                                                       hres(n, jj), fon, ALU.mult, ALU.add))
                    s_v.wait(V)
                    s_v.inc(V.scalar_tensor_tensor(fon, yv, hpost(n), fon, ALU.mult, ALU.add))
                    s_v.wait(V)
                v_used_xnt[tt] = s_v.n
                s_v.wait(GP, s_v.n)
                s_ff.inc(GP.dma_start(ffn_dram[tt * 128:(tt + 1) * 128, :], fo_f), 16)
                yield

        qtmp = [Q[:, 7168:7680], Q[:, 15872:16384]]
        qsl = lambda blk, tch, u: qtmp[u % 2][:, 0:512]
        qdr = lambda blk, tch: q_dram[:, (tch * H + blk) * 512:(tch * H + blk + 1) * 512]
        with contextlib.ExitStack() as ps_mx:
            pf_ = [ps_mx.enter_context(nc.psum_tensor(f"pf{i}", [128, 512], F32)) for i in range(2)]
            pfb_ = [ps_mx.enter_context(nc.psum_tensor(f"pfb{i}", [128, 128], BF16)) for i in range(2)]
            mgen = mix_gen(pf_, pfb_)
            proj(wtr_all_q, 0, 512, DM, norm_epi(qsl, 0, qdr), "q", bg=mgen,
                 bg_steps=2, ws_need=2)
            for _ in mgen:
                pass

        if phase_limit == "P3":
            finish_debug()
            return nc

        if phase_limit == "P2b":
            finish_debug()
            return nc
        VOFF = [0, 9216]
        POFF = 18432
        ATOFF = 21504
        PLOFF = [22016, 23680]
        masksb = B[:, 16384:24576]
        qslab = Q[:, 0:16384]

        with contextlib.ExitStack() as ps_st:
            ps_sc = [ps_st.enter_context(nc.psum_tensor(f"psc{i}", [128, 512], F32)) for i in range(5)]
            ps_av = [ps_st.enter_context(nc.psum_tensor(f"pav{i}", [128, 512], F32)) for i in range(2)]
            ps_pl = ps_st.enter_context(nc.psum_tensor("pspl", [128, 512], F32))

            s_pe.wait(SY, s_pe.n)
            s_ff.wait(SY, s_ff.n)
            s_v.wait(SY, s_v.n)
            s_kvs.wait(SY, s_kvs.n)
            s_ct.inc(SY.dma_start(B[:, 16384:24576], mask_in), 16)
            ring_ws.wait_all(GP)
            s_ws.inc(GP.collective_compute(
                "AllGather", ALU.bypass,
                replica_groups=[list(range(NC))],
                ins=[wtr_o.opt()], outs=[wtr_all_o.opt()],
            ))
            s_ct.wait(GP, s_ct.n)

            s_pe.wait(GP, s_pe.n)
            s_v.wait(GP, s_v.n)
            for par_ in range(2):
                vv = A[:, VOFF[par_]:VOFF[par_] + 9216].rearrange(
                    "p (t c) -> p t c", c=144)[:, :, 128:129]
                s_cm.inc(GP.memset(vv, 1.0), 1)
            s_cm.wait(T)

            score_cnt_end = {}
            av_cnt_end = {}
            ex_cnt_end = {}
            KEYSEQ = [(qg_, kv_) for qg_ in range(2) for kv_ in range(HKV)]

            def load_kv(qg, kv):
                seq = KEYSEQ.index((qg, kv))
                par = seq % 2
                kb = C[:, par * 8192:(par + 1) * 8192]
                s_cc.wait(SY, 1)
                if seq >= 2:
                    key = KEYSEQ[seq - 2]
                    s_sc.wait(SY, score_cnt_end[key])
                    s_av.wait(SY, av_cnt_end[key])
                jlo = 0 if qg == 0 else 4
                nj = NT - jlo
                for r in range(NC):
                    ring_kv.inc(SY.dma_start(kb[:, r * 1024:(r + 1) * 1024],
                                             kv_all[r * 2048 + kv * 128: r * 2048 + (kv + 1) * 128, :]), seq)
                    vsrc = kv_all[r * 2048 + 1024 + jlo * 128: r * 2048 + 2048,
                                  kv * 128:(kv + 1) * 128].rearrange(
                        "(j p) h -> p j h", p=128)
                    vdst = A[:, VOFF[par] + (r * 8 + jlo) * 144:
                             VOFF[par] + (r * 8 + 8) * 144].rearrange(
                        "p (j c) -> p j c", c=144)[:, :, 0:128]
                    ring_kv.inc(SY.dma_start(vdst, vsrc), seq)
                return seq

            def build_plane(qg, kv):
                seq = KEYSEQ.index((qg, kv))
                par = seq % 2
                ncols = PLANE_COUNTS[(qg, kv)]
                off = PLANE_OFF[(qg, kv)]
                if seq >= 2:
                    key = KEYSEQ[seq - 2]
                    s_ex.wait(SY, ex_cnt_end[key])
                prb = prhs_bufs[seq % 2]
                ring_pr.inc(SY.dma_start(prb[0:2, 0:ncols], prhs_in[:, off:off + ncols]), seq)
                ring_pr.wait(T, seq)
                pl = A[:, PLOFF[par]:PLOFF[par] + 1664].bitcast(F32)
                nch = (ncols + 511) // 512
                for chk in range(nch):
                    c0 = chk * 512
                    n = min(512, ncols - c0)
                    s_pl.wait(T, s_pl.n)
                    s_pe.inc(T.matmul(ps_pl[:, 0:n], k2[0:2, 0:128], prb[0:2, c0:c0 + n],
                                      start=True, stop=True))
                    s_pe.wait(V, s_pe.n)
                    s_pl.inc(V.tensor_scalar(pl[:, c0:c0 + n], ps_pl[:, 0:n],
                                             shift_b[:, 0:1], None, ALU.subtract))
                return s_pl.n

            s_v.wait(T, s_v.n)
            s_a.wait(T, s_a.n)
            s_pe.inc(T.matmul(ps_pl[:, 0:1], onesf[0:1, 0:128], gcol[0:1, 7:8],
                              start=True, stop=True))
            s_pe.wait(V, s_pe.n)
            s_pl.inc(V.tensor_copy(shift_b[:, 0:1], ps_pl[:, 0:1]))

            prhs_bufs = [B[:, 0:4992].bitcast(F32), B[:, 4992:9984].bitcast(F32)]
            pb_cnt = [0]
            at_cnt = [0]
            ex_tile_cnt = {}
            pfree = {}
            pending = {}

            pending[(0, 0)] = (load_kv(0, 0), build_plane(0, 0))
            pending[(0, 1)] = (load_kv(0, 1), build_plane(0, 1))

            for qg in range(2):
                ring_q.wait_all(SY)
                s_sc.wait(SY, s_sc.n)
                s_ld.inc(SY.dma_start(qslab, q_dram[:, qg * 16384:(qg + 1) * 16384]), 16)
                qload = s_ld.n
                tmax = T_SLOT[4 * qg]
                for kv in range(HKV):
                    klcnt, plcnt = pending.pop((qg, kv))
                    pcol = [0]
                    par = kv % 2
                    kb = C[:, par * 8192:(par + 1) * 8192]
                    pl = A[:, PLOFF[par]:PLOFF[par] + 1664].bitcast(F32)
                    for hg in range(4):
                        h = 4 * kv + hg
                        nf_before = s_nf.n
                        s_nf.wait(T, nf_before)
                        for bk in range(2):
                            T.matmul(ps_av[bk][:, 0:265], ident[0:1, 0:128],
                                     zrow[0:1, 0:265], start=True, stop=False)
                        kept = [t_ for t_ in range(tmax) if active_slots(h, qg, t_)]
                        PIPE = 3
                        sc_cnt = {}

                        def emit_score(idx):
                            t_ = kept[idx]
                            act_ = active_slots(h, qg, t_)
                            jr0_ = act_[0] - 4 * qg
                            nc_ = len(act_) * 128
                            ab = pb_cnt[0] + idx
                            ring_kv.wait(T, klcnt)
                            s_ld.wait(T, qload)
                            if ab >= 5:
                                s_ex.wait(T, ex_tile_cnt[ab - 5])
                            mm_ = T.matmul(ps_sc[ab % 5][:, jr0_ * 128: jr0_ * 128 + nc_],
                                           kb[:, KCOL[t_]:KCOL[t_] + 128],
                                           qslab[:, h * 512 + jr0_ * 128: h * 512 + jr0_ * 128 + nc_],
                                           start=True, stop=True)
                            s_sc.inc(mm_)
                            sc_cnt[ab] = s_sc.n

                        for i0 in range(min(PIPE, len(kept))):
                            emit_score(i0)
                        for idx, t in enumerate(kept):
                            act = active_slots(h, qg, t)
                            ab = pb_cnt[0] + idx
                            scpar = ab % 5
                            pbuf = A[:, POFF + (ab % 6) * 512: POFF + (ab % 6 + 1) * 512]
                            s_sc.wait(SC, sc_cnt[ab])
                            s_pl.wait(SC, plcnt)
                            if ab >= 6:
                                knd, cnt_ = pfree[ab - 6]
                                (s_av if knd == 'av' else s_pb).wait(SC, cnt_)
                            w = CHUNK_W[h]
                            nch = 128 // w
                            last_act = None
                            for j in act:
                                jr = j - 4 * qg
                                for ci in range(nch):
                                    last_act = SC.activation(
                                        pbuf[:, jr * 128 + ci * w: jr * 128 + (ci + 1) * w],
                                        ps_sc[scpar][:, jr * 128 + ci * w: jr * 128 + (ci + 1) * w],
                                        AF.Exp, bias=pl[:, pcol[0]:pcol[0] + 1], scale=ISQD)
                                    pcol[0] += 1
                            s_ex.inc(last_act)
                            ex_tile_cnt[ab] = s_ex.n
                            masked = [j for j in act if t >= T_SLOT[j] - 8]
                            for j in masked:
                                jr = j - 4 * qg
                                s_ex.wait(GP, s_ex.n)
                                s_mk.inc(GP.tensor_tensor(
                                    pbuf[:, jr * 128:(jr + 1) * 128],
                                    pbuf[:, jr * 128:(jr + 1) * 128],
                                    masksb[:, (j * 8 + (t - (T_SLOT[j] - 8))) * 128:
                                           (j * 8 + (t - (T_SLOT[j] - 8)) + 1) * 128],
                                    ALU.mult))
                            s_ex.wait(T, s_ex.n)
                            if masked:
                                s_mk.wait(T, s_mk.n)
                            bank_close = False
                            lastav = None
                            for j in act:
                                jr = j - 4 * qg
                                bk, half = jr // 2, jr % 2
                                closes = (jr in (0, 2) and t == T_SLOT[j] - 1)
                                pos_t = OWNER[t] * 8 + SLOT[t]
                                av = T.matmul(ps_av[bk][:, half * 136: half * 136 + 129],
                                              pbuf[:, jr * 128:(jr + 1) * 128],
                                              A[:, VOFF[par] + pos_t * 144: VOFF[par] + pos_t * 144 + 129],
                                              start=False, stop=closes)
                                if closes:
                                    s_av.inc(av)
                                    bank_close = True
                                lastav = av
                            if bank_close:
                                pfree[ab] = ('av', s_av.n)
                            else:
                                s_pb.inc(lastav)
                                pfree[ab] = ('pb', s_pb.n)
                            if idx + PIPE < len(kept):
                                emit_score(idx + PIPE)
                            for j in act:
                                jr = j - 4 * qg
                                if not (jr in (0, 2) and t == T_SLOT[j] - 1):
                                    continue
                                bk = jr // 2
                                s_av.wait(V, s_av.n)
                                for half in range(2):
                                    jn = 4 * qg + 2 * bk + half
                                    atb = A[:, ATOFF + (at_cnt[0] % 4) * 128: ATOFF + (at_cnt[0] % 4 + 1) * 128]
                                    ring_at.wait(V, at_cnt[0])
                                    dc = dencol[:, (at_cnt[0] % 4) * 2:(at_cnt[0] % 4) * 2 + 2]
                                    s_v.inc(V.tensor_scalar(dc[:, 0:1],
                                                            ps_av[bk][:, half * 136 + 128: half * 136 + 129],
                                                            DEN_GUARD, None, ALU.add))
                                    s_v.wait(V)
                                    s_v.inc(V.reciprocal(dc[:, 1:2], dc[:, 0:1]))
                                    s_v.wait(V)
                                    nrm = V.tensor_scalar(atb, ps_av[bk][:, half * 136: half * 136 + 128],
                                                          dc[:, 1:2], None, ALU.mult)
                                    s_nf.inc(nrm)
                                    s_nf.wait(SC, s_nf.n)
                                    ring_at.inc(SC.dma_start(
                                        attn_dram[jn * 128:(jn + 1) * 128, h * 128:(h + 1) * 128], atb), at_cnt[0])
                                    at_cnt[0] += 1
                        pb_cnt[0] += len(kept)
                    score_cnt_end[(qg, kv)] = s_sc.n
                    av_cnt_end[(qg, kv)] = s_av.n
                    ex_cnt_end[(qg, kv)] = s_ex.n
                    nseq = KEYSEQ.index((qg, kv)) + 2
                    if nseq < len(KEYSEQ):
                        nk = KEYSEQ[nseq]
                        pending[nk] = (load_kv(*nk), build_plane(*nk))

        if phase_limit == "P4":
            finish_debug()
            return nc
        attn_fm = A[:, 0:32768]
        xres = Q[:, 0:4096].bitcast(F32)
        outt = Q[:, 4096:8192].bitcast(F32)
        s_ff.wait(SC, s_ff.n)
        with contextlib.ExitStack() as ps_st:
            pw = [ps_st.enter_context(nc.psum_tensor(f"pw{i}", [128, 256], F32)) for i in range(2)]
            ftp = [ps_st.enter_context(nc.psum_tensor(f"ftp{i}", [128, 128], BF16))
                   for i in range(4)]
            s_v.wait(T, s_v.n)
            s_a.wait(T, s_a.n)

            def wo_load(g):
                par = g % 2
                wob = C[:, par * 8192:(par + 1) * 8192]
                r_ = (g * 256) // 512
                lo_ = (g * 256) % 512
                for hc in range(32):
                    ring_wt.inc(SY.dma_start(
                        wob[:, hc * 256:(hc + 1) * 256],
                        wtr_all_o[r_ * DM + hc * 128: r_ * DM + (hc + 1) * 128,
                                  lo_: lo_ + 256]), g)
                return wob

            s_ws.wait(SY, 3)
            s_sc.wait(SY, s_sc.n)
            wob_pre = {0: wo_load(0), 1: wo_load(1)}

            ring_at.wait_all(SY)
            s_pe.wait(SY, s_pe.n)
            arow = [B[:, 0:4096], B[:, 4096:8192]]
            evf = {}
            trf = {}
            for j in range(NT):
                if j >= 2:
                    s_pe.wait(SY, trf[j - 2])
                ring_kv.inc(SY.dma_start(arow[j % 2], attn_dram[j * 128:(j + 1) * 128, :]),
                            16 + j)
                ring_kv.wait(T, 16 + j)
                for hc in range(32):
                    u = j * 32 + hc
                    if u >= 4:
                        s_fm.wait(T, evf[u - 4])
                    s_pe.inc(T.transpose(ftp[u % 4][:, 0:128],
                                         arow[j % 2][:, hc * 128:(hc + 1) * 128], ident))
                    s_pe.wait(V, s_pe.n)
                    if u == 0:
                        s_av.wait(V, s_av.n)
                        s_mk.wait(V, s_mk.n)
                        ring_at.wait_all(V)
                    s_fm.inc(V.tensor_copy(
                        attn_fm[:, hc * 1024 + j * 128: hc * 1024 + (j + 1) * 128],
                        ftp[u % 4][:, 0:128]))
                    evf[u] = s_fm.n
                trf[j] = s_pe.n
            fm_all = s_fm.n

            wo_done = {}
            p5_done = {}
            for g in range(16):
                wob = wob_pre.pop(g, None)
                if wob is None:
                    s_pe.wait(SY, wo_done[g - 2])
                    wob = wo_load(g)
                ring_wt.wait(T, g)
                s_fm.wait(T, fm_all)
                for tt in range(NT):
                    un = (g * 8 + tt)
                    ps = pw[un % 2]
                    if un >= 2:
                        s_v.wait(T, p5_done[un - 2])
                    ring_y.wait(SC, un)
                    ring_x.inc(SC.dma_start(xres[:, (un % 2) * 1024: (un % 2) * 1024 + 256],
                                            x_in[tt * 128:(tt + 1) * 128, g * 256:(g + 1) * 256]), un)
                    ring_x.inc(SC.dma_start(xres[:, (un % 2) * 1024 + 256: (un % 2) * 1024 + 512],
                                            ffn_dram[tt * 128:(tt + 1) * 128, g * 256:(g + 1) * 256]), un)
                    mm = None
                    for hc in range(32):
                        mm = T.matmul(ps[:, 0:256],
                                      attn_fm[:, hc * 1024 + tt * 128: hc * 1024 + (tt + 1) * 128],
                                      wob[:, hc * 256:(hc + 1) * 256],
                                      start=(hc == 0), stop=(hc == 31))
                    s_pe.inc(mm)
                    s_pe.wait(V, s_pe.n)
                    ring_x.wait(V, un)
                    ot = outt[:, (un % 2) * 512:(un % 2) * 512 + 256]
                    s_v.inc(V.tensor_add(ot, ps[:, 0:256], xres[:, (un % 2) * 1024: (un % 2) * 1024 + 256]))
                    s_v.wait(V)
                    s_v.inc(V.tensor_add(ot, ot, xres[:, (un % 2) * 1024 + 256: (un % 2) * 1024 + 512]))
                    p5_done[un] = s_v.n
                    s_v.wait(GP, s_v.n)
                    ring_y.inc(GP.dma_start(y_out[tt * 128:(tt + 1) * 128, g * 256:(g + 1) * 256], ot), un)
                wo_done[g] = s_pe.n

            for tp, src in tap_outs:
                s_y.wait(GP, s_y.n)
                s_y.inc(GP.dma_start(tp, src), 16)
            s_y.wait(GP, s_y.n)

    return nc



_CONST = {}

def _consts():
    if not _CONST:
        _CONST["ident"] = np.eye(128, dtype=np.float32).astype(ml_dtypes.bfloat16)
        _CONST["ones"] = np.ones((1, 128), np.float32)
        _CONST["k2"] = np.stack([np.arange(128, dtype=np.float32),
                                 np.ones(128, np.float32)])
        _CONST["planes"] = [build_plane_rhs(c) for c in range(NC)]
        _CONST["masks"] = [build_masks(c) for c in range(NC)]
    return _CONST


def run(inputs, taps=(), phase_limit=None):
    cst = _consts()
    x = np.asarray(inputs["L0_attn_x"])
    xr = np.ascontiguousarray(x.reshape(NB, 128, DM))
    wq_np = np.asarray(inputs["L0_attn_Wq"])
    wk_np = np.asarray(inputs["L0_attn_Wk"])
    wv_np = np.asarray(inputs["L0_attn_Wv"])
    wo_np = np.asarray(inputs["L0_attn_Wo"])
    in_maps = []
    for c in range(NC):
        blocks = blocks_of_core(c)
        x_own = np.ascontiguousarray(xr[blocks].reshape(TOK, DM))
        w_slice = np.ascontiguousarray(np.concatenate([
            wq_np[512 * c:512 * (c + 1)], wk_np[128 * c:128 * (c + 1)],
            wv_np[128 * c:128 * (c + 1)], wo_np[512 * c:512 * (c + 1)]], axis=0))
        m = {
            "x_own": x_own,
            "w_slice": w_slice,
            "L0_attn_Wq": np.ascontiguousarray(inputs["L0_attn_Wq"]),
            "L0_attn_Wk": np.ascontiguousarray(inputs["L0_attn_Wk"]),
            "L0_attn_Wv": np.ascontiguousarray(inputs["L0_attn_Wv"]),
            "L0_attn_Wo": np.ascontiguousarray(inputs["L0_attn_Wo"]),
            "L0_attn_q_norm_w": np.ascontiguousarray(np.asarray(inputs["L0_attn_q_norm_w"]).reshape(1, HD)),
            "L0_attn_k_norm_w": np.ascontiguousarray(np.asarray(inputs["L0_attn_k_norm_w"]).reshape(1, HD)),
            "L0_ffn_W_res": np.ascontiguousarray(inputs["L0_ffn_W_res"]),
            "L0_ffn_W_pre": np.ascontiguousarray(inputs["L0_ffn_W_pre"]),
            "L0_ffn_W_post": np.ascontiguousarray(inputs["L0_ffn_W_post"]),
            "L0_ffn_H_res_base": np.ascontiguousarray(np.asarray(inputs["L0_ffn_H_res_base"]).reshape(1, 16)),
            "L0_ffn_H_pre_base": np.ascontiguousarray(np.asarray(inputs["L0_ffn_H_pre_base"]).reshape(1, 4)),
            "L0_ffn_H_post_base": np.ascontiguousarray(np.asarray(inputs["L0_ffn_H_post_base"]).reshape(1, 4)),
            "L0_ffn_gate": np.ascontiguousarray(np.asarray(inputs["L0_ffn_gate"]).reshape(1, 1)),
            "L0_ffn_W_branch": np.ascontiguousarray(inputs["L0_ffn_W_branch"]),
            "c_ident": cst["ident"],
            "c_ones_f32": cst["ones"],
            "c_k2": cst["k2"],
            "c_plane_rhs": cst["planes"][c],
            "c_masks": cst["masks"][c],
        }
        in_maps.append(m)

    nc_prog = build_program(taps=taps, phase_limit=phase_limit)
    res = run_bass_kernel_spmd(nc_prog, in_maps, core_ids=list(range(NC)))

    y = np.zeros((NB, 128, DM), np.float32)
    for c in range(NC):
        blocks = blocks_of_core(c)
        yc = np.asarray(res.results[c]["y_own"]).reshape(NT, 128, DM)
        for j, b in enumerate(blocks):
            y[b] = yc[j]
    out = y.reshape(1, S, DM)
    if taps:
        return out, res
    return out


def kernel(**inputs):
    return run(inputs)


# version 12
# speedup vs baseline: 2.6615x; 1.0763x over previous
import math
import contextlib
import numpy as np
import ml_dtypes

import concourse.bass as bass
import concourse.mybir as mybir
from concourse.bass_utils import run_bass_kernel_spmd

F32 = mybir.dt.float32
BF16 = mybir.dt.bfloat16
AF = mybir.ActivationFunctionType
ALU = mybir.AluOpType
AX = mybir.AxisListType.X

NC = 8
S = 8192
DM = 4096
H = 32
HKV = 8
HD = 128
NB = 64
TOK = 1024
NT = 8

LN_EPS = 1e-5
QK_EPS = 1e-6
RMS_EPS = 1.1920929e-07
TAU = 0.1
SINK_ITERS = 50
NEG = -1e30
DEN_GUARD = 1e-30
ISQD = 1.0 / math.sqrt(HD)

SLOPES = [2.0 ** (-(h + 1) / 4.0) for h in range(H)]

def _chunk_w(h):
    for w in (128, 64, 32):
        if SLOPES[h] * (w - 1) <= 76.0:
            return w
    return 32
CHUNK_W = [_chunk_w(h) for h in range(H)]

T_SLOT = [64 - 8 * j for j in range(NT)]
WINDOW_TILES = [min(64, int(math.ceil(90.0 / SLOPES[h] / 128.0)) + 1) for h in range(H)]

def blocks_of_core(c):
    return [63 - c, 48 + c, 47 - c, 32 + c, 31 - c, 16 + c, 15 - c, c]

OWNER = [0] * NB
SLOT = [0] * NB
for _c in range(NC):
    for _j, _b in enumerate(blocks_of_core(_c)):
        OWNER[_b] = _c
        SLOT[_b] = _j
KCOL = [OWNER[g] * TOK + SLOT[g] * 128 for g in range(NB)]

def kept_lo(h, j):
    if WINDOW_TILES is None:
        return 0
    return max(0, T_SLOT[j] - 8 - WINDOW_TILES[h])

def active_slots(h, qg, t):
    out = []
    for j in range(4 * qg, 4 * qg + 4):
        if kept_lo(h, j) <= t < T_SLOT[j]:
            out.append(j)
    return out


def att_exp_cols():
    for qg in range(2):
        tmax = T_SLOT[4 * qg]
        for kv in range(HKV):
            for hg in range(4):
                h = 4 * kv + hg
                for t in range(tmax):
                    for j in active_slots(h, qg, t):
                        for ci in range(128 // CHUNK_W[h]):
                            yield (kv, h, qg, j, t, ci)

PLANE_COUNTS = {}
for _rec in att_exp_cols():
    _k = (_rec[2], _rec[0])
    PLANE_COUNTS[_k] = PLANE_COUNTS.get(_k, 0) + 1
PLANE_OFF = {}
_off = 0
for _qg in range(2):
    for _kv in range(HKV):
        PLANE_OFF[(_qg, _kv)] = _off
        _off += PLANE_COUNTS[(_qg, _kv)]
PLANE_TOTAL = _off
PLANE_MAX = max(PLANE_COUNTS.values())



def build_plane_rhs(c):
    blocks = blocks_of_core(c)
    r0 = np.zeros(PLANE_TOTAL, np.float32)
    r1 = np.zeros(PLANE_TOTAL, np.float32)
    i = 0
    for (kv, h, qg, j, t, ci) in att_exp_cols():
        b = blocks[j]
        w = CHUNK_W[h]
        if t > b:
            r0[i] = 0.0
            r1[i] = NEG
        else:
            cq = b * 128 + ci * w + w // 2
            r0[i] = SLOPES[h]
            r1[i] = SLOPES[h] * (128.0 * t - cq)
        i += 1
    return np.stack([r0, r1]).astype(np.float32)


def build_masks(c):
    blocks = blocks_of_core(c)
    tri = (np.arange(128)[:, None] <= np.arange(128)[None, :]).astype(np.float32)
    out = np.zeros((128, NT * 8 * 128), np.float32)
    for j in range(NT):
        b = blocks[j]
        for r8 in range(8):
            t = T_SLOT[j] - 8 + r8
            col = (j * 8 + r8) * 128
            if t < b:
                out[:, col:col + 128] = 1.0
            elif t == b:
                out[:, col:col + 128] = tri
    return out.astype(ml_dtypes.bfloat16)



class Sem:
    def __init__(self, st, nc, name):
        self.h = st.enter_context(nc.semaphore(name))
        self.n = 0
        self._wcache = {}

    def inc(self, instr, k=1):
        instr.then_inc(self.h, k)
        self.n += k

    def wait(self, eng, val=None):
        v = self.n if val is None else val
        key = id(eng)
        if self._wcache.get(key, -1) >= v:
            return
        self._wcache[key] = v
        eng.wait_ge(self.h, v)


class Ring:
    def __init__(self, st, nc, name, n):
        self.s = [Sem(st, nc, f"{name}{i}") for i in range(n)]
        self.n = n

    def inc(self, instr, slot, k=16):
        self.s[slot % self.n].inc(instr, k)

    def wait(self, eng, slot):
        sm = self.s[slot % self.n]
        sm.wait(eng)

    def wait_all(self, eng):
        for sm in self.s:
            sm.wait(eng)


def W(eng, sem, v):
    sem.wait(eng, v)


def build_program(taps=(), phase_limit=None):
    nc = bass.Bass("TRN2", target_bir_lowering=False, debug=False)

    def I(name, shape, dt=F32):
        return nc.dram_tensor(name, shape, dt, kind="ExternalInput").ap()

    x_in = I("x_own", [TOK, DM])
    wsl_in = I("w_slice", [1280, DM])
    wq_in = I("L0_attn_Wq", [DM, DM])
    wk_in = I("L0_attn_Wk", [HKV * HD, DM])
    wv_in = I("L0_attn_Wv", [HKV * HD, DM])
    wo_in = I("L0_attn_Wo", [DM, DM])
    qnw_in = I("L0_attn_q_norm_w", [1, HD])
    knw_in = I("L0_attn_k_norm_w", [1, HD])
    wres_in = I("L0_ffn_W_res", [16, DM])
    wpre_in = I("L0_ffn_W_pre", [4, DM])
    wpost_in = I("L0_ffn_W_post", [4, DM])
    hresb_in = I("L0_ffn_H_res_base", [1, 16])
    hpreb_in = I("L0_ffn_H_pre_base", [1, 4])
    hpostb_in = I("L0_ffn_H_post_base", [1, 4])
    gate_in = I("L0_ffn_gate", [1, 1])
    wb_in = I("L0_ffn_W_branch", [1024, 1024])
    ident_in = I("c_ident", [128, 128], BF16)
    onesf_in = I("c_ones_f32", [1, 128])
    k2_in = I("c_k2", [2, 128])
    prhs_in = I("c_plane_rhs", [2, PLANE_TOTAL])
    mask_in = I("c_masks", [128, NT * 8 * 128], BF16)

    y_out = nc.dram_tensor("y_own", [TOK, DM], F32, kind="ExternalOutput").ap()

    wtr_kv = nc.dram_tensor("wtr_kv", [DM, 256], BF16).ap()
    wtr_all_kv = nc.dram_tensor("wtr_all_kv", [NC * DM, 256], BF16).ap()
    wtr_q = nc.dram_tensor("wtr_q", [DM, 512], BF16).ap()
    wtr_all_q = nc.dram_tensor("wtr_all_q", [NC * DM, 512], BF16).ap()
    wtr_o = nc.dram_tensor("wtr_o", [DM, 512], BF16).ap()
    wtr_all_o = nc.dram_tensor("wtr_all_o", [NC * DM, 512], BF16).ap()
    xn_dram = nc.dram_tensor("xn_dram", [TOK, DM], BF16).ap()
    q_dram = nc.dram_tensor("q_dram", [128, 2 * H * 512], BF16).ap()
    attn_dram = nc.dram_tensor("attn_dram", [TOK, DM], BF16).ap()
    ffn_dram = nc.dram_tensor("ffn_dram", [TOK, DM], F32).ap()
    kv_send = nc.dram_tensor("kv_send", [2 * HKV * HD, TOK], BF16).ap()
    kv_all = nc.dram_tensor("kv_all", [NC * 2 * HKV * HD, TOK], BF16).ap()

    tap_map = {
        "xn": (xn_dram, [TOK, DM], BF16),
        "q": (q_dram, [128, 2 * H * 512], BF16),
        "attn": (attn_dram, [TOK, DM], BF16),
        "ffn": (ffn_dram, [TOK, DM], F32),
        "kvall": (kv_all, [NC * 2 * HKV * HD, TOK], BF16),
    }
    tap_outs = []
    tap_sb = []
    for tname in taps:
        if tname == "hco":
            tap_sb.append(("hco", nc.dram_tensor("tap_hco", [128, 256], F32, kind="ExternalOutput").ap()))
            continue
        if tname == "sink":
            tap_sb.append(("sink", nc.dram_tensor("tap_sink", [128, 1152], F32, kind="ExternalOutput").ap()))
            continue
        tsrc, shp, dt = tap_map[tname]
        tap_outs.append((nc.dram_tensor("tap_" + tname, shp, dt, kind="ExternalOutput").ap(), tsrc))

    st = contextlib.ExitStack()
    with st:
        A = st.enter_context(nc.sbuf_tensor("arenaA", [128, 32768], BF16))
        B = st.enter_context(nc.sbuf_tensor("arenaB", [128, 24576], BF16))
        Q = st.enter_context(nc.sbuf_tensor("arenaQ", [128, 16384], BF16))
        C = st.enter_context(nc.sbuf_tensor("arenaC", [128, 16384], BF16))
        M = st.enter_context(nc.sbuf_tensor("arenaM", [128, 10240], BF16))

        names = ["s_ct", "s_x", "s_v", "s_a", "s_xs", "s_fm", "s_wck", "s_wcq",
                 "s_wco", "s_wb", "s_pe", "s_pp", "s_kvs", "s_cc", "s_kl",
                 "s_ql", "s_pl", "s_sc", "s_scf", "s_ex", "s_mk", "s_av",
                 "s_nf", "s_pb", "s_at", "s_ats", "s_ff", "s_y", "s_ld", "s_cm", "s_ws"]
        SM = {n: Sem(st, nc, n) for n in names}
        (s_ct, s_x, s_v, s_a, s_xs, s_fm, s_wck, s_wcq, s_wco, s_wb, s_pe,
         s_pp, s_kvs, s_cc, s_kl, s_ql, s_pl, s_sc, s_scf, s_ex, s_mk, s_av,
         s_nf, s_pb, s_at, s_ats, s_ff, s_y, s_ld, s_cm, s_ws) = (SM[n] for n in names)

        ring_x = Ring(st, nc, "rx", 2)
        ring_wt = Ring(st, nc, "rwt", 2)
        ring_kv = Ring(st, nc, "rkv", 2)
        ring_pr = Ring(st, nc, "rpr", 2)
        ring_at = Ring(st, nc, "rat", 4)
        ring_y = Ring(st, nc, "ry", 2)
        ring_ws = Ring(st, nc, "rws", 4)
        ring_wc = Ring(st, nc, "rwc", 2)
        ring_q = Ring(st, nc, "rq", 2)

        T, V, SC, GP, SY = nc.tensor, nc.vector, nc.scalar, nc.gpsimd, nc.sync

        def finish_debug():
            for sm in SM.values():
                GP.wait_ge(sm.h, sm.n)
            for rg in (ring_x, ring_wt, ring_kv, ring_pr, ring_at, ring_y, ring_ws,
                       ring_wc, ring_q):
                rg.wait_all(GP)
            for tp, tsrc in tap_outs:
                s_y.inc(GP.dma_start(tp, tsrc), 16)
            for kind, tp in tap_sb:
                if kind == "hco":
                    s_y.inc(GP.dma_start(tp, hcoef[:, 0:256]), 16)
                else:
                    s_y.inc(GP.dma_start(tp, sink[:, 0:1152]), 16)
            s_y.wait(GP, s_y.n)

        _mo = [0]
        def mslice(n_bf16, dt=BF16):
            o = _mo[0]
            _mo[0] += (n_bf16 + 15) & ~15
            assert _mo[0] <= 10240, f"M overflow {_mo[0]}"
            ap = M[:, o:o + n_bf16]
            return ap.bitcast(dt) if dt != BF16 else ap

        ident = mslice(128)
        onesf = mslice(256, F32)
        k2 = mslice(256, F32)
        onescol = mslice(16)
        qnw = mslice(16, F32)
        nrow = mslice(512, F32)
        shift_b = mslice(16, F32)
        stats = mslice(160, F32)
        hcoef = mslice(512, F32)
        gcol = mslice(16, F32)
        waT = mslice(768)
        sink = mslice(2304, F32)
        dencol = mslice(32, F32)
        zrow = mslice(288)

        ST_BN, ST_AG, ST_R2, ST_SCR = 0, 48, 50, 58
        HC_RES, HC_PRE, HC_POST, HC_BASE = 0, 128, 160, 192

        CONST_VALS = [LN_EPS, RMS_EPS, QK_EPS, -0.5, -1.0, 1.0 / HD, ISQD,
                      1.0, 0.0, DEN_GUARD]
        cvals = mslice(16 * ((2 * len(CONST_VALS) + 15) // 16), F32)
        for ci_, val_ in enumerate(CONST_VALS):
            nc.const_aps.register(F32, val_, cvals[:, ci_:ci_ + 1])

        hco = hcoef
        assert PLANE_MAX <= 2496, PLANE_MAX

        for ci_, val_ in enumerate(CONST_VALS):
            GP.memset(cvals[:, ci_:ci_ + 1], float(val_))
        GP.memset(onescol[:, 0:1], 1.0)
        GP.memset(sink[:, 0:1152], 0.0)
        GP.memset(hco[:, 0:256], 0.0)
        GP.memset(zrow[:, 0:288], 0.0)
        s_cm.inc(GP.memset(onescol[:, 1:2], 0.0), 1)
        s_cm.wait(SY, 1)
        s_ct.inc(SY.dma_start(ident, ident_in), 16)
        s_ct.inc(SY.dma_start(onesf[0:1, 0:128], onesf_in), 16)
        s_ct.inc(SY.dma_start(k2[0:2, 0:128], k2_in), 16)
        s_ct.inc(SY.dma_start(qnw[:, 0:1], qnw_in.rearrange("o (f u) -> (o f) u", u=1)), 16)
        s_ct.inc(SY.dma_start(qnw[:, 1:2], knw_in.rearrange("o (f u) -> (o f) u", u=1)), 16)
        s_ct.inc(SY.dma_start(nrow[0:1, 0:128], qnw_in), 16)
        s_ct.inc(SY.dma_start(nrow[0:1, 128:256], knw_in), 16)
        s_ct.inc(SY.dma_start(hco[0:1, HC_BASE:HC_BASE + 16], hresb_in), 16)
        s_ct.inc(SY.dma_start(hco[0:1, HC_BASE + 16:HC_BASE + 20], hpreb_in), 16)
        s_ct.inc(SY.dma_start(hco[0:1, HC_BASE + 20:HC_BASE + 24], hpostb_in), 16)
        s_ct.inc(SY.dma_start(gcol[0:1, 3:4], gate_in), 16)

        s_ct.wait(V)
        s_v.inc(V.reduce_max(gcol[0:1, 4:5], nrow[0:1, 0:128], axis=AX,
                             apply_absolute_value=True))
        s_v.inc(V.reduce_max(gcol[0:1, 5:6], nrow[0:1, 128:256], axis=AX,
                             apply_absolute_value=True))
        s_v.wait(V)
        s_v.inc(V.tensor_mul(gcol[0:1, 6:7], gcol[0:1, 4:5], gcol[0:1, 5:6]))
        s_v.wait(V)
        s_v.inc(V.tensor_scalar_mul(gcol[0:1, 7:8], gcol[0:1, 6:7],
                                    float(math.sqrt(128.0))))

        RT_ORDER = [4, 5, 0, 1, 2, 3, 6, 7, 8, 9]

        def wdst(rt):
            if rt == 4:
                return wtr_kv, 0
            if rt == 5:
                return wtr_kv, 128
            if rt < 4:
                return wtr_q, rt * 128
            return wtr_o, (rt - 6) * 128

        xn_fm = A[:, 0:32768]
        with contextlib.ExitStack() as pws:
            wtp = [pws.enter_context(nc.psum_tensor(f"wtp{i}", [128, 128], BF16))
                   for i in range(4)]
            xtp = [pws.enter_context(nc.psum_tensor(f"xtp{i}", [128, 128], BF16))
                   for i in range(4)]
            wrowb = [C[:, 0:4096], C[:, 4096:8192]]
            wslab = [B[:, 16384:20480], B[:, 20480:24576]]

            def p0w_gen():
                ev_w = {}
                tr_end = {}
                s_ct.wait(T)
                for i, rt in enumerate(RT_ORDER):
                    if i >= 2:
                        s_pe.wait(GP, tr_end[i - 2])
                    ring_wc.inc(GP.dma_start(wrowb[i % 2],
                                             wsl_in[rt * 128:(rt + 1) * 128, :]), i)
                    ring_wc.wait(T, i)
                    for fc in range(32):
                        u = i * 32 + fc
                        if u >= 4:
                            s_v.wait(T, ev_w[u - 4])
                        s_pe.inc(T.transpose(wtp[u % 4][:, 0:128],
                                             wrowb[i % 2][:, fc * 128:(fc + 1) * 128],
                                             ident))
                        s_pe.wait(V, s_pe.n)
                        if fc == 0 and i >= 2:
                            ring_ws.wait(V, i - 2)
                        s_v.inc(V.tensor_copy(wslab[i % 2][:, fc * 128:(fc + 1) * 128],
                                              wtp[u % 4][:, 0:128]))
                        ev_w[u] = s_v.n
                    tr_end[i] = s_pe.n
                    s_v.wait(SY, s_v.n)
                    dstt, c0 = wdst(rt)
                    ring_ws.inc(SY.dma_start(
                        dstt[:, c0:c0 + 128].rearrange("(f p) c -> p f c", p=128),
                        wslab[i % 2].rearrange("p (f c) -> p f c", c=128)), i)
                    if i == 1:
                        ring_ws.wait(GP, 0)
                        ring_ws.wait(GP, 1)
                        s_ws.inc(GP.collective_compute(
                            "AllGather", ALU.bypass,
                            replica_groups=[list(range(NC))],
                            ins=[wtr_kv.opt()], outs=[wtr_all_kv.opt()],
                        ))
                    if i == 5:
                        ring_ws.wait_all(GP)
                        s_ws.inc(GP.collective_compute(
                            "AllGather", ALU.bypass,
                            replica_groups=[list(range(NC))],
                            ins=[wtr_q.opt()], outs=[wtr_all_q.opt()],
                        ))
                    yield

            pgen = p0w_gen()

            s_ct.wait(SC)
            s_cm.wait(SC)
            s_cm.wait(V)
            s_ct.wait(T)
            s_cm.wait(T)
            s_ct.wait(GP)
            xbuf = [B[:, 0:8192].bitcast(F32), B[:, 8192:16384].bitcast(F32)]
            xnb = [Q[:, 0:4096], Q[:, 4096:8192]]
            dump = sink[:, 0:1024]
            ev_xn = {}
            tr_xn = {}
            for tt in range(NT):
                par = tt % 2
                xb = xbuf[par]
                if tt >= 2:
                    s_a.wait(SY, a_done_xn[tt - 2])
                ring_x.inc(SY.dma_start(xb, x_in[tt * 128:(tt + 1) * 128, :]), tt)
                ring_x.wait(V, tt)
                if tt >= 1:
                    s_a.wait(V)
                for ch in range(8):
                    s_v.inc(V.bn_stats(stats[:, ST_BN + ch * 6:ST_BN + (ch + 1) * 6],
                                       xb[:, ch * 512:(ch + 1) * 512]))
                s_v.wait(V)
                s_v.inc(V.bn_aggr(stats[:, ST_AG:ST_AG + 2], stats[:, ST_BN:ST_BN + 48]))
                s_v.wait(V)
                cm = stats[:, ST_AG + 0:ST_AG + 1]
                cvv = stats[:, ST_AG + 1:ST_AG + 2]
                crstd = stats[:, ST_SCR + 2:ST_SCR + 3]
                cbias = stats[:, ST_SCR + 3:ST_SCR + 4]
                s_v.wait(SC)
                s_a.inc(SC.activation(stats[:, ST_SCR + 4:ST_SCR + 5], cvv, AF.Ln, bias=LN_EPS))
                s_a.wait(SC)
                s_a.inc(SC.activation(crstd, stats[:, ST_SCR + 4:ST_SCR + 5], AF.Exp, scale=-0.5))
                s_a.wait(V)
                s_v.inc(V.scalar_tensor_tensor(cbias, cm, -1.0, crstd, ALU.mult, ALU.mult))
                s_v.inc(V.scalar_tensor_tensor(stats[:, ST_SCR + 5:ST_SCR + 6], crstd, 1.0, crstd,
                                               ALU.mult, ALU.mult))
                s_v.wait(V)
                s_v.inc(V.tensor_mul(stats[:, ST_SCR + 5:ST_SCR + 6], stats[:, ST_SCR + 5:ST_SCR + 6], cvv))
                s_v.wait(SC)
                s_a.inc(SC.activation(stats[:, ST_SCR + 6:ST_SCR + 7], stats[:, ST_SCR + 5:ST_SCR + 6],
                                      AF.Ln, bias=RMS_EPS))
                s_a.wait(SC)
                s_a.inc(SC.activation(stats[:, ST_R2 + tt:ST_R2 + tt + 1], stats[:, ST_SCR + 6:ST_SCR + 7],
                                      AF.Exp, scale=-0.5))
                s_v.wait(SC)
                s_xs.wait(SC, s_xs.n)
                if tt >= 2:
                    s_pe.wait(SC, tr_xn[tt - 2])
                xo = xnb[par]
                s_a.inc(SC.activation(xo, xb, AF.Identity, bias=cbias, scale=crstd))
                if tt == 0:
                    a_done_xn = {}
                a_done_xn[tt] = s_a.n
                s_a.wait(SC, s_a.n)
                s_xs.inc(SC.dma_start(xn_dram[tt * 128:(tt + 1) * 128, :], xo), 16)
                s_a.wait(T, a_done_xn[tt])
                for fc in range(32):
                    u = tt * 32 + fc
                    if u >= 4:
                        s_fm.wait(T, ev_xn[u - 4])
                    s_pe.inc(T.transpose(xtp[u % 4][:, 0:128],
                                         xo[:, fc * 128:(fc + 1) * 128], ident))
                    s_pe.wait(V, s_pe.n)
                    s_fm.inc(V.tensor_copy(
                        xn_fm[:, fc * 1024 + tt * 128: fc * 1024 + (tt + 1) * 128],
                        xtp[u % 4][:, 0:128]))
                    ev_xn[u] = s_fm.n
                tr_xn[tt] = s_pe.n
                next(pgen, None)
                next(pgen, None)
            for _ in pgen:
                pass
        p0_pe_end = s_pe.n

        if phase_limit == "P0":
            finish_debug()
            return nc
        if phase_limit == "P1":
            finish_debug()
            return nc
        wsm_f = B[:, 0:8192].bitcast(F32)
        wsm_b = B[:, 8192:16384]
        s_v.wait(SY, s_v.n)
        s_ct.inc(SY.dma_start(wsm_f[0:16, :], wres_in), 16)
        s_ct.inc(SY.dma_start(wsm_f[16:20, :], wpre_in), 16)
        s_ct.inc(SY.dma_start(wsm_f[20:24, :], wpost_in), 16)
        s_ct.wait(V)
        s_v.inc(V.tensor_copy(wsm_b[0:32, 0:4096], wsm_f[0:32, :]))

        Zf = sink
        U0, V0, SCR, R64, R32a, R32b, R32c = 128, 160, 192, 320, 384, 416, 448
        ZP, ZQ = 480, 512

        with contextlib.ExitStack() as pst:
            ps_a = pst.enter_context(nc.psum_tensor("ps_a", [128, 512], F32))
            ps_b = pst.enter_context(nc.psum_tensor("ps_b", [128, 512], F32))
            ps_tb = [pst.enter_context(nc.psum_tensor(f"ps_tb{i}", [128, 64], BF16))
                     for i in range(2)]
            psl = [ps_a, ps_b]

            s_v.wait(T)
            s_fm.wait(T)
            s_ct.wait(T)
            for fc in range(32):
                if fc >= 2:
                    s_pp.wait(T, s_pp.n - 1)
                s_pe.inc(T.transpose(ps_tb[fc % 2][:, 0:32],
                                     wsm_b[0:32, fc * 128:(fc + 1) * 128], ident[0:32, 0:32]))
                s_pe.wait(V, s_pe.n)
                s_pp.inc(V.tensor_copy(waT[:, fc * 24:(fc + 1) * 24],
                                       ps_tb[fc % 2][:, 0:24]))

            s_pp.wait(T, s_pp.n)
            s_pe.inc(T.matmul(ps_a[:, 0:24], onesf[0:1, 0:128],
                              hco[0:1, HC_BASE:HC_BASE + 24], start=True, stop=True))
            s_pe.inc(T.matmul(ps_b[:, 0:1], onesf[0:1, 0:128], gcol[0:1, 3:4],
                              start=True, stop=True))
            s_pe.wait(V, s_pe.n)
            s_v.inc(V.tensor_copy(hco[:, HC_BASE:HC_BASE + 24], ps_a[:, 0:24]))
            s_pe.wait(SC, s_pe.n)
            s_a.inc(SC.activation(gcol[:, 4:5], ps_b[:, 0:1], AF.Exp, scale=-1.0))
            s_a.wait(V)
            s_v.inc(V.tensor_scalar(gcol[:, 5:6], gcol[:, 4:5], 1.0, None, ALU.add))
            s_v.wait(V)
            s_v.inc(V.reciprocal(gcol[:, 0:1], gcol[:, 5:6]))
            s_v.wait(V)
            s_v.inc(V.tensor_scalar(gcol[:, 1:2], gcol[:, 0:1], -1.0, 1.0, ALU.mult, ALU.add))
            s_v.wait(V)

            s_fm.wait(T)
            s_v.wait(T, s_v.n)
            s_a.wait(T, s_a.n)
            logit_done = {}
            for tt in range(NT):
                ps = psl[tt % 2]
                if tt >= 2:
                    s_v.wait(T, logit_done[tt - 2])
                mm = None
                for fc in range(32):
                    mm = T.matmul(ps[:, 0:24],
                                  xn_fm[:, fc * 1024 + tt * 128: fc * 1024 + (tt + 1) * 128],
                                  waT[:, fc * 24:(fc + 1) * 24],
                                  start=(fc == 0), stop=(fc == 31))
                s_pe.inc(mm)
                s_pe.wait(V, s_pe.n)
                r2 = stats[:, ST_R2 + tt:ST_R2 + tt + 1]
                s_v.inc(V.scalar_tensor_tensor(Zf[:, tt * 16:(tt + 1) * 16], ps[:, 0:16],
                                               r2, hco[:, HC_BASE:HC_BASE + 16],
                                               ALU.mult, ALU.add))
                s_v.inc(V.scalar_tensor_tensor(Zf[:, ZP + tt * 4:ZP + (tt + 1) * 4], ps[:, 16:20],
                                               r2, hco[:, HC_BASE + 16:HC_BASE + 20],
                                               ALU.mult, ALU.add))
                s_v.inc(V.scalar_tensor_tensor(Zf[:, ZQ + tt * 4:ZQ + (tt + 1) * 4], ps[:, 20:24],
                                               r2, hco[:, HC_BASE + 20:HC_BASE + 24],
                                               ALU.mult, ALU.add))
                logit_done[tt] = s_v.n

            s_v.wait(V)
            if phase_limit == "P2a_logits":
                finish_debug()
                return nc
            s_v.inc(V.tensor_scalar_mul(Zf[:, 0:128], Zf[:, 0:128], 1.0 / TAU))
            s_v.inc(V.memset(Zf[:, U0:U0 + 32], 0.0))
            s_v.inc(V.memset(Zf[:, V0:V0 + 32], 0.0))
            s_v.wait(V)

            z4 = Zf[:, 0:128].rearrange("p (g i j) -> p g i j", g=8, i=4, j=4)
            z4T = Zf[:, 0:128].rearrange("p (g i j) -> p g j i", g=8, i=4, j=4)
            sc4 = Zf[:, SCR:SCR + 128].rearrange("p (g i j) -> p g i j", g=8, i=4, j=4)
            sc4T = Zf[:, SCR:SCR + 128].rearrange("p (g i j) -> p g j i", g=8, i=4, j=4)
            scf = Zf[:, SCR:SCR + 128]
            u3 = Zf[:, U0:U0 + 32].rearrange("p (g i) -> p g i", g=8)
            v3 = Zf[:, V0:V0 + 32].rearrange("p (g j) -> p g j", g=8)
            r64a = Zf[:, R64:R64 + 64]
            r32a = Zf[:, R32a:R32a + 32]
            r32b = Zf[:, R32b:R32b + 32]
            r32c = Zf[:, R32c:R32c + 32]

            def half_iter(out32, reduce_j):
                if reduce_j:
                    s_v.inc(V.tensor_tensor(sc4T, z4T, v3.broadcast_to([128, 8, 4, 4]), ALU.add))
                else:
                    s_v.inc(V.tensor_tensor(sc4, z4, u3.broadcast_to([128, 8, 4, 4]), ALU.add))
                s_v.wait(V)
                if reduce_j:
                    pv = scf.rearrange("p (x b) -> p x b", b=2)
                    s_v.inc(V.tensor_max(r64a, pv[:, :, 0], pv[:, :, 1]))
                    s_v.wait(V)
                    pv2 = r64a.rearrange("p (x b) -> p x b", b=2)
                    s_v.inc(V.tensor_max(r32a, pv2[:, :, 0], pv2[:, :, 1]))
                else:
                    g3a = Zf[:, R64:R64 + 32].rearrange("p (g j) -> p g j", g=8)
                    g3b = Zf[:, R64 + 32:R64 + 64].rearrange("p (g j) -> p g j", g=8)
                    s_v.inc(V.tensor_max(g3a, sc4[:, :, 0, :], sc4[:, :, 1, :]))
                    s_v.inc(V.tensor_max(g3b, sc4[:, :, 2, :], sc4[:, :, 3, :]))
                    s_v.wait(V)
                    s_v.inc(V.tensor_max(r32a, Zf[:, R64:R64 + 32], Zf[:, R64 + 32:R64 + 64]))
                s_v.wait(V)
                m3 = r32a.rearrange("p (g a) -> p g a", g=8)
                if reduce_j:
                    s_v.inc(V.tensor_tensor(sc4, sc4, m3.broadcast_to([128, 8, 4, 4]), ALU.subtract))
                else:
                    s_v.inc(V.tensor_tensor(sc4T, sc4T, m3.broadcast_to([128, 8, 4, 4]), ALU.subtract))
                s_v.wait(SC)
                s_a.inc(SC.activation(scf, scf, AF.Exp))
                s_a.wait(V)
                if reduce_j:
                    pv = scf.rearrange("p (x b) -> p x b", b=2)
                    s_v.inc(V.tensor_add(r64a, pv[:, :, 0], pv[:, :, 1]))
                    s_v.wait(V)
                    pv2 = r64a.rearrange("p (x b) -> p x b", b=2)
                    s_v.inc(V.tensor_add(r32b, pv2[:, :, 0], pv2[:, :, 1]))
                else:
                    g3a = Zf[:, R64:R64 + 32].rearrange("p (g j) -> p g j", g=8)
                    g3b = Zf[:, R64 + 32:R64 + 64].rearrange("p (g j) -> p g j", g=8)
                    s_v.inc(V.tensor_add(g3a, sc4[:, :, 0, :], sc4[:, :, 1, :]))
                    s_v.inc(V.tensor_add(g3b, sc4[:, :, 2, :], sc4[:, :, 3, :]))
                    s_v.wait(V)
                    s_v.inc(V.tensor_add(r32b, Zf[:, R64:R64 + 32], Zf[:, R64 + 32:R64 + 64]))
                s_v.wait(SC)
                s_a.inc(SC.activation(r32c, r32b, AF.Ln))
                s_a.wait(V)
                s_v.inc(V.tensor_add(r32c, r32c, r32a))
                s_v.wait(V)
                s_v.inc(V.tensor_scalar_mul(out32, r32c, -1.0))
                s_v.wait(V)

        def coef_gen():
            for _ in range(SINK_ITERS):
                half_iter(Zf[:, U0:U0 + 32], True)
                yield
                half_iter(Zf[:, V0:V0 + 32], False)
                yield

            s_v.inc(V.tensor_tensor(sc4, z4, u3.broadcast_to([128, 8, 4, 4]), ALU.add))
            s_v.wait(V)
            s_v.inc(V.tensor_tensor(sc4T, sc4T, v3.broadcast_to([128, 8, 4, 4]), ALU.add))
            s_v.wait(SC)
            s_a.inc(SC.activation(scf, scf, AF.Exp))
            s_a.wait(V)
            s_v.inc(V.tensor_scalar(hco[:, HC_RES:HC_RES + 128], scf, gcol[:, 0:1], None, ALU.mult))
            s_v.wait(V)
            hdiag = hco[:, HC_RES:HC_RES + 128].rearrange("p (g x) -> p g x", g=8)[:, :, 0:16:5]
            s_v.inc(V.tensor_scalar(hdiag, hdiag, gcol[:, 1:2], None, ALU.add))
            s_v.wait(V)
            yield

            def soft4(dst_off, src_off):
                sv = Zf[:, src_off:src_off + 32]
                p2 = sv.rearrange("p (x b) -> p x b", b=2)
                s_v.inc(V.tensor_max(Zf[:, R64:R64 + 16], p2[:, :, 0], p2[:, :, 1]))
                s_v.wait(V)
                p1 = Zf[:, R64:R64 + 16].rearrange("p (x b) -> p x b", b=2)
                s_v.inc(V.tensor_max(Zf[:, R32a:R32a + 8], p1[:, :, 0], p1[:, :, 1]))
                s_v.wait(V)
                s_v.inc(V.tensor_tensor(sv.rearrange("p (g j) -> p g j", g=8),
                                        sv.rearrange("p (g j) -> p g j", g=8),
                                        Zf[:, R32a:R32a + 8].broadcast_to([128, 8, 4]),
                                        ALU.subtract))
                s_v.wait(SC)
                s_a.inc(SC.activation(sv, sv, AF.Exp))
                s_a.wait(V)
                s_v.inc(V.tensor_add(Zf[:, R64:R64 + 16], p2[:, :, 0], p2[:, :, 1]))
                s_v.wait(V)
                s_v.inc(V.tensor_add(Zf[:, R32a:R32a + 8], p1[:, :, 0], p1[:, :, 1]))
                s_v.wait(V)
                s_v.inc(V.reciprocal(Zf[:, R32b:R32b + 8], Zf[:, R32a:R32a + 8]))
                s_v.wait(V)
                s_v.inc(V.tensor_tensor(hco[:, dst_off:dst_off + 32].rearrange("p (g j) -> p g j", g=8),
                                        sv.rearrange("p (g j) -> p g j", g=8),
                                        Zf[:, R32b:R32b + 8].broadcast_to([128, 8, 4]),
                                        ALU.mult))
                s_v.wait(V)

            soft4(HC_PRE, ZP)
            yield
            soft4(HC_POST, ZQ)
            s_v.inc(V.tensor_copy(hco[:, HC_BASE:HC_BASE + 1], hco[:, HC_BASE:HC_BASE + 1]))
            coef_done[0] = s_v.n
            yield

        coef_done = [None]
        cgen = coef_gen()

        if phase_limit == "P2a":
            for _ in cgen:
                pass
            finish_debug()
            return nc
        def proj(wbase, col_off, slice_w, fo_total, epilogue, gbase, bg=None,
                 bg_steps=0, ws_need=1, entry_guard=True):
            ngroups = fo_total // 256
            with contextlib.ExitStack() as ps_st:
                pp = [ps_st.enter_context(nc.psum_tensor(f"pp{gbase}_{i}", [128, 512], F32))
                      for i in range(2)]
                ps_ss = [ps_st.enter_context(nc.psum_tensor(f"pss{gbase}{i}", [1, 512], F32))
                         for i in range(1)]
                ps_iv = [ps_st.enter_context(nc.psum_tensor(f"piv{gbase}_{i}", [128, 512], F32))
                         for i in range(1)]
                wtb = [C[:, 8192:16384], C[:, 0:8192]]
                s_v.wait(T, s_v.n)
                s_a.wait(T, s_a.n)
                s_pp.wait(T, s_pp.n)
                if entry_guard:
                    s_pp.wait(SY, s_pp.n)
                    s_v.wait(SY, s_v.n)
                s_ws.wait(SY, ws_need)
                wt_done = {}
                unit_done = {}
                for g in range(ngroups):
                    fo0 = g * 256
                    if g == 1:
                        ring_wc.wait_all(SY)
                        s_pe.wait(SY, p0_pe_end)
                    if g >= 2:
                        sm_, ct_ = wt_done[g - 2]
                        SY.wait_ge(sm_.h, ct_)
                    for fc in range(32):
                        if slice_w == 512:
                            r_ = fo0 // 512
                            lo_ = fo0 % 512
                            ring_wt.inc(SY.dma_start(
                                wtb[g % 2][:, fc * 256:(fc + 1) * 256],
                                wbase[r_ * DM + fc * 128: r_ * DM + (fc + 1) * 128,
                                      col_off + lo_: col_off + lo_ + 256]), g)
                        else:
                            for hf in range(2):
                                r_ = (fo0 + hf * 128) // 128
                                ring_wt.inc(SY.dma_start(
                                    wtb[g % 2][:, fc * 256 + hf * 128: fc * 256 + (hf + 1) * 128],
                                    wbase[r_ * DM + fc * 128: r_ * DM + (fc + 1) * 128,
                                          col_off: col_off + 128]), g)
                    ring_wt.wait(T, g)
                    for blk in range(2):
                        for tch in range(2):
                            u = (g * 2 + blk) * 2 + tch
                            ps = pp[u % 2]
                            if u >= 2:
                                sm_, ct_ = unit_done[u - 2]
                                T.wait_ge(sm_.h, ct_)
                            mm = None
                            for fc in range(32):
                                mm = T.matmul(ps[:, 0:512],
                                              wtb[g % 2][:, fc * 256 + blk * 128: fc * 256 + (blk + 1) * 128],
                                              xn_fm[:, fc * 1024 + tch * 512: fc * 1024 + (tch + 1) * 512],
                                              start=(fc == 0), stop=(fc == 31))
                            s_pe.inc(mm)
                            ud = epilogue(g * 2 + blk, tch, ps, ps_ss[0], ps_iv[0])
                            unit_done[u] = ud
                            if bg is not None:
                                for _ in range(bg_steps):
                                    if next(bg, "DONE") == "DONE":
                                        bg = None
                                        break
                    wt_done[g] = ud
                flush = getattr(epilogue, "flush", None)
                if flush is not None:
                    flush()

        qf32 = [nrow[:, 0:0], nrow[:, 0:0]]
        qsc = Q[:, 0:4096].bitcast(F32)
        qsq = Q[:, 4096:5120]
        ivrow = Q[:, 5120:7168].bitcast(F32)
        kvdst = {"v": B[:, 8192:16384], "k": B[:, 0:8192]}

        def plain_epi(dst):
            def epi(blk, tch, ps, ps_ss, ps_iv):
                s_pe.wait(V, s_pe.n)
                s_pp.inc(V.tensor_copy(dst[:, blk * 1024 + tch * 512: blk * 1024 + (tch + 1) * 512],
                                       ps[:, 0:512]))
                return (s_pp, s_pp.n)
            return epi

        def norm_epi(dst, wcol, to_dram):
            st_ = {"recs": {}, "u": 0, "ln": s_a.n, "stt": s_v.n,
                   "ssmm": {}, "ivmm": {}, "ps_ss": None, "ps_iv": None}

            def stage_b(r):
                ps_ss = st_["ps_ss"]
                u = r["u"]
                s_v.wait(T, r["mul"])
                s_a.wait(T, st_["ln"])
                s_pe.inc(T.matmul(ps_ss[0:1, 0:512], onescol[:, 0:1],
                                  qsq[:, (u % 2) * 512:(u % 2) * 512 + 512],
                                  start=True, stop=True))
                st_["ssmm"][u] = s_pe.n
                ivr = ivrow[0:1, (u % 2) * 512:(u % 2) * 512 + 512]
                s_pe.wait(SC, s_pe.n)
                if u >= 2:
                    s_pe.wait(SC, st_["ivmm"][u - 2])
                s_a.inc(SC.activation(ivr, ps_ss[0:1, 0:512], AF.Ln,
                                      bias=QK_EPS, scale=1.0 / HD))
                s_a.wait(SC)
                s_a.inc(SC.activation(ivr, ivr, AF.Exp, scale=-0.5))
                r["exp"] = s_a.n
                st_["ln"] = s_a.n

            def stage_c(r):
                ps_iv = st_["ps_iv"]
                u = r["u"]
                ivr = ivrow[0:1, (u % 2) * 512:(u % 2) * 512 + 512]
                s_a.wait(T, r["exp"])
                s_v.wait(T, st_["stt"])
                s_pe.inc(T.matmul(ps_iv[:, 0:512], onesf[0:1, 0:128], ivr,
                                  start=True, stop=True))
                st_["ivmm"][u] = s_pe.n
                s_pe.wait(V, s_pe.n)
                if to_dram is not None and u >= 2:
                    ring_q.wait(V, u - 2)
                qf = qsc[:, (u % 4) * 512:(u % 4) * 512 + 512]
                d = dst(r["blk"], r["tch"], u)
                s_v.inc(V.scalar_tensor_tensor(d, qf, qnw[:, wcol:wcol + 1],
                                               ps_iv[:, 0:512], ALU.mult, ALU.mult))
                st_["stt"] = s_v.n
                if to_dram is not None:
                    s_v.wait(SC, s_v.n)
                    ring_q.inc(SC.dma_start(to_dram(r["blk"], r["tch"]), d), u)

            def epi(blk, tch, ps, ps_ss, ps_iv):
                st_["ps_ss"], st_["ps_iv"] = ps_ss, ps_iv
                u = st_["u"]
                st_["u"] = u + 1
                s_pe.wait(V, s_pe.n)
                qf = qsc[:, (u % 4) * 512:(u % 4) * 512 + 512]
                s_v.inc(V.tensor_copy(qf, ps[:, 0:512]))
                ud = (s_v, s_v.n)
                if u >= 2:
                    s_pe.wait(V, st_["ssmm"][u - 2])
                s_v.wait(V)
                s_v.inc(V.tensor_mul(qsq[:, (u % 2) * 512:(u % 2) * 512 + 512], qf, qf))
                st_["recs"][u] = {"u": u, "blk": blk, "tch": tch, "mul": s_v.n}
                if u >= 1:
                    stage_b(st_["recs"][u - 1])
                if u >= 2:
                    stage_c(st_["recs"].pop(u - 2))
                return ud

            def flush():
                n = st_["u"]
                if n >= 1:
                    stage_b(st_["recs"][n - 1])
                for uu in (n - 2, n - 1):
                    if uu >= 0 and uu in st_["recs"]:
                        stage_c(st_["recs"].pop(uu))
            epi.flush = flush
            return epi

        s_fm.wait(T)
        proj(wtr_all_kv, 128, 128, HKV * HD, plain_epi(kvdst["v"]), "v", bg=cgen,
             bg_steps=4, entry_guard=False)
        s_xs.wait(V)
        s_xs.wait(SC)
        kslice = lambda blk, tch, u: kvdst["k"][:, blk * 1024 + tch * 512: blk * 1024 + (tch + 1) * 512]
        proj(wtr_all_kv, 0, 128, HKV * HD, norm_epi(kslice, 1, None), "k", bg=cgen, bg_steps=4)
        for _ in cgen:
            pass

        vtm = B[:, 16384:24576]
        ring_ws.wait_all(V)
        with contextlib.ExitStack() as pvs:
            vtp = [pvs.enter_context(nc.psum_tensor(f"vtp{i}", [128, 128], BF16))
                   for i in range(2)]
            s_v.wait(T, s_v.n)
            s_a.wait(T, s_a.n)
            vt_done = {}
            for kv in range(HKV):
                for j in range(NT):
                    u = kv * NT + j
                    if u >= 2:
                        s_v.wait(T, vt_done[u - 2])
                    s_pe.inc(T.transpose(vtp[u % 2][:, 0:128],
                                         kvdst["v"][:, kv * 1024 + j * 128: kv * 1024 + (j + 1) * 128],
                                         ident))
                    s_pe.wait(V, s_pe.n)
                    s_v.inc(V.tensor_copy(vtm[:, (j * 8 + kv) * 128:(j * 8 + kv + 1) * 128],
                                          vtp[u % 2][:, 0:128]))
                    vt_done[u] = s_v.n
        s_pp.wait(SC, s_pp.n)
        s_v.wait(SC, s_v.n)
        for kv in range(HKV):
            s_kvs.inc(SC.dma_start(kv_send[kv * 128:(kv + 1) * 128, :],
                                   kvdst["k"][:, kv * 1024:(kv + 1) * 1024]), 16)
        for j in range(NT):
            s_kvs.inc(SC.dma_start(kv_send[1024 + j * 128:1024 + (j + 1) * 128, :],
                                   vtm[:, j * 1024:(j + 1) * 1024]), 16)
        s_kvs.wait(GP, s_kvs.n)
        s_cc.inc(GP.collective_compute(
            "AllGather", ALU.bypass,
            replica_groups=[list(range(NC))],
            ins=[kv_send.opt()], outs=[kv_all.opt()],
        ))

        s_fm.wait(GP)
        s_xs.wait(GP)
        for ci in range(8):
            s_wb.inc(GP.dma_start(Q[:, 7680 + ci * 1024:7680 + (ci + 1) * 1024],
                                  wb_in[ci * 128:(ci + 1) * 128, :]), 16)
        wbsb = Q[:, 7680:15872]
        xnt = [B[:, 0:4096], B[:, 4096:8192]]
        xpf = B[:, 8192:12288].bitcast(F32)
        xpre_b = B[:, 12288:13312]
        xprefm = B[:, 13312:14336]
        yv = xpf[:, 1024:2048]
        fo_f = B[:, 14336:22528].bitcast(F32)

        def mix_gen(pf, pfb):
            s_kvs.wait(SY, s_kvs.n)
            v_used_xnt = {}
            for tt in range(NT):
                par = tt % 2
                if tt >= 2:
                    s_v.wait(SY, v_used_xnt[tt - 2])
                ring_x.inc(SY.dma_start(xnt[par], xn_dram[tt * 128:(tt + 1) * 128, :]), tt)
                ring_x.wait(V, tt)
                if tt >= 1:
                    s_ff.wait(V, s_ff.n)
                hpre = lambda n: hco[:, HC_PRE + tt * 4 + n: HC_PRE + tt * 4 + n + 1]
                hpost = lambda n: hco[:, HC_POST + tt * 4 + n: HC_POST + tt * 4 + n + 1]
                hres = lambda n, jj: hco[:, HC_RES + tt * 16 + n * 4 + jj: HC_RES + tt * 16 + n * 4 + jj + 1]
                s_v.inc(V.tensor_scalar(xpf[:, 0:1024], xnt[par][:, 0:1024], hpre(0), None, ALU.mult))
                for n in range(1, 4):
                    s_v.wait(V)
                    s_v.inc(V.scalar_tensor_tensor(xpf[:, 0:1024], xnt[par][:, n * 1024:(n + 1) * 1024],
                                                   hpre(n), xpf[:, 0:1024], ALU.mult, ALU.add))
                s_v.wait(V)
                s_v.inc(V.tensor_copy(xpre_b, xpf[:, 0:1024]))
                s_v.wait(T)
                yield
                for cc in range(8):
                    s_pe.inc(T.transpose(pfb[cc % 2][:, 0:128],
                                         xpre_b[:, cc * 128:(cc + 1) * 128], ident))
                    s_pe.wait(V, s_pe.n)
                    s_v.inc(V.tensor_copy(xprefm[:, cc * 128:(cc + 1) * 128],
                                          pfb[cc % 2][:, 0:128]))
                    if cc >= 1:
                        s_v.wait(T, s_v.n - 1)
                s_v.wait(T, s_v.n)
                s_wb.wait(T, s_wb.n)
                yield
                for co in range(2):
                    ps = pf[co % 2]
                    mm = None
                    for cc in range(8):
                        mm = T.matmul(ps[:, 0:512], xprefm[:, cc * 128:(cc + 1) * 128],
                                      wbsb[:, cc * 1024 + co * 512: cc * 1024 + (co + 1) * 512],
                                      start=(cc == 0), stop=(cc == 7))
                    s_pe.inc(mm)
                    s_pe.wait(V, s_pe.n)
                    s_v.inc(V.tensor_copy(yv[:, co * 512:(co + 1) * 512], ps[:, 0:512]))
                s_v.wait(V)
                yield
                for n in range(4):
                    fon = fo_f[:, n * 1024:(n + 1) * 1024]
                    s_v.inc(V.tensor_scalar(fon, xnt[par][:, 0:1024], hres(n, 0), None, ALU.mult))
                    for jj in range(1, 4):
                        s_v.wait(V)
                        s_v.inc(V.scalar_tensor_tensor(fon, xnt[par][:, jj * 1024:(jj + 1) * 1024],
                                                       hres(n, jj), fon, ALU.mult, ALU.add))
                    s_v.wait(V)
                    s_v.inc(V.scalar_tensor_tensor(fon, yv, hpost(n), fon, ALU.mult, ALU.add))
                    s_v.wait(V)
                v_used_xnt[tt] = s_v.n
                s_v.wait(GP, s_v.n)
                s_ff.inc(GP.dma_start(ffn_dram[tt * 128:(tt + 1) * 128, :], fo_f), 16)
                yield

        qtmp = [Q[:, 7168:7680], Q[:, 15872:16384]]
        qsl = lambda blk, tch, u: qtmp[u % 2][:, 0:512]
        qdr = lambda blk, tch: q_dram[:, (tch * H + blk) * 512:(tch * H + blk + 1) * 512]
        with contextlib.ExitStack() as ps_mx:
            pf_ = [ps_mx.enter_context(nc.psum_tensor(f"pf{i}", [128, 512], F32)) for i in range(2)]
            pfb_ = [ps_mx.enter_context(nc.psum_tensor(f"pfb{i}", [128, 128], BF16)) for i in range(2)]
            mgen = mix_gen(pf_, pfb_)
            proj(wtr_all_q, 0, 512, DM, norm_epi(qsl, 0, qdr), "q", bg=mgen,
                 bg_steps=2, ws_need=2)
            for _ in mgen:
                pass

        if phase_limit == "P3":
            finish_debug()
            return nc

        if phase_limit == "P2b":
            finish_debug()
            return nc
        VOFF = [0, 9216]
        POFF = 18432
        ATOFF = 21504
        PLOFF = [22016, 23680]
        masksb = B[:, 16384:24576]
        qslab = Q[:, 0:16384]

        with contextlib.ExitStack() as ps_st:
            ps_sc = [ps_st.enter_context(nc.psum_tensor(f"psc{i}", [128, 512], F32)) for i in range(5)]
            ps_av = [ps_st.enter_context(nc.psum_tensor(f"pav{i}", [128, 512], F32)) for i in range(2)]
            ps_pl = ps_st.enter_context(nc.psum_tensor("pspl", [128, 512], F32))

            s_pe.wait(SY, s_pe.n)
            s_ff.wait(SY, s_ff.n)
            s_v.wait(SY, s_v.n)
            s_kvs.wait(SY, s_kvs.n)
            s_ct.inc(SY.dma_start(B[:, 16384:24576], mask_in), 16)
            ring_ws.wait_all(GP)
            s_ws.inc(GP.collective_compute(
                "AllGather", ALU.bypass,
                replica_groups=[list(range(NC))],
                ins=[wtr_o.opt()], outs=[wtr_all_o.opt()],
            ))
            s_ct.wait(GP, s_ct.n)

            s_pe.wait(GP, s_pe.n)
            s_v.wait(GP, s_v.n)
            for par_ in range(2):
                vv = A[:, VOFF[par_]:VOFF[par_] + 9216].rearrange(
                    "p (t c) -> p t c", c=144)[:, :, 128:129]
                s_cm.inc(GP.memset(vv, 1.0), 1)
            s_cm.wait(T)

            score_cnt_end = {}
            av_cnt_end = {}
            ex_cnt_end = {}
            KEYSEQ = [(qg_, kv_) for qg_ in range(2) for kv_ in range(HKV)]

            def load_kv(qg, kv):
                seq = KEYSEQ.index((qg, kv))
                par = seq % 2
                kb = C[:, par * 8192:(par + 1) * 8192]
                s_cc.wait(SY, 1)
                if seq >= 2:
                    key = KEYSEQ[seq - 2]
                    s_sc.wait(SY, score_cnt_end[key])
                    s_av.wait(SY, av_cnt_end[key])
                jlo = 0 if qg == 0 else 4
                nj = NT - jlo
                for r in range(NC):
                    ring_kv.inc(SY.dma_start(kb[:, r * 1024:(r + 1) * 1024],
                                             kv_all[r * 2048 + kv * 128: r * 2048 + (kv + 1) * 128, :]), seq)
                    vsrc = kv_all[r * 2048 + 1024 + jlo * 128: r * 2048 + 2048,
                                  kv * 128:(kv + 1) * 128].rearrange(
                        "(j p) h -> p j h", p=128)
                    vdst = A[:, VOFF[par] + (r * 8 + jlo) * 144:
                             VOFF[par] + (r * 8 + 8) * 144].rearrange(
                        "p (j c) -> p j c", c=144)[:, :, 0:128]
                    ring_kv.inc(SY.dma_start(vdst, vsrc), seq)
                return seq

            def build_plane(qg, kv):
                seq = KEYSEQ.index((qg, kv))
                par = seq % 2
                ncols = PLANE_COUNTS[(qg, kv)]
                off = PLANE_OFF[(qg, kv)]
                if seq >= 2:
                    key = KEYSEQ[seq - 2]
                    s_ex.wait(SY, ex_cnt_end[key])
                prb = prhs_bufs[seq % 2]
                ring_pr.inc(SY.dma_start(prb[0:2, 0:ncols], prhs_in[:, off:off + ncols]), seq)
                ring_pr.wait(T, seq)
                pl = A[:, PLOFF[par]:PLOFF[par] + 1664].bitcast(F32)
                nch = (ncols + 511) // 512
                for chk in range(nch):
                    c0 = chk * 512
                    n = min(512, ncols - c0)
                    s_pl.wait(T, s_pl.n)
                    s_pe.inc(T.matmul(ps_pl[:, 0:n], k2[0:2, 0:128], prb[0:2, c0:c0 + n],
                                      start=True, stop=True))
                    s_pe.wait(V, s_pe.n)
                    s_pl.inc(V.tensor_scalar(pl[:, c0:c0 + n], ps_pl[:, 0:n],
                                             shift_b[:, 0:1], None, ALU.subtract))
                return s_pl.n

            s_v.wait(T, s_v.n)
            s_a.wait(T, s_a.n)
            s_pe.inc(T.matmul(ps_pl[:, 0:1], onesf[0:1, 0:128], gcol[0:1, 7:8],
                              start=True, stop=True))
            s_pe.wait(V, s_pe.n)
            s_pl.inc(V.tensor_copy(shift_b[:, 0:1], ps_pl[:, 0:1]))

            prhs_bufs = [B[:, 0:4992].bitcast(F32), B[:, 4992:9984].bitcast(F32)]
            pb_cnt = [0]
            at_cnt = [0]
            ex_tile_cnt = {}
            pfree = {}
            pending = {}

            pending[(0, 0)] = (load_kv(0, 0), build_plane(0, 0))
            pending[(0, 1)] = (load_kv(0, 1), build_plane(0, 1))

            for qg in range(2):
                ring_q.wait_all(SY)
                s_sc.wait(SY, s_sc.n)
                s_ld.inc(SY.dma_start(qslab, q_dram[:, qg * 16384:(qg + 1) * 16384]), 16)
                qload = s_ld.n
                tmax = T_SLOT[4 * qg]
                for kv in range(HKV):
                    klcnt, plcnt = pending.pop((qg, kv))
                    pcol = [0]
                    par = kv % 2
                    kb = C[:, par * 8192:(par + 1) * 8192]
                    pl = A[:, PLOFF[par]:PLOFF[par] + 1664].bitcast(F32)
                    for hg in range(4):
                        h = 4 * kv + hg
                        nf_before = s_nf.n
                        s_nf.wait(T, nf_before)
                        for bk in range(2):
                            T.matmul(ps_av[bk][:, 0:265], ident[0:1, 0:128],
                                     zrow[0:1, 0:265], start=True, stop=False)
                        kept = [t_ for t_ in range(tmax) if active_slots(h, qg, t_)]
                        PIPE = 5
                        sc_cnt = {}

                        def emit_score(idx):
                            t_ = kept[idx]
                            act_ = active_slots(h, qg, t_)
                            jr0_ = act_[0] - 4 * qg
                            nc_ = len(act_) * 128
                            ab = pb_cnt[0] + idx
                            ring_kv.wait(T, klcnt)
                            s_ld.wait(T, qload)
                            if ab >= 5:
                                s_ex.wait(T, ex_tile_cnt[ab - 5])
                            mm_ = T.matmul(ps_sc[ab % 5][:, jr0_ * 128: jr0_ * 128 + nc_],
                                           kb[:, KCOL[t_]:KCOL[t_] + 128],
                                           qslab[:, h * 512 + jr0_ * 128: h * 512 + jr0_ * 128 + nc_],
                                           start=True, stop=True)
                            s_sc.inc(mm_)
                            sc_cnt[ab] = s_sc.n

                        for i0 in range(min(PIPE, len(kept))):
                            emit_score(i0)
                        for idx, t in enumerate(kept):
                            act = active_slots(h, qg, t)
                            ab = pb_cnt[0] + idx
                            scpar = ab % 5
                            pbuf = A[:, POFF + (ab % 6) * 512: POFF + (ab % 6 + 1) * 512]
                            s_sc.wait(SC, sc_cnt[ab])
                            s_pl.wait(SC, plcnt)
                            if ab >= 6:
                                knd, cnt_ = pfree[ab - 6]
                                (s_av if knd == 'av' else s_pb).wait(SC, cnt_)
                            w = CHUNK_W[h]
                            nch = 128 // w
                            last_act = None
                            for j in act:
                                jr = j - 4 * qg
                                for ci in range(nch):
                                    last_act = SC.activation(
                                        pbuf[:, jr * 128 + ci * w: jr * 128 + (ci + 1) * w],
                                        ps_sc[scpar][:, jr * 128 + ci * w: jr * 128 + (ci + 1) * w],
                                        AF.Exp, bias=pl[:, pcol[0]:pcol[0] + 1], scale=ISQD)
                                    pcol[0] += 1
                            s_ex.inc(last_act)
                            ex_tile_cnt[ab] = s_ex.n
                            masked = [j for j in act if t >= T_SLOT[j] - 8]
                            for j in masked:
                                jr = j - 4 * qg
                                s_ex.wait(GP, s_ex.n)
                                s_mk.inc(GP.tensor_tensor(
                                    pbuf[:, jr * 128:(jr + 1) * 128],
                                    pbuf[:, jr * 128:(jr + 1) * 128],
                                    masksb[:, (j * 8 + (t - (T_SLOT[j] - 8))) * 128:
                                           (j * 8 + (t - (T_SLOT[j] - 8)) + 1) * 128],
                                    ALU.mult))
                            s_ex.wait(T, s_ex.n)
                            if masked:
                                s_mk.wait(T, s_mk.n)
                            bank_close = False
                            lastav = None
                            for j in act:
                                jr = j - 4 * qg
                                bk, half = jr // 2, jr % 2
                                closes = (jr in (0, 2) and t == T_SLOT[j] - 1)
                                pos_t = OWNER[t] * 8 + SLOT[t]
                                av = T.matmul(ps_av[bk][:, half * 136: half * 136 + 129],
                                              pbuf[:, jr * 128:(jr + 1) * 128],
                                              A[:, VOFF[par] + pos_t * 144: VOFF[par] + pos_t * 144 + 129],
                                              start=False, stop=closes)
                                if closes:
                                    s_av.inc(av)
                                    bank_close = True
                                lastav = av
                            if bank_close:
                                pfree[ab] = ('av', s_av.n)
                            else:
                                s_pb.inc(lastav)
                                pfree[ab] = ('pb', s_pb.n)
                            if idx + PIPE < len(kept):
                                emit_score(idx + PIPE)
                            for j in act:
                                jr = j - 4 * qg
                                if not (jr in (0, 2) and t == T_SLOT[j] - 1):
                                    continue
                                bk = jr // 2
                                s_av.wait(V, s_av.n)
                                for half in range(2):
                                    jn = 4 * qg + 2 * bk + half
                                    atb = A[:, ATOFF + (at_cnt[0] % 4) * 128: ATOFF + (at_cnt[0] % 4 + 1) * 128]
                                    ring_at.wait(V, at_cnt[0])
                                    dc = dencol[:, (at_cnt[0] % 4) * 2:(at_cnt[0] % 4) * 2 + 2]
                                    s_v.inc(V.tensor_scalar(dc[:, 0:1],
                                                            ps_av[bk][:, half * 136 + 128: half * 136 + 129],
                                                            DEN_GUARD, None, ALU.add))
                                    s_v.wait(V)
                                    s_v.inc(V.reciprocal(dc[:, 1:2], dc[:, 0:1]))
                                    s_v.wait(V)
                                    nrm = V.tensor_scalar(atb, ps_av[bk][:, half * 136: half * 136 + 128],
                                                          dc[:, 1:2], None, ALU.mult)
                                    s_nf.inc(nrm)
                                    s_nf.wait(SY, s_nf.n)
                                    ring_at.inc(SY.dma_start(
                                        attn_dram[jn * 128:(jn + 1) * 128, h * 128:(h + 1) * 128], atb), at_cnt[0])
                                    at_cnt[0] += 1
                        pb_cnt[0] += len(kept)
                    score_cnt_end[(qg, kv)] = s_sc.n
                    av_cnt_end[(qg, kv)] = s_av.n
                    ex_cnt_end[(qg, kv)] = s_ex.n
                    nseq = KEYSEQ.index((qg, kv)) + 2
                    if nseq < len(KEYSEQ):
                        nk = KEYSEQ[nseq]
                        pending[nk] = (load_kv(*nk), build_plane(*nk))

        if phase_limit == "P4":
            finish_debug()
            return nc
        attn_fm = A[:, 0:32768]
        xres = Q[:, 0:4096].bitcast(F32)
        outt = Q[:, 4096:8192].bitcast(F32)
        s_ff.wait(SC, s_ff.n)
        with contextlib.ExitStack() as ps_st:
            pw = [ps_st.enter_context(nc.psum_tensor(f"pw{i}", [128, 256], F32)) for i in range(2)]
            ftp = [ps_st.enter_context(nc.psum_tensor(f"ftp{i}", [128, 128], BF16))
                   for i in range(4)]
            s_v.wait(T, s_v.n)
            s_a.wait(T, s_a.n)

            def wo_load(g):
                par = g % 2
                wob = C[:, par * 8192:(par + 1) * 8192]
                r_ = (g * 256) // 512
                lo_ = (g * 256) % 512
                for hc in range(32):
                    ring_wt.inc(SY.dma_start(
                        wob[:, hc * 256:(hc + 1) * 256],
                        wtr_all_o[r_ * DM + hc * 128: r_ * DM + (hc + 1) * 128,
                                  lo_: lo_ + 256]), g)
                return wob

            s_ws.wait(SY, 3)
            s_sc.wait(SY, s_sc.n)
            wob_pre = {0: wo_load(0), 1: wo_load(1)}

            ring_at.wait_all(SY)
            s_pe.wait(SY, s_pe.n)
            arow = [B[:, 0:4096], B[:, 4096:8192]]
            evf = {}
            trf = {}
            for j in range(NT):
                if j >= 2:
                    s_pe.wait(SY, trf[j - 2])
                ring_kv.inc(SY.dma_start(arow[j % 2], attn_dram[j * 128:(j + 1) * 128, :]),
                            16 + j)
                ring_kv.wait(T, 16 + j)
                for hc in range(32):
                    u = j * 32 + hc
                    if u >= 4:
                        s_fm.wait(T, evf[u - 4])
                    s_pe.inc(T.transpose(ftp[u % 4][:, 0:128],
                                         arow[j % 2][:, hc * 128:(hc + 1) * 128], ident))
                    s_pe.wait(V, s_pe.n)
                    if u == 0:
                        s_av.wait(V, s_av.n)
                        s_mk.wait(V, s_mk.n)
                        ring_at.wait_all(V)
                    s_fm.inc(V.tensor_copy(
                        attn_fm[:, hc * 1024 + j * 128: hc * 1024 + (j + 1) * 128],
                        ftp[u % 4][:, 0:128]))
                    evf[u] = s_fm.n
                trf[j] = s_pe.n
            fm_all = s_fm.n

            wo_done = {}
            p5_done = {}
            for g in range(16):
                wob = wob_pre.pop(g, None)
                if wob is None:
                    s_pe.wait(SY, wo_done[g - 2])
                    wob = wo_load(g)
                ring_wt.wait(T, g)
                s_fm.wait(T, fm_all)
                for tt in range(NT):
                    un = (g * 8 + tt)
                    ps = pw[un % 2]
                    if un >= 2:
                        s_v.wait(T, p5_done[un - 2])
                    ring_y.wait(SC, un)
                    ring_x.inc(SC.dma_start(xres[:, (un % 2) * 1024: (un % 2) * 1024 + 256],
                                            x_in[tt * 128:(tt + 1) * 128, g * 256:(g + 1) * 256]), un)
                    ring_x.inc(SC.dma_start(xres[:, (un % 2) * 1024 + 256: (un % 2) * 1024 + 512],
                                            ffn_dram[tt * 128:(tt + 1) * 128, g * 256:(g + 1) * 256]), un)
                    mm = None
                    for hc in range(32):
                        mm = T.matmul(ps[:, 0:256],
                                      attn_fm[:, hc * 1024 + tt * 128: hc * 1024 + (tt + 1) * 128],
                                      wob[:, hc * 256:(hc + 1) * 256],
                                      start=(hc == 0), stop=(hc == 31))
                    s_pe.inc(mm)
                    s_pe.wait(V, s_pe.n)
                    ring_x.wait(V, un)
                    ot = outt[:, (un % 2) * 512:(un % 2) * 512 + 256]
                    s_v.inc(V.tensor_add(ot, ps[:, 0:256], xres[:, (un % 2) * 1024: (un % 2) * 1024 + 256]))
                    s_v.wait(V)
                    s_v.inc(V.tensor_add(ot, ot, xres[:, (un % 2) * 1024 + 256: (un % 2) * 1024 + 512]))
                    p5_done[un] = s_v.n
                    s_v.wait(GP, s_v.n)
                    ring_y.inc(GP.dma_start(y_out[tt * 128:(tt + 1) * 128, g * 256:(g + 1) * 256], ot), un)
                wo_done[g] = s_pe.n

            for tp, src in tap_outs:
                s_y.wait(GP, s_y.n)
                s_y.inc(GP.dma_start(tp, src), 16)
            s_y.wait(GP, s_y.n)

    return nc



_CONST = {}

def _consts():
    if not _CONST:
        _CONST["ident"] = np.eye(128, dtype=np.float32).astype(ml_dtypes.bfloat16)
        _CONST["ones"] = np.ones((1, 128), np.float32)
        _CONST["k2"] = np.stack([np.arange(128, dtype=np.float32),
                                 np.ones(128, np.float32)])
        _CONST["planes"] = [build_plane_rhs(c) for c in range(NC)]
        _CONST["masks"] = [build_masks(c) for c in range(NC)]
    return _CONST


def run(inputs, taps=(), phase_limit=None):
    cst = _consts()
    x = np.asarray(inputs["L0_attn_x"])
    xr = np.ascontiguousarray(x.reshape(NB, 128, DM))
    wq_np = np.asarray(inputs["L0_attn_Wq"])
    wk_np = np.asarray(inputs["L0_attn_Wk"])
    wv_np = np.asarray(inputs["L0_attn_Wv"])
    wo_np = np.asarray(inputs["L0_attn_Wo"])
    in_maps = []
    for c in range(NC):
        blocks = blocks_of_core(c)
        x_own = np.ascontiguousarray(xr[blocks].reshape(TOK, DM))
        w_slice = np.ascontiguousarray(np.concatenate([
            wq_np[512 * c:512 * (c + 1)], wk_np[128 * c:128 * (c + 1)],
            wv_np[128 * c:128 * (c + 1)], wo_np[512 * c:512 * (c + 1)]], axis=0))
        m = {
            "x_own": x_own,
            "w_slice": w_slice,
            "L0_attn_Wq": np.ascontiguousarray(inputs["L0_attn_Wq"]),
            "L0_attn_Wk": np.ascontiguousarray(inputs["L0_attn_Wk"]),
            "L0_attn_Wv": np.ascontiguousarray(inputs["L0_attn_Wv"]),
            "L0_attn_Wo": np.ascontiguousarray(inputs["L0_attn_Wo"]),
            "L0_attn_q_norm_w": np.ascontiguousarray(np.asarray(inputs["L0_attn_q_norm_w"]).reshape(1, HD)),
            "L0_attn_k_norm_w": np.ascontiguousarray(np.asarray(inputs["L0_attn_k_norm_w"]).reshape(1, HD)),
            "L0_ffn_W_res": np.ascontiguousarray(inputs["L0_ffn_W_res"]),
            "L0_ffn_W_pre": np.ascontiguousarray(inputs["L0_ffn_W_pre"]),
            "L0_ffn_W_post": np.ascontiguousarray(inputs["L0_ffn_W_post"]),
            "L0_ffn_H_res_base": np.ascontiguousarray(np.asarray(inputs["L0_ffn_H_res_base"]).reshape(1, 16)),
            "L0_ffn_H_pre_base": np.ascontiguousarray(np.asarray(inputs["L0_ffn_H_pre_base"]).reshape(1, 4)),
            "L0_ffn_H_post_base": np.ascontiguousarray(np.asarray(inputs["L0_ffn_H_post_base"]).reshape(1, 4)),
            "L0_ffn_gate": np.ascontiguousarray(np.asarray(inputs["L0_ffn_gate"]).reshape(1, 1)),
            "L0_ffn_W_branch": np.ascontiguousarray(inputs["L0_ffn_W_branch"]),
            "c_ident": cst["ident"],
            "c_ones_f32": cst["ones"],
            "c_k2": cst["k2"],
            "c_plane_rhs": cst["planes"][c],
            "c_masks": cst["masks"][c],
        }
        in_maps.append(m)

    nc_prog = build_program(taps=taps, phase_limit=phase_limit)
    res = run_bass_kernel_spmd(nc_prog, in_maps, core_ids=list(range(NC)))

    y = np.zeros((NB, 128, DM), np.float32)
    for c in range(NC):
        blocks = blocks_of_core(c)
        yc = np.asarray(res.results[c]["y_own"]).reshape(NT, 128, DM)
        for j, b in enumerate(blocks):
            y[b] = yc[j]
    out = y.reshape(1, S, DM)
    if taps:
        return out, res
    return out


def kernel(**inputs):
    return run(inputs)


# version 13 (final)
# speedup vs baseline: 2.6898x; 1.0106x over previous
import math
import contextlib
import numpy as np
import ml_dtypes

import concourse.bass as bass
import concourse.mybir as mybir
from concourse.bass_utils import run_bass_kernel_spmd

F32 = mybir.dt.float32
BF16 = mybir.dt.bfloat16
AF = mybir.ActivationFunctionType
ALU = mybir.AluOpType
AX = mybir.AxisListType.X

NC = 8
S = 8192
DM = 4096
H = 32
HKV = 8
HD = 128
NB = 64
TOK = 1024
NT = 8

LN_EPS = 1e-5
QK_EPS = 1e-6
RMS_EPS = 1.1920929e-07
TAU = 0.1
SINK_ITERS = 50
NEG = -1e30
DEN_GUARD = 1e-30
ISQD = 1.0 / math.sqrt(HD)

SLOPES = [2.0 ** (-(h + 1) / 4.0) for h in range(H)]

def _chunk_w(h):
    for w in (128, 64, 32):
        if SLOPES[h] * (w - 1) <= 76.0:
            return w
    return 32
CHUNK_W = [_chunk_w(h) for h in range(H)]

T_SLOT = [64 - 8 * j for j in range(NT)]
WINDOW_TILES = [min(64, int(math.ceil(90.0 / SLOPES[h] / 128.0)) + 1) for h in range(H)]

def blocks_of_core(c):
    return [63 - c, 48 + c, 47 - c, 32 + c, 31 - c, 16 + c, 15 - c, c]

OWNER = [0] * NB
SLOT = [0] * NB
for _c in range(NC):
    for _j, _b in enumerate(blocks_of_core(_c)):
        OWNER[_b] = _c
        SLOT[_b] = _j
KCOL = [OWNER[g] * TOK + SLOT[g] * 128 for g in range(NB)]

def kept_lo(h, j):
    if WINDOW_TILES is None:
        return 0
    return max(0, T_SLOT[j] - 8 - WINDOW_TILES[h])

def active_slots(h, qg, t):
    out = []
    for j in range(4 * qg, 4 * qg + 4):
        if kept_lo(h, j) <= t < T_SLOT[j]:
            out.append(j)
    return out


def att_exp_cols():
    for qg in range(2):
        tmax = T_SLOT[4 * qg]
        for kv in range(HKV):
            for hg in range(4):
                h = 4 * kv + hg
                for t in range(tmax):
                    for j in active_slots(h, qg, t):
                        for ci in range(128 // CHUNK_W[h]):
                            yield (kv, h, qg, j, t, ci)

PLANE_COUNTS = {}
for _rec in att_exp_cols():
    _k = (_rec[2], _rec[0])
    PLANE_COUNTS[_k] = PLANE_COUNTS.get(_k, 0) + 1
PLANE_OFF = {}
_off = 0
for _qg in range(2):
    for _kv in range(HKV):
        PLANE_OFF[(_qg, _kv)] = _off
        _off += PLANE_COUNTS[(_qg, _kv)]
PLANE_TOTAL = _off
PLANE_MAX = max(PLANE_COUNTS.values())



def build_plane_rhs(c):
    blocks = blocks_of_core(c)
    r0 = np.zeros(PLANE_TOTAL, np.float32)
    r1 = np.zeros(PLANE_TOTAL, np.float32)
    i = 0
    for (kv, h, qg, j, t, ci) in att_exp_cols():
        b = blocks[j]
        w = CHUNK_W[h]
        if t > b:
            r0[i] = 0.0
            r1[i] = NEG
        else:
            cq = b * 128 + ci * w + w // 2
            r0[i] = SLOPES[h]
            r1[i] = SLOPES[h] * (128.0 * t - cq)
        i += 1
    return np.stack([r0, r1]).astype(np.float32)


def build_masks(c):
    blocks = blocks_of_core(c)
    tri = (np.arange(128)[:, None] <= np.arange(128)[None, :]).astype(np.float32)
    out = np.zeros((128, NT * 8 * 128), np.float32)
    for j in range(NT):
        b = blocks[j]
        for r8 in range(8):
            t = T_SLOT[j] - 8 + r8
            col = (j * 8 + r8) * 128
            if t < b:
                out[:, col:col + 128] = 1.0
            elif t == b:
                out[:, col:col + 128] = tri
    return out.astype(ml_dtypes.bfloat16)



class Sem:
    def __init__(self, st, nc, name):
        self.h = st.enter_context(nc.semaphore(name))
        self.n = 0
        self._wcache = {}

    def inc(self, instr, k=1):
        instr.then_inc(self.h, k)
        self.n += k

    def wait(self, eng, val=None):
        v = self.n if val is None else val
        key = id(eng)
        if self._wcache.get(key, -1) >= v:
            return
        self._wcache[key] = v
        eng.wait_ge(self.h, v)


class Ring:
    def __init__(self, st, nc, name, n):
        self.s = [Sem(st, nc, f"{name}{i}") for i in range(n)]
        self.n = n

    def inc(self, instr, slot, k=16):
        self.s[slot % self.n].inc(instr, k)

    def wait(self, eng, slot):
        sm = self.s[slot % self.n]
        sm.wait(eng)

    def wait_all(self, eng):
        for sm in self.s:
            sm.wait(eng)


def W(eng, sem, v):
    sem.wait(eng, v)


def build_program(taps=(), phase_limit=None):
    nc = bass.Bass("TRN2", target_bir_lowering=False, debug=False)

    def I(name, shape, dt=F32):
        return nc.dram_tensor(name, shape, dt, kind="ExternalInput").ap()

    x_in = I("x_own", [TOK, DM])
    wsl_in = I("w_slice", [1280, DM])
    wq_in = I("L0_attn_Wq", [DM, DM])
    wk_in = I("L0_attn_Wk", [HKV * HD, DM])
    wv_in = I("L0_attn_Wv", [HKV * HD, DM])
    wo_in = I("L0_attn_Wo", [DM, DM])
    qnw_in = I("L0_attn_q_norm_w", [1, HD])
    knw_in = I("L0_attn_k_norm_w", [1, HD])
    wres_in = I("L0_ffn_W_res", [16, DM])
    wpre_in = I("L0_ffn_W_pre", [4, DM])
    wpost_in = I("L0_ffn_W_post", [4, DM])
    hresb_in = I("L0_ffn_H_res_base", [1, 16])
    hpreb_in = I("L0_ffn_H_pre_base", [1, 4])
    hpostb_in = I("L0_ffn_H_post_base", [1, 4])
    gate_in = I("L0_ffn_gate", [1, 1])
    wb_in = I("L0_ffn_W_branch", [1024, 1024])
    ident_in = I("c_ident", [128, 128], BF16)
    onesf_in = I("c_ones_f32", [1, 128])
    k2_in = I("c_k2", [2, 128])
    prhs_in = I("c_plane_rhs", [2, PLANE_TOTAL])
    mask_in = I("c_masks", [128, NT * 8 * 128], BF16)

    y_out = nc.dram_tensor("y_own", [TOK, DM], F32, kind="ExternalOutput").ap()

    wtr_kv = nc.dram_tensor("wtr_kv", [DM, 256], BF16).ap()
    wtr_all_kv = nc.dram_tensor("wtr_all_kv", [NC * DM, 256], BF16).ap()
    wtr_q = nc.dram_tensor("wtr_q", [DM, 512], BF16).ap()
    wtr_all_q = nc.dram_tensor("wtr_all_q", [NC * DM, 512], BF16).ap()
    wtr_o = nc.dram_tensor("wtr_o", [DM, 512], BF16).ap()
    wtr_all_o = nc.dram_tensor("wtr_all_o", [NC * DM, 512], BF16).ap()
    xn_dram = nc.dram_tensor("xn_dram", [TOK, DM], BF16).ap()
    q_dram = nc.dram_tensor("q_dram", [128, 2 * H * 512], BF16).ap()
    attn_dram = nc.dram_tensor("attn_dram", [TOK, DM], BF16).ap()
    ffn_dram = nc.dram_tensor("ffn_dram", [TOK, DM], F32).ap()
    kv_send = nc.dram_tensor("kv_send", [2 * HKV * HD, TOK], BF16).ap()
    kv_all = nc.dram_tensor("kv_all", [NC * 2 * HKV * HD, TOK], BF16).ap()

    tap_map = {
        "xn": (xn_dram, [TOK, DM], BF16),
        "q": (q_dram, [128, 2 * H * 512], BF16),
        "attn": (attn_dram, [TOK, DM], BF16),
        "ffn": (ffn_dram, [TOK, DM], F32),
        "kvall": (kv_all, [NC * 2 * HKV * HD, TOK], BF16),
    }
    tap_outs = []
    tap_sb = []
    for tname in taps:
        if tname == "hco":
            tap_sb.append(("hco", nc.dram_tensor("tap_hco", [128, 256], F32, kind="ExternalOutput").ap()))
            continue
        if tname == "sink":
            tap_sb.append(("sink", nc.dram_tensor("tap_sink", [128, 1152], F32, kind="ExternalOutput").ap()))
            continue
        tsrc, shp, dt = tap_map[tname]
        tap_outs.append((nc.dram_tensor("tap_" + tname, shp, dt, kind="ExternalOutput").ap(), tsrc))

    st = contextlib.ExitStack()
    with st:
        A = st.enter_context(nc.sbuf_tensor("arenaA", [128, 32768], BF16))
        B = st.enter_context(nc.sbuf_tensor("arenaB", [128, 24576], BF16))
        Q = st.enter_context(nc.sbuf_tensor("arenaQ", [128, 16384], BF16))
        C = st.enter_context(nc.sbuf_tensor("arenaC", [128, 16384], BF16))
        M = st.enter_context(nc.sbuf_tensor("arenaM", [128, 10240], BF16))

        names = ["s_ct", "s_x", "s_v", "s_a", "s_xs", "s_fm", "s_wck", "s_wcq",
                 "s_wco", "s_wb", "s_pe", "s_pp", "s_kvs", "s_cc", "s_kl",
                 "s_ql", "s_pl", "s_sc", "s_scf", "s_ex", "s_mk", "s_av",
                 "s_nf", "s_pb", "s_at", "s_ats", "s_ff", "s_y", "s_ld", "s_cm", "s_ws"]
        SM = {n: Sem(st, nc, n) for n in names}
        (s_ct, s_x, s_v, s_a, s_xs, s_fm, s_wck, s_wcq, s_wco, s_wb, s_pe,
         s_pp, s_kvs, s_cc, s_kl, s_ql, s_pl, s_sc, s_scf, s_ex, s_mk, s_av,
         s_nf, s_pb, s_at, s_ats, s_ff, s_y, s_ld, s_cm, s_ws) = (SM[n] for n in names)

        ring_x = Ring(st, nc, "rx", 2)
        ring_wt = Ring(st, nc, "rwt", 2)
        ring_kv = Ring(st, nc, "rkv", 2)
        ring_pr = Ring(st, nc, "rpr", 2)
        ring_at = Ring(st, nc, "rat", 4)
        ring_y = Ring(st, nc, "ry", 2)
        ring_ws = Ring(st, nc, "rws", 4)
        ring_wc = Ring(st, nc, "rwc", 2)
        ring_q = Ring(st, nc, "rq", 2)

        T, V, SC, GP, SY = nc.tensor, nc.vector, nc.scalar, nc.gpsimd, nc.sync

        def finish_debug():
            for sm in SM.values():
                GP.wait_ge(sm.h, sm.n)
            for rg in (ring_x, ring_wt, ring_kv, ring_pr, ring_at, ring_y, ring_ws,
                       ring_wc, ring_q):
                rg.wait_all(GP)
            for tp, tsrc in tap_outs:
                s_y.inc(GP.dma_start(tp, tsrc), 16)
            for kind, tp in tap_sb:
                if kind == "hco":
                    s_y.inc(GP.dma_start(tp, hcoef[:, 0:256]), 16)
                else:
                    s_y.inc(GP.dma_start(tp, sink[:, 0:1152]), 16)
            s_y.wait(GP, s_y.n)

        _mo = [0]
        def mslice(n_bf16, dt=BF16):
            o = _mo[0]
            _mo[0] += (n_bf16 + 15) & ~15
            assert _mo[0] <= 10240, f"M overflow {_mo[0]}"
            ap = M[:, o:o + n_bf16]
            return ap.bitcast(dt) if dt != BF16 else ap

        ident = mslice(128)
        onesf = mslice(256, F32)
        k2 = mslice(256, F32)
        onescol = mslice(16)
        qnw = mslice(16, F32)
        nrow = mslice(512, F32)
        shift_b = mslice(16, F32)
        stats = mslice(160, F32)
        hcoef = mslice(512, F32)
        gcol = mslice(16, F32)
        waT = mslice(768)
        sink = mslice(2304, F32)
        dencol = mslice(32, F32)
        zrow = mslice(288)

        ST_BN, ST_AG, ST_R2, ST_SCR = 0, 48, 50, 58
        HC_RES, HC_PRE, HC_POST, HC_BASE = 0, 128, 160, 192

        CONST_VALS = [LN_EPS, RMS_EPS, QK_EPS, -0.5, -1.0, 1.0 / HD, ISQD,
                      1.0, 0.0, DEN_GUARD]
        cvals = mslice(16 * ((2 * len(CONST_VALS) + 15) // 16), F32)
        for ci_, val_ in enumerate(CONST_VALS):
            nc.const_aps.register(F32, val_, cvals[:, ci_:ci_ + 1])

        hco = hcoef
        assert PLANE_MAX <= 2496, PLANE_MAX

        for ci_, val_ in enumerate(CONST_VALS):
            GP.memset(cvals[:, ci_:ci_ + 1], float(val_))
        GP.memset(onescol[:, 0:1], 1.0)
        GP.memset(sink[:, 0:1152], 0.0)
        GP.memset(hco[:, 0:256], 0.0)
        GP.memset(zrow[:, 0:288], 0.0)
        s_cm.inc(GP.memset(onescol[:, 1:2], 0.0), 1)
        s_cm.wait(SY, 1)
        s_ct.inc(SY.dma_start(ident, ident_in), 16)
        s_ct.inc(SY.dma_start(onesf[0:1, 0:128], onesf_in), 16)
        s_ct.inc(SY.dma_start(k2[0:2, 0:128], k2_in), 16)
        s_ct.inc(SY.dma_start(qnw[:, 0:1], qnw_in.rearrange("o (f u) -> (o f) u", u=1)), 16)
        s_ct.inc(SY.dma_start(qnw[:, 1:2], knw_in.rearrange("o (f u) -> (o f) u", u=1)), 16)
        s_ct.inc(SY.dma_start(nrow[0:1, 0:128], qnw_in), 16)
        s_ct.inc(SY.dma_start(nrow[0:1, 128:256], knw_in), 16)
        s_ct.inc(SY.dma_start(hco[0:1, HC_BASE:HC_BASE + 16], hresb_in), 16)
        s_ct.inc(SY.dma_start(hco[0:1, HC_BASE + 16:HC_BASE + 20], hpreb_in), 16)
        s_ct.inc(SY.dma_start(hco[0:1, HC_BASE + 20:HC_BASE + 24], hpostb_in), 16)
        s_ct.inc(SY.dma_start(gcol[0:1, 3:4], gate_in), 16)

        s_ct.wait(V)
        s_v.inc(V.reduce_max(gcol[0:1, 4:5], nrow[0:1, 0:128], axis=AX,
                             apply_absolute_value=True))
        s_v.inc(V.reduce_max(gcol[0:1, 5:6], nrow[0:1, 128:256], axis=AX,
                             apply_absolute_value=True))
        s_v.wait(V)
        s_v.inc(V.tensor_mul(gcol[0:1, 6:7], gcol[0:1, 4:5], gcol[0:1, 5:6]))
        s_v.wait(V)
        s_v.inc(V.tensor_scalar_mul(gcol[0:1, 7:8], gcol[0:1, 6:7],
                                    float(math.sqrt(128.0))))

        RT_ORDER = [4, 5, 0, 1, 2, 3, 6, 7, 8, 9]

        def wdst(rt):
            if rt == 4:
                return wtr_kv, 0
            if rt == 5:
                return wtr_kv, 128
            if rt < 4:
                return wtr_q, rt * 128
            return wtr_o, (rt - 6) * 128

        xn_fm = A[:, 0:32768]
        with contextlib.ExitStack() as pws:
            wtp = [pws.enter_context(nc.psum_tensor(f"wtp{i}", [128, 128], BF16))
                   for i in range(4)]
            xtp = [pws.enter_context(nc.psum_tensor(f"xtp{i}", [128, 128], BF16))
                   for i in range(4)]
            wrowb = [C[:, 0:4096], C[:, 4096:8192]]
            wslab = [B[:, 16384:20480], B[:, 20480:24576]]

            def p0w_gen():
                ev_w = {}
                tr_end = {}
                s_ct.wait(T)
                for i, rt in enumerate(RT_ORDER):
                    if i >= 2:
                        s_pe.wait(GP, tr_end[i - 2])
                    ring_wc.inc(GP.dma_start(wrowb[i % 2],
                                             wsl_in[rt * 128:(rt + 1) * 128, :]), i)
                    ring_wc.wait(T, i)
                    for fc in range(32):
                        u = i * 32 + fc
                        if u >= 4:
                            s_v.wait(T, ev_w[u - 4])
                        s_pe.inc(T.transpose(wtp[u % 4][:, 0:128],
                                             wrowb[i % 2][:, fc * 128:(fc + 1) * 128],
                                             ident))
                        s_pe.wait(V, s_pe.n)
                        if fc == 0 and i >= 2:
                            ring_ws.wait(V, i - 2)
                        s_v.inc(V.tensor_copy(wslab[i % 2][:, fc * 128:(fc + 1) * 128],
                                              wtp[u % 4][:, 0:128]))
                        ev_w[u] = s_v.n
                    tr_end[i] = s_pe.n
                    s_v.wait(SY, s_v.n)
                    dstt, c0 = wdst(rt)
                    ring_ws.inc(SY.dma_start(
                        dstt[:, c0:c0 + 128].rearrange("(f p) c -> p f c", p=128),
                        wslab[i % 2].rearrange("p (f c) -> p f c", c=128)), i)
                    if i == 1:
                        ring_ws.wait(GP, 0)
                        ring_ws.wait(GP, 1)
                        s_ws.inc(GP.collective_compute(
                            "AllGather", ALU.bypass,
                            replica_groups=[list(range(NC))],
                            ins=[wtr_kv.opt()], outs=[wtr_all_kv.opt()],
                        ))
                    if i == 5:
                        ring_ws.wait_all(GP)
                        s_ws.inc(GP.collective_compute(
                            "AllGather", ALU.bypass,
                            replica_groups=[list(range(NC))],
                            ins=[wtr_q.opt()], outs=[wtr_all_q.opt()],
                        ))
                    yield

            pgen = p0w_gen()

            s_ct.wait(SC)
            s_cm.wait(SC)
            s_cm.wait(V)
            s_ct.wait(T)
            s_cm.wait(T)
            s_ct.wait(GP)
            xbuf = [B[:, 0:8192].bitcast(F32), B[:, 8192:16384].bitcast(F32)]
            xnb = [Q[:, 0:4096], Q[:, 4096:8192]]
            dump = sink[:, 0:1024]
            ev_xn = {}
            tr_xn = {}
            for tt in range(NT):
                par = tt % 2
                xb = xbuf[par]
                if tt >= 2:
                    s_a.wait(SY, a_done_xn[tt - 2])
                ring_x.inc(SY.dma_start(xb, x_in[tt * 128:(tt + 1) * 128, :]), tt)
                ring_x.wait(V, tt)
                if tt >= 1:
                    s_a.wait(V)
                for ch in range(8):
                    s_v.inc(V.bn_stats(stats[:, ST_BN + ch * 6:ST_BN + (ch + 1) * 6],
                                       xb[:, ch * 512:(ch + 1) * 512]))
                s_v.wait(V)
                s_v.inc(V.bn_aggr(stats[:, ST_AG:ST_AG + 2], stats[:, ST_BN:ST_BN + 48]))
                s_v.wait(V)
                cm = stats[:, ST_AG + 0:ST_AG + 1]
                cvv = stats[:, ST_AG + 1:ST_AG + 2]
                crstd = stats[:, ST_SCR + 2:ST_SCR + 3]
                cbias = stats[:, ST_SCR + 3:ST_SCR + 4]
                s_v.wait(SC)
                s_a.inc(SC.activation(stats[:, ST_SCR + 4:ST_SCR + 5], cvv, AF.Ln, bias=LN_EPS))
                s_a.wait(SC)
                s_a.inc(SC.activation(crstd, stats[:, ST_SCR + 4:ST_SCR + 5], AF.Exp, scale=-0.5))
                s_a.wait(V)
                s_v.inc(V.scalar_tensor_tensor(cbias, cm, -1.0, crstd, ALU.mult, ALU.mult))
                s_v.inc(V.scalar_tensor_tensor(stats[:, ST_SCR + 5:ST_SCR + 6], crstd, 1.0, crstd,
                                               ALU.mult, ALU.mult))
                s_v.wait(V)
                s_v.inc(V.tensor_mul(stats[:, ST_SCR + 5:ST_SCR + 6], stats[:, ST_SCR + 5:ST_SCR + 6], cvv))
                s_v.wait(SC)
                s_a.inc(SC.activation(stats[:, ST_SCR + 6:ST_SCR + 7], stats[:, ST_SCR + 5:ST_SCR + 6],
                                      AF.Ln, bias=RMS_EPS))
                s_a.wait(SC)
                s_a.inc(SC.activation(stats[:, ST_R2 + tt:ST_R2 + tt + 1], stats[:, ST_SCR + 6:ST_SCR + 7],
                                      AF.Exp, scale=-0.5))
                s_v.wait(SC)
                s_xs.wait(SC, s_xs.n)
                if tt >= 2:
                    s_pe.wait(SC, tr_xn[tt - 2])
                xo = xnb[par]
                s_a.inc(SC.activation(xo, xb, AF.Identity, bias=cbias, scale=crstd))
                if tt == 0:
                    a_done_xn = {}
                a_done_xn[tt] = s_a.n
                s_a.wait(SC, s_a.n)
                s_xs.inc(SC.dma_start(xn_dram[tt * 128:(tt + 1) * 128, :], xo), 16)
                s_a.wait(T, a_done_xn[tt])
                for fc in range(32):
                    u = tt * 32 + fc
                    if u >= 4:
                        s_fm.wait(T, ev_xn[u - 4])
                    s_pe.inc(T.transpose(xtp[u % 4][:, 0:128],
                                         xo[:, fc * 128:(fc + 1) * 128], ident))
                    s_pe.wait(V, s_pe.n)
                    s_fm.inc(V.tensor_copy(
                        xn_fm[:, fc * 1024 + tt * 128: fc * 1024 + (tt + 1) * 128],
                        xtp[u % 4][:, 0:128]))
                    ev_xn[u] = s_fm.n
                tr_xn[tt] = s_pe.n
                next(pgen, None)
                next(pgen, None)
            for _ in pgen:
                pass
        p0_pe_end = s_pe.n

        if phase_limit == "P0":
            finish_debug()
            return nc
        if phase_limit == "P1":
            finish_debug()
            return nc
        wsm_f = B[:, 0:8192].bitcast(F32)
        wsm_b = B[:, 8192:16384]
        s_v.wait(SY, s_v.n)
        s_ct.inc(SY.dma_start(wsm_f[0:16, :], wres_in), 16)
        s_ct.inc(SY.dma_start(wsm_f[16:20, :], wpre_in), 16)
        s_ct.inc(SY.dma_start(wsm_f[20:24, :], wpost_in), 16)
        s_ct.wait(V)
        s_v.inc(V.tensor_copy(wsm_b[0:32, 0:4096], wsm_f[0:32, :]))

        Zf = sink
        U0, V0, SCR, R64, R32a, R32b, R32c = 128, 160, 192, 320, 384, 416, 448
        ZP, ZQ = 480, 512

        with contextlib.ExitStack() as pst:
            ps_a = pst.enter_context(nc.psum_tensor("ps_a", [128, 512], F32))
            ps_b = pst.enter_context(nc.psum_tensor("ps_b", [128, 512], F32))
            ps_tb = [pst.enter_context(nc.psum_tensor(f"ps_tb{i}", [128, 64], BF16))
                     for i in range(2)]
            psl = [ps_a, ps_b]

            s_v.wait(T)
            s_fm.wait(T)
            s_ct.wait(T)
            for fc in range(32):
                if fc >= 2:
                    s_pp.wait(T, s_pp.n - 1)
                s_pe.inc(T.transpose(ps_tb[fc % 2][:, 0:32],
                                     wsm_b[0:32, fc * 128:(fc + 1) * 128], ident[0:32, 0:32]))
                s_pe.wait(V, s_pe.n)
                s_pp.inc(V.tensor_copy(waT[:, fc * 24:(fc + 1) * 24],
                                       ps_tb[fc % 2][:, 0:24]))

            s_pp.wait(T, s_pp.n)
            s_pe.inc(T.matmul(ps_a[:, 0:24], onesf[0:1, 0:128],
                              hco[0:1, HC_BASE:HC_BASE + 24], start=True, stop=True))
            s_pe.inc(T.matmul(ps_b[:, 0:1], onesf[0:1, 0:128], gcol[0:1, 3:4],
                              start=True, stop=True))
            s_pe.wait(V, s_pe.n)
            s_v.inc(V.tensor_copy(hco[:, HC_BASE:HC_BASE + 24], ps_a[:, 0:24]))
            s_pe.wait(SC, s_pe.n)
            s_a.inc(SC.activation(gcol[:, 4:5], ps_b[:, 0:1], AF.Exp, scale=-1.0))
            s_a.wait(V)
            s_v.inc(V.tensor_scalar(gcol[:, 5:6], gcol[:, 4:5], 1.0, None, ALU.add))
            s_v.wait(V)
            s_v.inc(V.reciprocal(gcol[:, 0:1], gcol[:, 5:6]))
            s_v.wait(V)
            s_v.inc(V.tensor_scalar(gcol[:, 1:2], gcol[:, 0:1], -1.0, 1.0, ALU.mult, ALU.add))
            s_v.wait(V)

            s_fm.wait(T)
            s_v.wait(T, s_v.n)
            s_a.wait(T, s_a.n)
            logit_done = {}
            for tt in range(NT):
                ps = psl[tt % 2]
                if tt >= 2:
                    s_v.wait(T, logit_done[tt - 2])
                mm = None
                for fc in range(32):
                    mm = T.matmul(ps[:, 0:24],
                                  xn_fm[:, fc * 1024 + tt * 128: fc * 1024 + (tt + 1) * 128],
                                  waT[:, fc * 24:(fc + 1) * 24],
                                  start=(fc == 0), stop=(fc == 31))
                s_pe.inc(mm)
                s_pe.wait(V, s_pe.n)
                r2 = stats[:, ST_R2 + tt:ST_R2 + tt + 1]
                s_v.inc(V.scalar_tensor_tensor(Zf[:, tt * 16:(tt + 1) * 16], ps[:, 0:16],
                                               r2, hco[:, HC_BASE:HC_BASE + 16],
                                               ALU.mult, ALU.add))
                s_v.inc(V.scalar_tensor_tensor(Zf[:, ZP + tt * 4:ZP + (tt + 1) * 4], ps[:, 16:20],
                                               r2, hco[:, HC_BASE + 16:HC_BASE + 20],
                                               ALU.mult, ALU.add))
                s_v.inc(V.scalar_tensor_tensor(Zf[:, ZQ + tt * 4:ZQ + (tt + 1) * 4], ps[:, 20:24],
                                               r2, hco[:, HC_BASE + 20:HC_BASE + 24],
                                               ALU.mult, ALU.add))
                logit_done[tt] = s_v.n

            s_v.wait(V)
            if phase_limit == "P2a_logits":
                finish_debug()
                return nc
            s_v.inc(V.tensor_scalar_mul(Zf[:, 0:128], Zf[:, 0:128], 1.0 / TAU))
            s_v.inc(V.memset(Zf[:, U0:U0 + 32], 0.0))
            s_v.inc(V.memset(Zf[:, V0:V0 + 32], 0.0))
            s_v.wait(V)

            z4 = Zf[:, 0:128].rearrange("p (g i j) -> p g i j", g=8, i=4, j=4)
            z4T = Zf[:, 0:128].rearrange("p (g i j) -> p g j i", g=8, i=4, j=4)
            sc4 = Zf[:, SCR:SCR + 128].rearrange("p (g i j) -> p g i j", g=8, i=4, j=4)
            sc4T = Zf[:, SCR:SCR + 128].rearrange("p (g i j) -> p g j i", g=8, i=4, j=4)
            scf = Zf[:, SCR:SCR + 128]
            u3 = Zf[:, U0:U0 + 32].rearrange("p (g i) -> p g i", g=8)
            v3 = Zf[:, V0:V0 + 32].rearrange("p (g j) -> p g j", g=8)
            r64a = Zf[:, R64:R64 + 64]
            r32a = Zf[:, R32a:R32a + 32]
            r32b = Zf[:, R32b:R32b + 32]
            r32c = Zf[:, R32c:R32c + 32]

            def half_iter(out32, reduce_j):
                if reduce_j:
                    s_v.inc(V.tensor_tensor(sc4T, z4T, v3.broadcast_to([128, 8, 4, 4]), ALU.add))
                else:
                    s_v.inc(V.tensor_tensor(sc4, z4, u3.broadcast_to([128, 8, 4, 4]), ALU.add))
                s_v.wait(V)
                if reduce_j:
                    pv = scf.rearrange("p (x b) -> p x b", b=2)
                    s_v.inc(V.tensor_max(r64a, pv[:, :, 0], pv[:, :, 1]))
                    s_v.wait(V)
                    pv2 = r64a.rearrange("p (x b) -> p x b", b=2)
                    s_v.inc(V.tensor_max(r32a, pv2[:, :, 0], pv2[:, :, 1]))
                else:
                    g3a = Zf[:, R64:R64 + 32].rearrange("p (g j) -> p g j", g=8)
                    g3b = Zf[:, R64 + 32:R64 + 64].rearrange("p (g j) -> p g j", g=8)
                    s_v.inc(V.tensor_max(g3a, sc4[:, :, 0, :], sc4[:, :, 1, :]))
                    s_v.inc(V.tensor_max(g3b, sc4[:, :, 2, :], sc4[:, :, 3, :]))
                    s_v.wait(V)
                    s_v.inc(V.tensor_max(r32a, Zf[:, R64:R64 + 32], Zf[:, R64 + 32:R64 + 64]))
                s_v.wait(V)
                m3 = r32a.rearrange("p (g a) -> p g a", g=8)
                if reduce_j:
                    s_v.inc(V.tensor_tensor(sc4, sc4, m3.broadcast_to([128, 8, 4, 4]), ALU.subtract))
                else:
                    s_v.inc(V.tensor_tensor(sc4T, sc4T, m3.broadcast_to([128, 8, 4, 4]), ALU.subtract))
                s_v.wait(SC)
                s_a.inc(SC.activation(scf, scf, AF.Exp))
                s_a.wait(V)
                if reduce_j:
                    pv = scf.rearrange("p (x b) -> p x b", b=2)
                    s_v.inc(V.tensor_add(r64a, pv[:, :, 0], pv[:, :, 1]))
                    s_v.wait(V)
                    pv2 = r64a.rearrange("p (x b) -> p x b", b=2)
                    s_v.inc(V.tensor_add(r32b, pv2[:, :, 0], pv2[:, :, 1]))
                else:
                    g3a = Zf[:, R64:R64 + 32].rearrange("p (g j) -> p g j", g=8)
                    g3b = Zf[:, R64 + 32:R64 + 64].rearrange("p (g j) -> p g j", g=8)
                    s_v.inc(V.tensor_add(g3a, sc4[:, :, 0, :], sc4[:, :, 1, :]))
                    s_v.inc(V.tensor_add(g3b, sc4[:, :, 2, :], sc4[:, :, 3, :]))
                    s_v.wait(V)
                    s_v.inc(V.tensor_add(r32b, Zf[:, R64:R64 + 32], Zf[:, R64 + 32:R64 + 64]))
                s_v.wait(SC)
                s_a.inc(SC.activation(r32c, r32b, AF.Ln))
                s_a.wait(V)
                s_v.inc(V.tensor_add(r32c, r32c, r32a))
                s_v.wait(V)
                s_v.inc(V.tensor_scalar_mul(out32, r32c, -1.0))
                s_v.wait(V)

        def coef_gen():
            for _ in range(SINK_ITERS):
                half_iter(Zf[:, U0:U0 + 32], True)
                yield
                half_iter(Zf[:, V0:V0 + 32], False)
                yield

            s_v.inc(V.tensor_tensor(sc4, z4, u3.broadcast_to([128, 8, 4, 4]), ALU.add))
            s_v.wait(V)
            s_v.inc(V.tensor_tensor(sc4T, sc4T, v3.broadcast_to([128, 8, 4, 4]), ALU.add))
            s_v.wait(SC)
            s_a.inc(SC.activation(scf, scf, AF.Exp))
            s_a.wait(V)
            s_v.inc(V.tensor_scalar(hco[:, HC_RES:HC_RES + 128], scf, gcol[:, 0:1], None, ALU.mult))
            s_v.wait(V)
            hdiag = hco[:, HC_RES:HC_RES + 128].rearrange("p (g x) -> p g x", g=8)[:, :, 0:16:5]
            s_v.inc(V.tensor_scalar(hdiag, hdiag, gcol[:, 1:2], None, ALU.add))
            s_v.wait(V)
            yield

            def soft4(dst_off, src_off):
                sv = Zf[:, src_off:src_off + 32]
                p2 = sv.rearrange("p (x b) -> p x b", b=2)
                s_v.inc(V.tensor_max(Zf[:, R64:R64 + 16], p2[:, :, 0], p2[:, :, 1]))
                s_v.wait(V)
                p1 = Zf[:, R64:R64 + 16].rearrange("p (x b) -> p x b", b=2)
                s_v.inc(V.tensor_max(Zf[:, R32a:R32a + 8], p1[:, :, 0], p1[:, :, 1]))
                s_v.wait(V)
                s_v.inc(V.tensor_tensor(sv.rearrange("p (g j) -> p g j", g=8),
                                        sv.rearrange("p (g j) -> p g j", g=8),
                                        Zf[:, R32a:R32a + 8].broadcast_to([128, 8, 4]),
                                        ALU.subtract))
                s_v.wait(SC)
                s_a.inc(SC.activation(sv, sv, AF.Exp))
                s_a.wait(V)
                s_v.inc(V.tensor_add(Zf[:, R64:R64 + 16], p2[:, :, 0], p2[:, :, 1]))
                s_v.wait(V)
                s_v.inc(V.tensor_add(Zf[:, R32a:R32a + 8], p1[:, :, 0], p1[:, :, 1]))
                s_v.wait(V)
                s_v.inc(V.reciprocal(Zf[:, R32b:R32b + 8], Zf[:, R32a:R32a + 8]))
                s_v.wait(V)
                s_v.inc(V.tensor_tensor(hco[:, dst_off:dst_off + 32].rearrange("p (g j) -> p g j", g=8),
                                        sv.rearrange("p (g j) -> p g j", g=8),
                                        Zf[:, R32b:R32b + 8].broadcast_to([128, 8, 4]),
                                        ALU.mult))
                s_v.wait(V)

            soft4(HC_PRE, ZP)
            yield
            soft4(HC_POST, ZQ)
            s_v.inc(V.tensor_copy(hco[:, HC_BASE:HC_BASE + 1], hco[:, HC_BASE:HC_BASE + 1]))
            coef_done[0] = s_v.n
            yield

        coef_done = [None]
        cgen = coef_gen()

        if phase_limit == "P2a":
            for _ in cgen:
                pass
            finish_debug()
            return nc
        def proj(wbase, col_off, slice_w, fo_total, epilogue, gbase, bg=None,
                 bg_steps=0, ws_need=1, entry_guard=True):
            ngroups = fo_total // 256
            with contextlib.ExitStack() as ps_st:
                pp = [ps_st.enter_context(nc.psum_tensor(f"pp{gbase}_{i}", [128, 512], F32))
                      for i in range(2)]
                ps_ss = [ps_st.enter_context(nc.psum_tensor(f"pss{gbase}{i}", [1, 512], F32))
                         for i in range(1)]
                ps_iv = [ps_st.enter_context(nc.psum_tensor(f"piv{gbase}_{i}", [128, 512], F32))
                         for i in range(1)]
                wtb = [C[:, 8192:16384], C[:, 0:8192]]
                s_v.wait(T, s_v.n)
                s_a.wait(T, s_a.n)
                s_pp.wait(T, s_pp.n)
                if entry_guard:
                    s_pp.wait(SY, s_pp.n)
                    s_v.wait(SY, s_v.n)
                s_ws.wait(SY, ws_need)
                wt_done = {}
                unit_done = {}
                for g in range(ngroups):
                    fo0 = g * 256
                    if g == 1:
                        ring_wc.wait_all(SY)
                        s_pe.wait(SY, p0_pe_end)
                    if g >= 2:
                        sm_, ct_ = wt_done[g - 2]
                        SY.wait_ge(sm_.h, ct_)
                    for fc in range(32):
                        if slice_w == 512:
                            r_ = fo0 // 512
                            lo_ = fo0 % 512
                            ring_wt.inc(SY.dma_start(
                                wtb[g % 2][:, fc * 256:(fc + 1) * 256],
                                wbase[r_ * DM + fc * 128: r_ * DM + (fc + 1) * 128,
                                      col_off + lo_: col_off + lo_ + 256]), g)
                        else:
                            for hf in range(2):
                                r_ = (fo0 + hf * 128) // 128
                                ring_wt.inc(SY.dma_start(
                                    wtb[g % 2][:, fc * 256 + hf * 128: fc * 256 + (hf + 1) * 128],
                                    wbase[r_ * DM + fc * 128: r_ * DM + (fc + 1) * 128,
                                          col_off: col_off + 128]), g)
                    ring_wt.wait(T, g)
                    for blk in range(2):
                        for tch in range(2):
                            u = (g * 2 + blk) * 2 + tch
                            ps = pp[u % 2]
                            if u >= 2:
                                sm_, ct_ = unit_done[u - 2]
                                T.wait_ge(sm_.h, ct_)
                            mm = None
                            for fc in range(32):
                                mm = T.matmul(ps[:, 0:512],
                                              wtb[g % 2][:, fc * 256 + blk * 128: fc * 256 + (blk + 1) * 128],
                                              xn_fm[:, fc * 1024 + tch * 512: fc * 1024 + (tch + 1) * 512],
                                              start=(fc == 0), stop=(fc == 31))
                            s_pe.inc(mm)
                            ud = epilogue(g * 2 + blk, tch, ps, ps_ss[0], ps_iv[0])
                            unit_done[u] = ud
                            if bg is not None:
                                for _ in range(bg_steps):
                                    if next(bg, "DONE") == "DONE":
                                        bg = None
                                        break
                    wt_done[g] = ud
                flush = getattr(epilogue, "flush", None)
                if flush is not None:
                    flush()

        qf32 = [nrow[:, 0:0], nrow[:, 0:0]]
        qsc = Q[:, 0:4096].bitcast(F32)
        qsq = Q[:, 4096:5120]
        ivrow = Q[:, 5120:7168].bitcast(F32)
        kvdst = {"v": B[:, 8192:16384], "k": B[:, 0:8192]}

        def plain_epi(dst):
            def epi(blk, tch, ps, ps_ss, ps_iv):
                s_pe.wait(V, s_pe.n)
                s_pp.inc(V.tensor_copy(dst[:, blk * 1024 + tch * 512: blk * 1024 + (tch + 1) * 512],
                                       ps[:, 0:512]))
                return (s_pp, s_pp.n)
            return epi

        def norm_epi(dst, wcol, to_dram):
            st_ = {"recs": {}, "u": 0, "ln": s_a.n, "stt": s_v.n,
                   "ssmm": {}, "ivmm": {}, "ps_ss": None, "ps_iv": None}

            def stage_b(r):
                ps_ss = st_["ps_ss"]
                u = r["u"]
                s_v.wait(T, r["mul"])
                s_a.wait(T, st_["ln"])
                s_pe.inc(T.matmul(ps_ss[0:1, 0:512], onescol[:, 0:1],
                                  qsq[:, (u % 2) * 512:(u % 2) * 512 + 512],
                                  start=True, stop=True))
                st_["ssmm"][u] = s_pe.n
                ivr = ivrow[0:1, (u % 2) * 512:(u % 2) * 512 + 512]
                s_pe.wait(SC, s_pe.n)
                if u >= 2:
                    s_pe.wait(SC, st_["ivmm"][u - 2])
                s_a.inc(SC.activation(ivr, ps_ss[0:1, 0:512], AF.Ln,
                                      bias=QK_EPS, scale=1.0 / HD))
                s_a.wait(SC)
                s_a.inc(SC.activation(ivr, ivr, AF.Exp, scale=-0.5))
                r["exp"] = s_a.n
                st_["ln"] = s_a.n

            def stage_c(r):
                ps_iv = st_["ps_iv"]
                u = r["u"]
                ivr = ivrow[0:1, (u % 2) * 512:(u % 2) * 512 + 512]
                s_a.wait(T, r["exp"])
                s_v.wait(T, st_["stt"])
                s_pe.inc(T.matmul(ps_iv[:, 0:512], onesf[0:1, 0:128], ivr,
                                  start=True, stop=True))
                st_["ivmm"][u] = s_pe.n
                s_pe.wait(V, s_pe.n)
                if to_dram is not None and u >= 2:
                    ring_q.wait(V, u - 2)
                qf = qsc[:, (u % 4) * 512:(u % 4) * 512 + 512]
                d = dst(r["blk"], r["tch"], u)
                s_v.inc(V.scalar_tensor_tensor(d, qf, qnw[:, wcol:wcol + 1],
                                               ps_iv[:, 0:512], ALU.mult, ALU.mult))
                st_["stt"] = s_v.n
                if to_dram is not None:
                    s_v.wait(SC, s_v.n)
                    ring_q.inc(SC.dma_start(to_dram(r["blk"], r["tch"]), d), u)

            def epi(blk, tch, ps, ps_ss, ps_iv):
                st_["ps_ss"], st_["ps_iv"] = ps_ss, ps_iv
                u = st_["u"]
                st_["u"] = u + 1
                s_pe.wait(V, s_pe.n)
                qf = qsc[:, (u % 4) * 512:(u % 4) * 512 + 512]
                s_v.inc(V.tensor_copy(qf, ps[:, 0:512]))
                ud = (s_v, s_v.n)
                if u >= 2:
                    s_pe.wait(V, st_["ssmm"][u - 2])
                s_v.wait(V)
                s_v.inc(V.tensor_mul(qsq[:, (u % 2) * 512:(u % 2) * 512 + 512], qf, qf))
                st_["recs"][u] = {"u": u, "blk": blk, "tch": tch, "mul": s_v.n}
                if u >= 1:
                    stage_b(st_["recs"][u - 1])
                if u >= 2:
                    stage_c(st_["recs"].pop(u - 2))
                return ud

            def flush():
                n = st_["u"]
                if n >= 1:
                    stage_b(st_["recs"][n - 1])
                for uu in (n - 2, n - 1):
                    if uu >= 0 and uu in st_["recs"]:
                        stage_c(st_["recs"].pop(uu))
            epi.flush = flush
            return epi

        s_fm.wait(T)
        proj(wtr_all_kv, 128, 128, HKV * HD, plain_epi(kvdst["v"]), "v", bg=cgen,
             bg_steps=4, entry_guard=False)
        s_xs.wait(V)
        s_xs.wait(SC)
        kslice = lambda blk, tch, u: kvdst["k"][:, blk * 1024 + tch * 512: blk * 1024 + (tch + 1) * 512]
        proj(wtr_all_kv, 0, 128, HKV * HD, norm_epi(kslice, 1, None), "k", bg=cgen, bg_steps=4)
        for _ in cgen:
            pass

        vtm = B[:, 16384:24576]
        ring_ws.wait_all(V)
        with contextlib.ExitStack() as pvs:
            vtp = [pvs.enter_context(nc.psum_tensor(f"vtp{i}", [128, 128], BF16))
                   for i in range(2)]
            s_v.wait(T, s_v.n)
            s_a.wait(T, s_a.n)
            vt_done = {}
            for kv in range(HKV):
                for j in range(NT):
                    u = kv * NT + j
                    if u >= 2:
                        s_v.wait(T, vt_done[u - 2])
                    s_pe.inc(T.transpose(vtp[u % 2][:, 0:128],
                                         kvdst["v"][:, kv * 1024 + j * 128: kv * 1024 + (j + 1) * 128],
                                         ident))
                    s_pe.wait(V, s_pe.n)
                    s_v.inc(V.tensor_copy(vtm[:, (j * 8 + kv) * 128:(j * 8 + kv + 1) * 128],
                                          vtp[u % 2][:, 0:128]))
                    vt_done[u] = s_v.n
        s_pp.wait(SC, s_pp.n)
        s_v.wait(SC, s_v.n)
        for kv in range(HKV):
            s_kvs.inc(SC.dma_start(kv_send[kv * 128:(kv + 1) * 128, :],
                                   kvdst["k"][:, kv * 1024:(kv + 1) * 1024]), 16)
        for j in range(NT):
            s_kvs.inc(SC.dma_start(kv_send[1024 + j * 128:1024 + (j + 1) * 128, :],
                                   vtm[:, j * 1024:(j + 1) * 1024]), 16)
        s_kvs.wait(GP, s_kvs.n)
        s_cc.inc(GP.collective_compute(
            "AllGather", ALU.bypass,
            replica_groups=[list(range(NC))],
            ins=[kv_send.opt()], outs=[kv_all.opt()],
        ))

        s_fm.wait(GP)
        s_xs.wait(GP)
        for ci in range(8):
            s_wb.inc(GP.dma_start(Q[:, 7680 + ci * 1024:7680 + (ci + 1) * 1024],
                                  wb_in[ci * 128:(ci + 1) * 128, :]), 16)
        wbsb = Q[:, 7680:15872]
        xnt = [B[:, 0:4096], B[:, 4096:8192]]
        xpf = B[:, 8192:12288].bitcast(F32)
        xpre_b = B[:, 12288:13312]
        xprefm = B[:, 13312:14336]
        yv = xpf[:, 1024:2048]
        fo_f = B[:, 14336:22528].bitcast(F32)

        def mix_gen(pf, pfb):
            s_kvs.wait(SY, s_kvs.n)
            v_used_xnt = {}
            for tt in range(NT):
                par = tt % 2
                if tt >= 2:
                    s_v.wait(SY, v_used_xnt[tt - 2])
                ring_x.inc(SY.dma_start(xnt[par], xn_dram[tt * 128:(tt + 1) * 128, :]), tt)
                ring_x.wait(V, tt)
                if tt >= 1:
                    s_ff.wait(V, s_ff.n)
                hpre = lambda n: hco[:, HC_PRE + tt * 4 + n: HC_PRE + tt * 4 + n + 1]
                hpost = lambda n: hco[:, HC_POST + tt * 4 + n: HC_POST + tt * 4 + n + 1]
                hres = lambda n, jj: hco[:, HC_RES + tt * 16 + n * 4 + jj: HC_RES + tt * 16 + n * 4 + jj + 1]
                s_v.inc(V.tensor_scalar(xpf[:, 0:1024], xnt[par][:, 0:1024], hpre(0), None, ALU.mult))
                for n in range(1, 4):
                    s_v.wait(V)
                    s_v.inc(V.scalar_tensor_tensor(xpf[:, 0:1024], xnt[par][:, n * 1024:(n + 1) * 1024],
                                                   hpre(n), xpf[:, 0:1024], ALU.mult, ALU.add))
                s_v.wait(V)
                s_v.inc(V.tensor_copy(xpre_b, xpf[:, 0:1024]))
                s_v.wait(T)
                yield
                for cc in range(8):
                    s_pe.inc(T.transpose(pfb[cc % 2][:, 0:128],
                                         xpre_b[:, cc * 128:(cc + 1) * 128], ident))
                    s_pe.wait(V, s_pe.n)
                    s_v.inc(V.tensor_copy(xprefm[:, cc * 128:(cc + 1) * 128],
                                          pfb[cc % 2][:, 0:128]))
                    if cc >= 1:
                        s_v.wait(T, s_v.n - 1)
                s_v.wait(T, s_v.n)
                s_wb.wait(T, s_wb.n)
                yield
                for co in range(2):
                    ps = pf[co % 2]
                    mm = None
                    for cc in range(8):
                        mm = T.matmul(ps[:, 0:512], xprefm[:, cc * 128:(cc + 1) * 128],
                                      wbsb[:, cc * 1024 + co * 512: cc * 1024 + (co + 1) * 512],
                                      start=(cc == 0), stop=(cc == 7))
                    s_pe.inc(mm)
                    s_pe.wait(V, s_pe.n)
                    s_v.inc(V.tensor_copy(yv[:, co * 512:(co + 1) * 512], ps[:, 0:512]))
                s_v.wait(V)
                yield
                for n in range(4):
                    fon = fo_f[:, n * 1024:(n + 1) * 1024]
                    s_v.inc(V.tensor_scalar(fon, xnt[par][:, 0:1024], hres(n, 0), None, ALU.mult))
                    for jj in range(1, 4):
                        s_v.wait(V)
                        s_v.inc(V.scalar_tensor_tensor(fon, xnt[par][:, jj * 1024:(jj + 1) * 1024],
                                                       hres(n, jj), fon, ALU.mult, ALU.add))
                    s_v.wait(V)
                    s_v.inc(V.scalar_tensor_tensor(fon, yv, hpost(n), fon, ALU.mult, ALU.add))
                    s_v.wait(V)
                v_used_xnt[tt] = s_v.n
                s_v.wait(GP, s_v.n)
                s_ff.inc(GP.dma_start(ffn_dram[tt * 128:(tt + 1) * 128, :], fo_f), 16)
                yield

        qtmp = [Q[:, 7168:7680], Q[:, 15872:16384]]
        qsl = lambda blk, tch, u: qtmp[u % 2][:, 0:512]
        qdr = lambda blk, tch: q_dram[:, (tch * H + blk) * 512:(tch * H + blk + 1) * 512]
        with contextlib.ExitStack() as ps_mx:
            pf_ = [ps_mx.enter_context(nc.psum_tensor(f"pf{i}", [128, 512], F32)) for i in range(2)]
            pfb_ = [ps_mx.enter_context(nc.psum_tensor(f"pfb{i}", [128, 128], BF16)) for i in range(2)]
            mgen = mix_gen(pf_, pfb_)
            proj(wtr_all_q, 0, 512, DM, norm_epi(qsl, 0, qdr), "q", bg=mgen,
                 bg_steps=2, ws_need=2)
            for _ in mgen:
                pass

        if phase_limit == "P3":
            finish_debug()
            return nc

        if phase_limit == "P2b":
            finish_debug()
            return nc
        VOFF = [0, 9216]
        POFF = 18432
        ATOFF = 22528
        PLOFF = [23040, 24704]
        masksb = B[:, 16384:24576]
        qslab = Q[:, 0:16384]

        with contextlib.ExitStack() as ps_st:
            ps_sc = [ps_st.enter_context(nc.psum_tensor(f"psc{i}", [128, 512], F32)) for i in range(5)]
            ps_av = [ps_st.enter_context(nc.psum_tensor(f"pav{i}", [128, 512], F32)) for i in range(2)]
            ps_pl = ps_st.enter_context(nc.psum_tensor("pspl", [128, 512], F32))

            s_pe.wait(SY, s_pe.n)
            s_ff.wait(SY, s_ff.n)
            s_v.wait(SY, s_v.n)
            s_kvs.wait(SY, s_kvs.n)
            s_ct.inc(SY.dma_start(B[:, 16384:24576], mask_in), 16)
            ring_ws.wait_all(GP)
            s_ws.inc(GP.collective_compute(
                "AllGather", ALU.bypass,
                replica_groups=[list(range(NC))],
                ins=[wtr_o.opt()], outs=[wtr_all_o.opt()],
            ))
            s_ct.wait(GP, s_ct.n)

            s_pe.wait(GP, s_pe.n)
            s_v.wait(GP, s_v.n)
            for par_ in range(2):
                vv = A[:, VOFF[par_]:VOFF[par_] + 9216].rearrange(
                    "p (t c) -> p t c", c=144)[:, :, 128:129]
                s_cm.inc(GP.memset(vv, 1.0), 1)
            s_cm.wait(T)

            score_cnt_end = {}
            av_cnt_end = {}
            ex_cnt_end = {}
            KEYSEQ = [(qg_, kv_) for qg_ in range(2) for kv_ in range(HKV)]

            def load_kv(qg, kv):
                seq = KEYSEQ.index((qg, kv))
                par = seq % 2
                kb = C[:, par * 8192:(par + 1) * 8192]
                s_cc.wait(SY, 1)
                if seq >= 2:
                    key = KEYSEQ[seq - 2]
                    s_sc.wait(SY, score_cnt_end[key])
                    s_av.wait(SY, av_cnt_end[key])
                jlo = 0 if qg == 0 else 4
                nj = NT - jlo
                for r in range(NC):
                    ring_kv.inc(SY.dma_start(kb[:, r * 1024:(r + 1) * 1024],
                                             kv_all[r * 2048 + kv * 128: r * 2048 + (kv + 1) * 128, :]), seq)
                    vsrc = kv_all[r * 2048 + 1024 + jlo * 128: r * 2048 + 2048,
                                  kv * 128:(kv + 1) * 128].rearrange(
                        "(j p) h -> p j h", p=128)
                    vdst = A[:, VOFF[par] + (r * 8 + jlo) * 144:
                             VOFF[par] + (r * 8 + 8) * 144].rearrange(
                        "p (j c) -> p j c", c=144)[:, :, 0:128]
                    ring_kv.inc(SY.dma_start(vdst, vsrc), seq)
                return seq

            def build_plane(qg, kv):
                seq = KEYSEQ.index((qg, kv))
                par = seq % 2
                ncols = PLANE_COUNTS[(qg, kv)]
                off = PLANE_OFF[(qg, kv)]
                if seq >= 2:
                    key = KEYSEQ[seq - 2]
                    s_ex.wait(SY, ex_cnt_end[key])
                prb = prhs_bufs[seq % 2]
                ring_pr.inc(SY.dma_start(prb[0:2, 0:ncols], prhs_in[:, off:off + ncols]), seq)
                ring_pr.wait(T, seq)
                pl = A[:, PLOFF[par]:PLOFF[par] + 1664].bitcast(F32)
                nch = (ncols + 511) // 512
                for chk in range(nch):
                    c0 = chk * 512
                    n = min(512, ncols - c0)
                    s_pl.wait(T, s_pl.n)
                    s_pe.inc(T.matmul(ps_pl[:, 0:n], k2[0:2, 0:128], prb[0:2, c0:c0 + n],
                                      start=True, stop=True))
                    s_pe.wait(V, s_pe.n)
                    s_pl.inc(V.tensor_scalar(pl[:, c0:c0 + n], ps_pl[:, 0:n],
                                             shift_b[:, 0:1], None, ALU.subtract))
                return s_pl.n

            s_v.wait(T, s_v.n)
            s_a.wait(T, s_a.n)
            s_pe.inc(T.matmul(ps_pl[:, 0:1], onesf[0:1, 0:128], gcol[0:1, 7:8],
                              start=True, stop=True))
            s_pe.wait(V, s_pe.n)
            s_pl.inc(V.tensor_copy(shift_b[:, 0:1], ps_pl[:, 0:1]))

            prhs_bufs = [B[:, 0:4992].bitcast(F32), B[:, 4992:9984].bitcast(F32)]
            pb_cnt = [0]
            at_cnt = [0]
            ex_tile_cnt = {}
            pfree = {}
            pending = {}

            pending[(0, 0)] = (load_kv(0, 0), build_plane(0, 0))
            pending[(0, 1)] = (load_kv(0, 1), build_plane(0, 1))

            for qg in range(2):
                ring_q.wait_all(SY)
                s_sc.wait(SY, s_sc.n)
                s_ld.inc(SY.dma_start(qslab, q_dram[:, qg * 16384:(qg + 1) * 16384]), 16)
                qload = s_ld.n
                tmax = T_SLOT[4 * qg]
                for kv in range(HKV):
                    klcnt, plcnt = pending.pop((qg, kv))
                    pcol = [0]
                    par = kv % 2
                    kb = C[:, par * 8192:(par + 1) * 8192]
                    pl = A[:, PLOFF[par]:PLOFF[par] + 1664].bitcast(F32)
                    for hg in range(4):
                        h = 4 * kv + hg
                        nf_before = s_nf.n
                        s_nf.wait(T, nf_before)
                        for bk in range(2):
                            T.matmul(ps_av[bk][:, 0:265], ident[0:1, 0:128],
                                     zrow[0:1, 0:265], start=True, stop=False)
                        kept = [t_ for t_ in range(tmax) if active_slots(h, qg, t_)]
                        PIPE = 5
                        sc_cnt = {}

                        def emit_score(idx):
                            t_ = kept[idx]
                            act_ = active_slots(h, qg, t_)
                            jr0_ = act_[0] - 4 * qg
                            nc_ = len(act_) * 128
                            ab = pb_cnt[0] + idx
                            ring_kv.wait(T, klcnt)
                            s_ld.wait(T, qload)
                            if ab >= 5:
                                s_ex.wait(T, ex_tile_cnt[ab - 5])
                            mm_ = T.matmul(ps_sc[ab % 5][:, jr0_ * 128: jr0_ * 128 + nc_],
                                           kb[:, KCOL[t_]:KCOL[t_] + 128],
                                           qslab[:, h * 512 + jr0_ * 128: h * 512 + jr0_ * 128 + nc_],
                                           start=True, stop=True)
                            s_sc.inc(mm_)
                            sc_cnt[ab] = s_sc.n

                        for i0 in range(min(PIPE, len(kept))):
                            emit_score(i0)
                        for idx, t in enumerate(kept):
                            act = active_slots(h, qg, t)
                            ab = pb_cnt[0] + idx
                            scpar = ab % 5
                            pbuf = A[:, POFF + (ab % 8) * 512: POFF + (ab % 8 + 1) * 512]
                            s_sc.wait(SC, sc_cnt[ab])
                            s_pl.wait(SC, plcnt)
                            if ab >= 8:
                                knd, cnt_ = pfree[ab - 8]
                                (s_av if knd == 'av' else s_pb).wait(SC, cnt_)
                            w = CHUNK_W[h]
                            nch = 128 // w
                            last_act = None
                            for j in act:
                                jr = j - 4 * qg
                                for ci in range(nch):
                                    last_act = SC.activation(
                                        pbuf[:, jr * 128 + ci * w: jr * 128 + (ci + 1) * w],
                                        ps_sc[scpar][:, jr * 128 + ci * w: jr * 128 + (ci + 1) * w],
                                        AF.Exp, bias=pl[:, pcol[0]:pcol[0] + 1], scale=ISQD)
                                    pcol[0] += 1
                            s_ex.inc(last_act)
                            ex_tile_cnt[ab] = s_ex.n
                            masked = [j for j in act if t >= T_SLOT[j] - 8]
                            for j in masked:
                                jr = j - 4 * qg
                                s_ex.wait(GP, s_ex.n)
                                s_mk.inc(GP.tensor_tensor(
                                    pbuf[:, jr * 128:(jr + 1) * 128],
                                    pbuf[:, jr * 128:(jr + 1) * 128],
                                    masksb[:, (j * 8 + (t - (T_SLOT[j] - 8))) * 128:
                                           (j * 8 + (t - (T_SLOT[j] - 8)) + 1) * 128],
                                    ALU.mult))
                            s_ex.wait(T, s_ex.n)
                            if masked:
                                s_mk.wait(T, s_mk.n)
                            bank_close = False
                            lastav = None
                            for j in act:
                                jr = j - 4 * qg
                                bk, half = jr // 2, jr % 2
                                closes = (jr in (0, 2) and t == T_SLOT[j] - 1)
                                pos_t = OWNER[t] * 8 + SLOT[t]
                                av = T.matmul(ps_av[bk][:, half * 136: half * 136 + 129],
                                              pbuf[:, jr * 128:(jr + 1) * 128],
                                              A[:, VOFF[par] + pos_t * 144: VOFF[par] + pos_t * 144 + 129],
                                              start=False, stop=closes)
                                if closes:
                                    s_av.inc(av)
                                    bank_close = True
                                lastav = av
                            if bank_close:
                                pfree[ab] = ('av', s_av.n)
                            else:
                                s_pb.inc(lastav)
                                pfree[ab] = ('pb', s_pb.n)
                            if idx + PIPE < len(kept):
                                emit_score(idx + PIPE)
                            for j in act:
                                jr = j - 4 * qg
                                if not (jr in (0, 2) and t == T_SLOT[j] - 1):
                                    continue
                                bk = jr // 2
                                s_av.wait(V, s_av.n)
                                for half in range(2):
                                    jn = 4 * qg + 2 * bk + half
                                    atb = A[:, ATOFF + (at_cnt[0] % 4) * 128: ATOFF + (at_cnt[0] % 4 + 1) * 128]
                                    ring_at.wait(V, at_cnt[0])
                                    dc = dencol[:, (at_cnt[0] % 4) * 2:(at_cnt[0] % 4) * 2 + 2]
                                    s_v.inc(V.tensor_scalar(dc[:, 0:1],
                                                            ps_av[bk][:, half * 136 + 128: half * 136 + 129],
                                                            DEN_GUARD, None, ALU.add))
                                    s_v.wait(V)
                                    s_v.inc(V.reciprocal(dc[:, 1:2], dc[:, 0:1]))
                                    s_v.wait(V)
                                    nrm = V.tensor_scalar(atb, ps_av[bk][:, half * 136: half * 136 + 128],
                                                          dc[:, 1:2], None, ALU.mult)
                                    s_nf.inc(nrm)
                                    s_nf.wait(SY, s_nf.n)
                                    ring_at.inc(SY.dma_start(
                                        attn_dram[jn * 128:(jn + 1) * 128, h * 128:(h + 1) * 128], atb), at_cnt[0])
                                    at_cnt[0] += 1
                        pb_cnt[0] += len(kept)
                    score_cnt_end[(qg, kv)] = s_sc.n
                    av_cnt_end[(qg, kv)] = s_av.n
                    ex_cnt_end[(qg, kv)] = s_ex.n
                    nseq = KEYSEQ.index((qg, kv)) + 2
                    if nseq < len(KEYSEQ):
                        nk = KEYSEQ[nseq]
                        pending[nk] = (load_kv(*nk), build_plane(*nk))

        if phase_limit == "P4":
            finish_debug()
            return nc
        attn_fm = A[:, 0:32768]
        xres = Q[:, 0:4096].bitcast(F32)
        outt = Q[:, 4096:8192].bitcast(F32)
        s_ff.wait(SC, s_ff.n)
        with contextlib.ExitStack() as ps_st:
            pw = [ps_st.enter_context(nc.psum_tensor(f"pw{i}", [128, 256], F32)) for i in range(2)]
            ftp = [ps_st.enter_context(nc.psum_tensor(f"ftp{i}", [128, 128], BF16))
                   for i in range(4)]
            s_v.wait(T, s_v.n)
            s_a.wait(T, s_a.n)

            def wo_load(g):
                par = g % 2
                wob = C[:, par * 8192:(par + 1) * 8192]
                r_ = (g * 256) // 512
                lo_ = (g * 256) % 512
                for hc in range(32):
                    ring_wt.inc(SY.dma_start(
                        wob[:, hc * 256:(hc + 1) * 256],
                        wtr_all_o[r_ * DM + hc * 128: r_ * DM + (hc + 1) * 128,
                                  lo_: lo_ + 256]), g)
                return wob

            s_ws.wait(SY, 3)
            s_sc.wait(SY, s_sc.n)
            wob_pre = {0: wo_load(0), 1: wo_load(1)}

            ring_at.wait_all(SY)
            s_pe.wait(SY, s_pe.n)
            arow = [B[:, 0:4096], B[:, 4096:8192]]
            evf = {}
            trf = {}
            for j in range(NT):
                if j >= 2:
                    s_pe.wait(SY, trf[j - 2])
                ring_kv.inc(SY.dma_start(arow[j % 2], attn_dram[j * 128:(j + 1) * 128, :]),
                            16 + j)
                ring_kv.wait(T, 16 + j)
                for hc in range(32):
                    u = j * 32 + hc
                    if u >= 4:
                        s_fm.wait(T, evf[u - 4])
                    s_pe.inc(T.transpose(ftp[u % 4][:, 0:128],
                                         arow[j % 2][:, hc * 128:(hc + 1) * 128], ident))
                    s_pe.wait(V, s_pe.n)
                    if u == 0:
                        s_av.wait(V, s_av.n)
                        s_mk.wait(V, s_mk.n)
                        ring_at.wait_all(V)
                    s_fm.inc(V.tensor_copy(
                        attn_fm[:, hc * 1024 + j * 128: hc * 1024 + (j + 1) * 128],
                        ftp[u % 4][:, 0:128]))
                    evf[u] = s_fm.n
                trf[j] = s_pe.n
            fm_all = s_fm.n

            wo_done = {}
            p5_done = {}
            for g in range(16):
                wob = wob_pre.pop(g, None)
                if wob is None:
                    s_pe.wait(SY, wo_done[g - 2])
                    wob = wo_load(g)
                ring_wt.wait(T, g)
                s_fm.wait(T, fm_all)
                for tt in range(NT):
                    un = (g * 8 + tt)
                    ps = pw[un % 2]
                    if un >= 2:
                        s_v.wait(T, p5_done[un - 2])
                    ring_y.wait(SC, un)
                    ring_x.inc(SC.dma_start(xres[:, (un % 2) * 1024: (un % 2) * 1024 + 256],
                                            x_in[tt * 128:(tt + 1) * 128, g * 256:(g + 1) * 256]), un)
                    ring_x.inc(SC.dma_start(xres[:, (un % 2) * 1024 + 256: (un % 2) * 1024 + 512],
                                            ffn_dram[tt * 128:(tt + 1) * 128, g * 256:(g + 1) * 256]), un)
                    mm = None
                    for hc in range(32):
                        mm = T.matmul(ps[:, 0:256],
                                      attn_fm[:, hc * 1024 + tt * 128: hc * 1024 + (tt + 1) * 128],
                                      wob[:, hc * 256:(hc + 1) * 256],
                                      start=(hc == 0), stop=(hc == 31))
                    s_pe.inc(mm)
                    s_pe.wait(V, s_pe.n)
                    ring_x.wait(V, un)
                    ot = outt[:, (un % 2) * 512:(un % 2) * 512 + 256]
                    s_v.inc(V.tensor_add(ot, ps[:, 0:256], xres[:, (un % 2) * 1024: (un % 2) * 1024 + 256]))
                    s_v.wait(V)
                    s_v.inc(V.tensor_add(ot, ot, xres[:, (un % 2) * 1024 + 256: (un % 2) * 1024 + 512]))
                    p5_done[un] = s_v.n
                    s_v.wait(GP, s_v.n)
                    ring_y.inc(GP.dma_start(y_out[tt * 128:(tt + 1) * 128, g * 256:(g + 1) * 256], ot), un)
                wo_done[g] = s_pe.n

            for tp, src in tap_outs:
                s_y.wait(GP, s_y.n)
                s_y.inc(GP.dma_start(tp, src), 16)
            s_y.wait(GP, s_y.n)

    return nc



_CONST = {}

def _consts():
    if not _CONST:
        _CONST["ident"] = np.eye(128, dtype=np.float32).astype(ml_dtypes.bfloat16)
        _CONST["ones"] = np.ones((1, 128), np.float32)
        _CONST["k2"] = np.stack([np.arange(128, dtype=np.float32),
                                 np.ones(128, np.float32)])
        _CONST["planes"] = [build_plane_rhs(c) for c in range(NC)]
        _CONST["masks"] = [build_masks(c) for c in range(NC)]
    return _CONST


def run(inputs, taps=(), phase_limit=None):
    cst = _consts()
    x = np.asarray(inputs["L0_attn_x"])
    xr = np.ascontiguousarray(x.reshape(NB, 128, DM))
    wq_np = np.asarray(inputs["L0_attn_Wq"])
    wk_np = np.asarray(inputs["L0_attn_Wk"])
    wv_np = np.asarray(inputs["L0_attn_Wv"])
    wo_np = np.asarray(inputs["L0_attn_Wo"])
    in_maps = []
    for c in range(NC):
        blocks = blocks_of_core(c)
        x_own = np.ascontiguousarray(xr[blocks].reshape(TOK, DM))
        w_slice = np.ascontiguousarray(np.concatenate([
            wq_np[512 * c:512 * (c + 1)], wk_np[128 * c:128 * (c + 1)],
            wv_np[128 * c:128 * (c + 1)], wo_np[512 * c:512 * (c + 1)]], axis=0))
        m = {
            "x_own": x_own,
            "w_slice": w_slice,
            "L0_attn_Wq": np.ascontiguousarray(inputs["L0_attn_Wq"]),
            "L0_attn_Wk": np.ascontiguousarray(inputs["L0_attn_Wk"]),
            "L0_attn_Wv": np.ascontiguousarray(inputs["L0_attn_Wv"]),
            "L0_attn_Wo": np.ascontiguousarray(inputs["L0_attn_Wo"]),
            "L0_attn_q_norm_w": np.ascontiguousarray(np.asarray(inputs["L0_attn_q_norm_w"]).reshape(1, HD)),
            "L0_attn_k_norm_w": np.ascontiguousarray(np.asarray(inputs["L0_attn_k_norm_w"]).reshape(1, HD)),
            "L0_ffn_W_res": np.ascontiguousarray(inputs["L0_ffn_W_res"]),
            "L0_ffn_W_pre": np.ascontiguousarray(inputs["L0_ffn_W_pre"]),
            "L0_ffn_W_post": np.ascontiguousarray(inputs["L0_ffn_W_post"]),
            "L0_ffn_H_res_base": np.ascontiguousarray(np.asarray(inputs["L0_ffn_H_res_base"]).reshape(1, 16)),
            "L0_ffn_H_pre_base": np.ascontiguousarray(np.asarray(inputs["L0_ffn_H_pre_base"]).reshape(1, 4)),
            "L0_ffn_H_post_base": np.ascontiguousarray(np.asarray(inputs["L0_ffn_H_post_base"]).reshape(1, 4)),
            "L0_ffn_gate": np.ascontiguousarray(np.asarray(inputs["L0_ffn_gate"]).reshape(1, 1)),
            "L0_ffn_W_branch": np.ascontiguousarray(inputs["L0_ffn_W_branch"]),
            "c_ident": cst["ident"],
            "c_ones_f32": cst["ones"],
            "c_k2": cst["k2"],
            "c_plane_rhs": cst["planes"][c],
            "c_masks": cst["masks"][c],
        }
        in_maps.append(m)

    nc_prog = build_program(taps=taps, phase_limit=phase_limit)
    res = run_bass_kernel_spmd(nc_prog, in_maps, core_ids=list(range(NC)))

    y = np.zeros((NB, 128, DM), np.float32)
    for c in range(NC):
        blocks = blocks_of_core(c)
        yc = np.asarray(res.results[c]["y_own"]).reshape(NT, 128, DM)
        for j, b in enumerate(blocks):
            y[b] = yc[j]
    out = y.reshape(1, S, DM)
    if taps:
        return out, res
    return out


def kernel(**inputs):
    return run(inputs)
